# version 02
# baseline (speedup 1.0000x reference)
import math
from contextlib import ExitStack

import numpy as np
import ml_dtypes

import concourse.bass as bass
import concourse.mybir as mybir
from concourse.bass_utils import run_bass_kernel_spmd
from concourse.alu_op_type import AluOpType as OP

F32 = mybir.dt.float32
BF16 = mybir.dt.bfloat16
F16 = mybir.dt.float16
I32 = mybir.dt.int32
U32 = mybir.dt.uint32
AF = mybir.ActivationFunctionType
AX = mybir.AxisListType
BF = ml_dtypes.bfloat16

FULL = dict(L=4096, D=2048, NHC=8, I0=5632, I1=8192)
MINI = dict(L=2048, D=512, NHC=2, I0=768, I1=1024)


def derive(cfg):
    c = dict(cfg)
    c["DC"] = c["D"] // 128
    c["NTC"] = c["L"] // 512
    c["NQC"] = c["L"] // 512
    c["NTT"] = c["L"] // 128
    c["TH"] = c["L"] // 2
    c["NQL"] = c["TH"] // 512
    c["NJ"] = 2 * c["NHC"]
    c["E"] = 3 * c["D"]
    return c


_SEM_LOG = []


class Sem:
    def __init__(self, k, name):
        self.h = k.stack.enter_context(k.nc.semaphore(name))
        self.n = 0
        _SEM_LOG.append(self.h.num)


class K:
    def __init__(self, nc, block, stack):
        self.nc = nc
        self.block = block
        self.stack = stack

    def sem(self, name):
        return Sem(self, name)

    def sb(self, name, shape, dtype):
        return self.stack.enter_context(self.nc.sbuf_tensor(name, shape, dtype))

    def ps(self, name, shape, dtype=F32):
        return self.stack.enter_context(self.nc.psum_tensor(name, shape, dtype))


def emit_l0attn(k, c, io):
    nc, block = k.nc, k.block
    L, D, DC, NHC, NJ = c["L"], c["D"], c["DC"], c["NHC"], c["NJ"]
    NTC, NQC, NTT, NQL = c["NTC"], c["NQC"], c["NTT"], c["NQL"]
    TH = c["TH"]
    HD = NHC * 128
    NVC = max(1, HD // 256)
    VW = min(HD, 256)
    NVT = (HD // VW) * 4
    ESC = 1.0 / math.sqrt(128.0)
    assert DC == 2 * NHC

    nkt = [4 * qc + 4 for qc in range(NQC)]
    KTT = sum(nkt)
    cumk = [0]
    for n in nkt:
        cumk.append(cumk[-1] + n)

    def tile0(h, qc, p):
        return 2 * KTT * h + 2 * cumk[qc] + p * nkt[qc]

    ones_col = k.sb("ones_col", [128, 1], BF16)
    ones_row_f = k.sb("ones_row_f", [1, 128], F32)
    ones_col_f = k.sb("ones_col_f", [128, 1], F32)
    bl_sb = k.sb("bl_sb", [8, 1], F32)
    tmp1 = k.sb("tmp1", [128, 512], F32)
    tmp2 = k.sb("tmp2", [128, 512], F32)
    ps8 = k.ps("ps8", [128, 8, 512])

    m_cstg = k.sem("m_cstg")
    m_csts = k.sem("m_csts")
    m_x = k.sem("m_x")
    m_wk = [k.sem("m_wk0"), k.sem("m_wk1")]
    m_wq = [k.sem("m_wq0"), k.sem("m_wq1")]
    m_wv = k.sem("m_wv")
    m_cs = k.sem("m_cs")
    m_qst = [k.sem("m_qst0"), k.sem("m_qst1")]
    m_lam = k.sem("m_lam")
    m_qb = k.sem("m_qb")
    m_lr = k.sem("m_lr")
    m_wo = k.sem("m_wo")
    m_ao = k.sem("m_ao")
    m_dld = k.sem("m_dld")
    m_out = k.sem("m_out")

    s_peK = k.sem("s_peK")
    s_peV = k.sem("s_peV")
    s_peQ = k.sem("s_peQ")
    s_peL = k.sem("s_peL")
    s_dvK = k.sem("s_dvK")
    s_dvV = k.sem("s_dvV")
    s_dvQ = k.sem("s_dvQ")
    s_acL = k.sem("s_acL")
    s_st = k.sem("s_st")
    s_exp = k.sem("s_exp")
    s_msk = k.sem("s_msk")
    s_sums = k.sem("s_sums")
    s_ctx = k.sem("s_ctx")
    s_dvS = k.sem("s_dvS")
    s_peB = k.sem("s_peB")
    s_cmb = k.sem("s_cmb")
    s_cc = k.sem("s_cc")
    s_peD = k.sem("s_peD")
    s_exC = k.sem("s_exC")
    s_ex = k.sem("s_ex")
    s_sq2 = k.sem("s_sq2")
    s_stat = k.sem("s_stat")
    s_row = k.sem("s_row")
    s_peB2 = k.sem("s_peB2")
    s_nrm = k.sem("s_nrm")

    AT = 2 * NHC * KTT
    vslot = [NHC if h == 0 else h - 1 for h in range(NHC)]

    with ExitStack() as sv:
        kv = K(nc, block, sv)
        Vt = kv.sb("Vt", [128, NTT, NHC + 1, 128], BF16)

        with ExitStack() as sk:
            kf_ = K(nc, block, sk)
            Kf = kf_.sb("Kf", [128, NHC, L], BF16)

            with ExitStack() as sa:
                ka = K(nc, block, sa)
                xbf = ka.sb("A_xbf", [128, DC, 512], BF16)
                wkb = ka.sb("A_wkb", [128, 2, DC, 128], BF16)
                wqb = ka.sb("A_wqb", [128, 2, DC, 128], BF16)
                wvb = ka.sb("A_wvb", [128, DC, VW], BF16)
                wlb = ka.sb("A_wlb", [128, DC, NHC], BF16)
                cosb = ka.sb("A_cosb", [128, 512], F32)
                sinb = ka.sb("A_sinb", [128, 512], F32)
                stg = ka.sb("A_stg", [128, 2, 512], BF16)
                lam_st = ka.sb("A_lam", [8, 512], F32)

                @block.gpsimd
                def _(g):
                    g.dma_start(out=wlb[:, :, :],
                                in_=io["wl"].rearrange("(c p) h -> p c h", p=128)
                                ).then_inc(m_cstg.h, 16)
                    for tc in range(NTC):
                        t0 = tc * 512
                        if tc >= 1:
                            g.wait_ge(s_peK.h, NHC * tc)
                            g.wait_ge(s_peV.h, NVT * tc)
                            g.wait_ge(s_peQ.h, NJ * tc)
                            g.wait_ge(s_peL.h, tc)
                        g.dma_start(out=xbf[:, :, :],
                                    in_=io["xT"].rearrange("(c p) t -> p c t", p=128)[:, :, t0:t0 + 512]
                                    ).then_inc(m_x.h, 16)
                        for h in range(NHC):
                            gidx = tc * NHC + h
                            if gidx >= 2:
                                g.wait_ge(s_peK.h, gidx - 1)
                            g.dma_start(out=wkb[:, h % 2, :, :],
                                        in_=io["wk"].rearrange("(c p) o -> p c o", p=128)[:, :, h * 128:(h + 1) * 128]
                                        ).then_inc(m_wk[h % 2].h, 16)
                        for v in range(NVC):
                            gidx = tc * NVC + v
                            if gidx >= 1:
                                g.wait_ge(s_peV.h, 4 * gidx)
                            g.dma_start(out=wvb[:, :, :],
                                        in_=io["wv"].rearrange("(c p) o -> p c o", p=128)[:, :, v * VW:(v + 1) * VW]
                                        ).then_inc(m_wv.h, 16)
                        for j in range(NJ):
                            gidx = tc * NJ + j
                            if gidx >= 2:
                                g.wait_ge(s_peQ.h, gidx - 1)
                            g.dma_start(out=wqb[:, j % 2, :, :],
                                        in_=io["wq"].rearrange("(c p) o -> p c o", p=128)[:, :, j * 128:(j + 1) * 128]
                                        ).then_inc(m_wq[j % 2].h, 16)

                @block.sync
                def _(s):
                    s.dma_start(out=bl_sb[0:NHC, :], in_=io["bl"][:, None]).then_inc(m_csts.h, 16)
                    for tc in range(NTC):
                        t0 = tc * 512
                        if tc >= 1:
                            s.wait_ge(s_dvK.h, NHC * tc)
                            s.wait_ge(s_dvQ.h, NJ * tc)
                        s.dma_start(out=cosb[:, :], in_=io["cosk"][:, t0:t0 + 512]).then_inc(m_cs.h, 16)
                        s.dma_start(out=sinb[:, :], in_=io["sink"][:, t0:t0 + 512]).then_inc(m_cs.h, 16)
                        for j in range(NJ):
                            s.wait_ge(s_dvQ.h, tc * NJ + j + 1)
                            s.dma_start(out=io["q_scr"][j * 128:(j + 1) * 128, t0:t0 + 512],
                                        in_=stg[:, j % 2, :]).then_inc(m_qst[j % 2].h, 16)
                        s.wait_ge(s_acL.h, tc + 1)
                        s.dma_start(out=io["lam_scr"][0:NHC, t0:t0 + 512],
                                    in_=lam_st[0:NHC, :]).then_inc(m_lam.h, 16)

                @block.tensor
                def _(t):
                    for tc in range(NTC):
                        t.wait_ge(m_x.h, 16 * (tc + 1))
                        for h in range(NHC):
                            gidx = tc * NHC + h
                            t.wait_ge(m_wk[h % 2].h, 16 * (gidx // 2 + 1))
                            if gidx >= 2:
                                t.wait_ge(s_dvK.h, gidx - 1)
                            for kc in range(DC):
                                mm = t.matmul(ps8[:, gidx % 2, :], wkb[:, h % 2, kc, :],
                                              xbf[:, kc, :], start=(kc == 0), stop=(kc == DC - 1))
                            mm.then_inc(s_peK.h, 1)
                            s_peK.n += 1
                        for v in range(NVC):
                            t.wait_ge(m_wv.h, 16 * (tc * NVC + v + 1))
                            for tt in range(4):
                                gidx = tc * NVT + v * 4 + tt
                                if gidx >= 2:
                                    t.wait_ge(s_dvV.h, gidx - 1)
                                for kc in range(DC):
                                    mm = t.matmul(ps8[:, 2 + gidx % 2, 0:VW],
                                                  xbf[:, kc, tt * 128:(tt + 1) * 128],
                                                  wvb[:, kc, :], start=(kc == 0), stop=(kc == DC - 1))
                                mm.then_inc(s_peV.h, 1)
                                s_peV.n += 1
                        for j in range(NJ):
                            gidx = tc * NJ + j
                            t.wait_ge(m_wq[j % 2].h, 16 * (gidx // 2 + 1))
                            if gidx >= 2:
                                t.wait_ge(s_dvQ.h, gidx - 1)
                            for kc in range(DC):
                                mm = t.matmul(ps8[:, 4 + gidx % 2, :], wqb[:, j % 2, kc, :],
                                              xbf[:, kc, :], start=(kc == 0), stop=(kc == DC - 1))
                            mm.then_inc(s_peQ.h, 1)
                            s_peQ.n += 1
                        if tc == 0:
                            t.wait_ge(m_cstg.h, 16)
                        if tc >= 1:
                            t.wait_ge(s_acL.h, tc)
                        for kc in range(DC):
                            mm = t.matmul(ps8[0:NHC, 6, :], wlb[:, kc, 0:NHC], xbf[:, kc, :],
                                          start=(kc == 0), stop=(kc == DC - 1))
                        mm.then_inc(s_peL.h, 1)
                        s_peL.n += 1

                @block.scalar
                def _(sc):
                    sc.wait_ge(m_csts.h, 16)
                    for tc in range(NTC):
                        sc.wait_ge(s_peL.h, tc + 1)
                        if tc >= 1:
                            sc.wait_ge(m_lam.h, 16 * tc)
                        op = sc.activation(out=lam_st[0:NHC, :], in_=ps8[0:NHC, 6, :],
                                           func=AF.Sigmoid, bias=bl_sb[0:NHC, :])
                        op.then_inc(s_acL.h, 1)
                        s_acL.n += 1

                @block.vector
                def _(v):
                    v.memset(ones_col[:, :], 1.0)
                    v.memset(ones_row_f[:, :], 1.0)
                    v.memset(ones_col_f[:, :], 1.0)
                    for tc in range(NTC):
                        v.wait_ge(m_cs.h, 32 * (tc + 1))
                        for h in range(NHC):
                            gidx = tc * NHC + h
                            v.wait_ge(s_peK.h, gidx + 1)
                            pk = ps8[:, gidx % 2, :]
                            v.tensor_tensor(out=tmp1[:, :], in0=pk, in1=cosb[:, :], op=OP.mult)
                            v.scalar_tensor_tensor(out=tmp2[0:64, :], in0=pk[64:128, :],
                                                   scalar=-1.0, in1=sinb[0:64, :],
                                                   op0=OP.mult, op1=OP.mult)
                            v.tensor_tensor(out=tmp2[64:128, :], in0=pk[0:64, :],
                                            in1=sinb[64:128, :], op=OP.mult)
                            op = v.tensor_tensor(out=Kf[:, h, tc * 512:tc * 512 + 512],
                                                 in0=tmp1[:, :], in1=tmp2[:, :], op=OP.add)
                            op.then_inc(s_dvK.h, 1)
                            s_dvK.n += 1
                        for vi in range(NVC):
                            for tt in range(4):
                                gidx = tc * NVT + vi * 4 + tt
                                v.wait_ge(s_peV.h, gidx + 1)
                                dst = Vt[:, tc * 4 + tt, vi * (VW // 128):(vi + 1) * (VW // 128), :]
                                op = v.tensor_copy(dst, ps8[:, 2 + gidx % 2, 0:VW])
                                op.then_inc(s_dvV.h, 1)
                                s_dvV.n += 1
                        for j in range(NJ):
                            gidx = tc * NJ + j
                            v.wait_ge(s_peQ.h, gidx + 1)
                            if gidx >= 2:
                                v.wait_ge(m_qst[gidx % 2].h, 16 * (gidx // 2))
                            pq_ = ps8[:, 4 + gidx % 2, :]
                            v.tensor_tensor(out=tmp1[:, :], in0=pq_, in1=cosb[:, :], op=OP.mult)
                            v.scalar_tensor_tensor(out=tmp2[0:64, :], in0=pq_[64:128, :],
                                                   scalar=-1.0, in1=sinb[0:64, :],
                                                   op0=OP.mult, op1=OP.mult)
                            v.tensor_tensor(out=tmp2[64:128, :], in0=pq_[0:64, :],
                                            in1=sinb[64:128, :], op=OP.mult)
                            op = v.tensor_tensor(out=stg[:, j % 2, :], in0=tmp1[:, :],
                                                 in1=tmp2[:, :], op=OP.add)
                            op.then_inc(s_dvQ.h, 1)
                            s_dvQ.n += 1

            with ExitStack() as sb_:
                kb = K(nc, block, sb_)
                qb = kb.sb("B_qb", [128, 2, L], BF16)
                ptb = kb.sb("B_pt", [128, 2, 512], BF16)
                mask_sb = kb.sb("B_mask", [128, 4, 512], BF16)
                sums_sb = kb.sb("B_sums", [1, 2, 512], F32)
                rec_sb = kb.sb("B_rec", [1, 2, 512], F32)
                lamr = kb.sb("B_lamr", [1, 512], F32)
                lr1 = kb.sb("B_lr1", [1, 512], F32)
                rb_sb = kb.sb("B_rb", [128, 2, 512], F32)
                ct0 = kb.sb("B_ct0", [128, 512], F32)

                @block.sync
                def _(s):
                    s.wait_ge(s_peL.h, NTC)
                    s.wait_ge(s_dvQ.h, NJ * NTC)
                    s.wait_ge(s_acL.h, NTC)
                    s.dma_start(out=mask_sb[:, :, :],
                                in_=io["masks"].rearrange("m p q -> p m q")).then_inc(m_csts.h, 16)
                    for h in range(NHC):
                        if h >= 1:
                            s.wait_ge(s_st.h, 2 * KTT * h)
                        s.dma_start(out=qb[:, 0, :],
                                    in_=io["q_scr"][(2 * h) * 128:(2 * h + 1) * 128, :]).then_inc(m_qb.h, 16)
                        s.dma_start(out=qb[:, 1, :],
                                    in_=io["q_scr"][(2 * h + 1) * 128:(2 * h + 2) * 128, :]).then_inc(m_qb.h, 16)
                        for qc in range(NQC):
                            hq = h * NQC + qc
                            if hq >= 1:
                                s.wait_ge(s_dvS.h, 2 * hq)
                            s.dma_start(out=lamr[:, :],
                                        in_=io["lam_scr"][h:h + 1, qc * 512:(qc + 1) * 512]
                                        ).then_inc(m_lr.h, 16)

                @block.tensor
                def _(t):
                    t.wait_ge(s_dvK.h, NHC * NTC)
                    t.wait_ge(s_dvV.h, NVT * NTC)
                    nst = 0
                    ncs = 0
                    nmsk = 0
                    for h in range(NHC):
                        t.wait_ge(m_qb.h, 32 * (h + 1))
                        for qc in range(NQC):
                            hq = h * NQC + qc
                            for p in range(2):
                                pq = 2 * hq + p
                                n = nkt[qc]
                                for kt in range(n + 1):
                                    if kt < n:
                                        i = nst
                                        if i >= 2:
                                            t.wait_ge(s_exp.h, i - 1)
                                        mm = t.matmul(ps8[:, i % 2, :],
                                                      Kf[:, h, kt * 128:(kt + 1) * 128],
                                                      qb[:, p, qc * 512:(qc + 1) * 512],
                                                      start=True, stop=True)
                                        mm.then_inc(s_st.h, 1)
                                        nst += 1
                                        s_st.n += 1
                                    if kt >= 1:
                                        ck = kt - 1
                                        i = ncs
                                        diag = ck >= 4 * qc
                                        if diag:
                                            t.wait_ge(s_msk.h, nmsk + (ck - 4 * qc) + 1)
                                        else:
                                            t.wait_ge(s_exp.h, i + 1)
                                        if ck == 0:
                                            if pq >= 2:
                                                t.wait_ge(s_dvS.h, pq - 1)
                                            if hq >= 1:
                                                t.wait_ge(s_cmb.h, hq)
                                        mm = t.matmul(ps8[0:1, 2 + pq % 2, :], ones_col[:, :],
                                                      ptb[:, i % 2, :],
                                                      start=(ck == 0), stop=(ck == n - 1))
                                        mm.then_inc(s_sums.h, 1)
                                        s_sums.n += 1
                                        mm = t.matmul(ps8[:, 4 + p, :], Vt[:, ck, h, :],
                                                      ptb[:, i % 2, :],
                                                      start=(ck == 0), stop=(ck == n - 1))
                                        mm.then_inc(s_ctx.h, 1)
                                        ncs += 1
                                        s_ctx.n += 1
                                nmsk += 4
                            t.wait_ge(s_dvS.h, 2 * hq + 2)
                            if hq >= 1:
                                t.wait_ge(s_cmb.h, hq)
                            mm = t.matmul(ps8[:, 6, :], ones_row_f[:, :], rec_sb[:, 0, :],
                                          start=True, stop=True)
                            mm = t.matmul(ps8[:, 7, :], ones_row_f[:, :], lr1[:, :],
                                          start=True, stop=True)
                            mm.then_inc(s_peB.h, 1)
                            s_peB.n += 1

                @block.scalar
                def _(sc):
                    for i in range(AT):
                        sc.wait_ge(s_st.h, i + 1)
                        if i >= 2:
                            sc.wait_ge(s_ctx.h, i - 1)
                        op = sc.activation(out=ptb[:, i % 2, :], in_=ps8[:, i % 2, :],
                                           func=AF.Exp, scale=ESC)
                        op.then_inc(s_exp.h, 1)
                        s_exp.n += 1

                @block.vector
                def _(v):
                    v.wait_ge(m_csts.h, 32)
                    for h in range(NHC):
                        for qc in range(NQC):
                            hq = h * NQC + qc
                            for p in range(2):
                                pq = 2 * hq + p
                                n = nkt[qc]
                                base = tile0(h, qc, p)
                                for m in range(4):
                                    i = base + 4 * qc + m
                                    v.wait_ge(s_exp.h, i + 1)
                                    op = v.tensor_tensor(out=ptb[:, i % 2, :], in0=ptb[:, i % 2, :],
                                                         in1=mask_sb[:, m, :], op=OP.mult)
                                    op.then_inc(s_msk.h, 1)
                                    s_msk.n += 1
                                v.wait_ge(s_sums.h, base + n)
                                v.tensor_copy(sums_sb[:, pq % 2, :], ps8[0:1, 2 + pq % 2, :])
                                if p == 0:
                                    op = v.reciprocal(rec_sb[:, pq % 2, :], sums_sb[:, pq % 2, :])
                                    op.then_inc(s_dvS.h, 1)
                                    s_dvS.n += 1
                                else:
                                    v.reciprocal(rec_sb[:, pq % 2, :], sums_sb[:, pq % 2, :])
                                    v.wait_ge(m_lr.h, 16 * (hq + 1))
                                    op = v.tensor_tensor(out=lr1[:, :], in0=lamr[:, :],
                                                         in1=rec_sb[:, pq % 2, :], op=OP.mult)
                                    op.then_inc(s_dvS.h, 1)
                                    s_dvS.n += 1
                            v.wait_ge(s_peB.h, hq + 1)
                            v.tensor_copy(rb_sb[:, 0, :], ps8[:, 6, :])
                            v.tensor_copy(rb_sb[:, 1, :], ps8[:, 7, :])
                            v.wait_ge(s_ctx.h, tile0(h, qc, 1) + nkt[qc])
                            v.tensor_tensor(out=ct0[:, :], in0=ps8[:, 4, :], in1=rb_sb[:, 0, :],
                                            op=OP.mult)
                            v.tensor_tensor(out=tmp1[:, :], in0=ps8[:, 5, :], in1=rb_sb[:, 1, :],
                                            op=OP.mult)
                            dst = Vt[:, qc * 4:(qc + 1) * 4, vslot[h], :]
                            op = v.tensor_tensor(out=dst, in0=ct0[:, :], in1=tmp1[:, :],
                                                 op=OP.subtract)
                            op.then_inc(s_cmb.h, 1)
                            s_cmb.n += 1

        with ExitStack() as sc2:
            kc2 = K(nc, block, sc2)
            wob = kc2.sb("C_wob", [128, NHC, D], BF16)
            sA = kc2.sb("C_sA", [128, DC, 512], F32)

            @block.gpsimd
            def _(g):
                g.wait_ge(s_cmb.h, NHC * NQC)
                g.dma_start(out=wob[:, :, :],
                            in_=io["wo"].rearrange("(c p) o -> p c o", p=128)
                            ).then_inc(m_wo.h, 16)
                g.wait_ge(m_ao.h, 16 * NQC)
                g.collective_compute("ReduceScatter", OP.add,
                                     replica_groups=[[0, 1], [2, 3], [4, 5], [6, 7]],
                                     ins=[io["cc_in"].ap().opt()],
                                     outs=[io["cc_out"].ap().opt()]).then_inc(s_cc.h, 1)
                s_cc.n += 1

            @block.tensor
            def _(t):
                t.wait_ge(m_wo.h, 16)
                for qc8 in range(NQC):
                    t.wait_ge(s_cmb.h, (NHC - 1) * NQC + qc8 + 1)
                    for oc in range(DC):
                        gi = qc8 * DC + oc
                        if gi >= 2:
                            t.wait_ge(s_exC.h, gi - 1)
                        for dc in range(NHC):
                            mm = t.matmul(ps8[:, gi % 2, :], wob[:, dc, oc * 128:(oc + 1) * 128],
                                          Vt[:, qc8 * 4:(qc8 + 1) * 4, vslot[dc], :],
                                          start=(dc == 0), stop=(dc == NHC - 1))
                        mm.then_inc(s_peD.h, 1)
                        s_peD.n += 1

            @block.vector
            def _(v):
                for qc8 in range(NQC):
                    for oc in range(DC):
                        gi = qc8 * DC + oc
                        v.wait_ge(s_peD.h, gi + 1)
                        if oc == 0 and qc8 >= 1:
                            v.wait_ge(m_ao.h, 16 * qc8)
                        op = v.tensor_copy(sA[:, oc, :], ps8[:, gi % 2, :])
                        op.then_inc(s_exC.h, 1)
                        s_exC.n += 1

            @block.sync
            def _(s):
                for qc8 in range(NQC):
                    r = qc8 // NQL
                    q0 = (qc8 % NQL) * 512
                    s.wait_ge(s_exC.h, DC * (qc8 + 1))
                    s.dma_start(out=io["cc_in"].ap()[r].rearrange("(c p) t -> p c t", p=128)[:, :, q0:q0 + 512],
                                in_=sA[:, :, :]).then_inc(m_ao.h, 16)

    with ExitStack() as sd:
        kd = K(nc, block, sd)
        aoslab = kd.sb("D_ao", [128, DC, 512], F32)
        xslab = kd.sb("D_x", [128, DC, 512], F32)
        sD = kd.sb("D_s", [128, DC, 512], F32)
        scr2 = kd.sb("D_scr2", [128, 2, 512], F32)
        rowbuf = kd.sb("D_rows", [1, 8, 512], F32)
        rb2 = kd.sb("D_rb2", [128, 2, 512], F32)

        @block.sync
        def _(s):
            for ql in range(NQL):
                q0 = ql * 512
                if ql == 0:
                    s.wait_ge(s_cc.h, 1)
                else:
                    s.wait_ge(s_ex.h, DC * ql)
                s.dma_start(out=aoslab[:, :, :],
                            in_=io["cc_out"].ap().rearrange("(c p) t -> p c t", p=128)[:, :, q0:q0 + 512]
                            ).then_inc(m_dld.h, 16)
                s.dma_start(out=xslab[:, :, :],
                            in_=io["xTh"].rearrange("(c p) t -> p c t", p=128)[:, :, q0:q0 + 512]
                            ).then_inc(m_dld.h, 16)
                s.wait_ge(s_nrm.h, ql + 1)
                s.dma_start(out=io["out"].rearrange("(c p) t -> p c t", p=128)[:, :, q0:q0 + 512],
                            in_=sD[:, :, :]).then_inc(m_out.h, 16)
            s.wait_ge(m_out.h, 16 * NQL)

        @block.tensor
        def _(t):
            for ql in range(NQL):
                for oc in range(DC):
                    gi = ql * DC + oc
                    t.wait_ge(s_ex.h, gi + 1)
                    if oc == 0 and ql >= 1:
                        t.wait_ge(s_row.h, 2 * ql - 1)
                    mm = t.matmul(ps8[0:1, 2, :], ones_col_f[:, :], sD[:, oc, :],
                                  start=(oc == 0), stop=(oc == DC - 1))
                    t.wait_ge(s_sq2.h, (DC + 1) * ql + oc + 1)
                    mm = t.matmul(ps8[0:1, 3, :], ones_col_f[:, :], scr2[:, oc % 2, :],
                                  start=(oc == 0), stop=(oc == DC - 1))
                    mm.then_inc(s_stat.h, 1)
                    s_stat.n += 1
                t.wait_ge(s_row.h, 2 * ql + 2)
                if ql >= 1:
                    t.wait_ge(s_nrm.h, ql)
                mm = t.matmul(ps8[:, 4, :], ones_row_f[:, :], rowbuf[:, 4, :],
                              start=True, stop=True)
                mm = t.matmul(ps8[:, 5, :], ones_row_f[:, :], rowbuf[:, 5, :],
                              start=True, stop=True)
                mm.then_inc(s_peB2.h, 1)
                s_peB2.n += 1

        @block.scalar
        def _(sc):
            for ql in range(NQL):
                for oc in range(DC):
                    gi = ql * DC + oc
                    sc.wait_ge(s_ex.h, gi + 1)
                    if gi >= 2:
                        sc.wait_ge(s_stat.h, gi - 1)
                    op = sc.activation(out=scr2[:, oc % 2, :], in_=sD[:, oc, :],
                                       func=AF.Square)
                    op.then_inc(s_sq2.h, 1)
                    s_sq2.n += 1
                sc.wait_ge(s_row.h, 2 * ql + 1)
                op = sc.activation(out=rowbuf[:, 6, :], in_=rowbuf[:, 1, :], func=AF.Sqrt)
                op.then_inc(s_sq2.h, 1)
                s_sq2.n += 1

        @block.vector
        def _(v):
            for ql in range(NQL):
                for oc in range(DC):
                    v.wait_ge(m_dld.h, 32 * (ql + 1))
                    if ql == 0 and oc == 0:
                        v.wait_ge(m_ao.h, 16 * NQC)
                    op = v.tensor_tensor(out=sD[:, oc, :], in0=aoslab[:, oc, :],
                                         in1=xslab[:, oc, :], op=OP.add)
                    op.then_inc(s_ex.h, 1)
                    s_ex.n += 1
                v.wait_ge(s_stat.h, DC * (ql + 1))
                v.tensor_scalar(out=rowbuf[:, 2, :], in0=ps8[0:1, 2, :],
                                scalar1=1.0 / D, scalar2=None, op0=OP.mult)
                v.tensor_scalar(out=rowbuf[:, 3, :], in0=ps8[0:1, 3, :],
                                scalar1=1.0 / D, scalar2=None, op0=OP.mult)
                v.tensor_tensor(out=rowbuf[:, 0, :], in0=rowbuf[:, 2, :],
                                in1=rowbuf[:, 2, :], op=OP.mult)
                v.tensor_tensor(out=rowbuf[:, 3, :], in0=rowbuf[:, 3, :],
                                in1=rowbuf[:, 0, :], op=OP.subtract)
                op = v.tensor_scalar(out=rowbuf[:, 1, :], in0=rowbuf[:, 3, :],
                                     scalar1=1e-5, scalar2=None, op0=OP.add)
                op.then_inc(s_row.h, 1)
                s_row.n += 1
                v.wait_ge(s_sq2.h, (DC + 1) * (ql + 1))
                v.reciprocal(rowbuf[:, 4, :], rowbuf[:, 6, :])
                op = v.tensor_tensor(out=rowbuf[:, 5, :], in0=rowbuf[:, 2, :],
                                     in1=rowbuf[:, 4, :], op=OP.mult)
                op.then_inc(s_row.h, 1)
                s_row.n += 1
                v.wait_ge(s_peB2.h, ql + 1)
                v.tensor_copy(rb2[:, 0, :], ps8[:, 4, :])
                v.tensor_copy(rb2[:, 1, :], ps8[:, 5, :])
                for oc in range(DC):
                    v.tensor_tensor(out=scr2[:, oc % 2, :], in0=sD[:, oc, :],
                                    in1=rb2[:, 0, :], op=OP.mult)
                    op = v.tensor_tensor(out=sD[:, oc, :], in0=scr2[:, oc % 2, :],
                                         in1=rb2[:, 1, :], op=OP.subtract)
                op.then_inc(s_nrm.h, 1)
                s_nrm.n += 1


def build_l0attn(cfg):
    c = derive(cfg)
    L, D, NHC, NJ, TH = c["L"], c["D"], c["NHC"], c["NJ"], c["TH"]
    HD = NHC * 128
    nc = bass.Bass("TRN2", target_bir_lowering=False, num_devices=8,
                   detect_race_conditions=False)
    io = {}
    io["xT"] = nc.dram_tensor("xT", [D, L], F32, kind="ExternalInput").ap()
    io["xTh"] = nc.dram_tensor("xTh", [D, TH], F32, kind="ExternalInput").ap()
    io["wq"] = nc.dram_tensor("wq", [D, NJ * 128], F32, kind="ExternalInput").ap()
    io["wk"] = nc.dram_tensor("wk", [D, HD], F32, kind="ExternalInput").ap()
    io["wv"] = nc.dram_tensor("wv", [D, HD], F32, kind="ExternalInput").ap()
    io["wl"] = nc.dram_tensor("wl", [D, NHC], F32, kind="ExternalInput").ap()
    io["bl"] = nc.dram_tensor("bl", [NHC], F32, kind="ExternalInput").ap()
    io["wo"] = nc.dram_tensor("wo", [NHC * 128, D], F32, kind="ExternalInput").ap()
    io["cosk"] = nc.dram_tensor("cosk", [128, L], F32, kind="ExternalInput").ap()
    io["sink"] = nc.dram_tensor("sink", [128, L], F32, kind="ExternalInput").ap()
    io["masks"] = nc.dram_tensor("masks", [4, 128, 512], BF16, kind="ExternalInput").ap()
    io["out"] = nc.dram_tensor("stream_out", [D, TH], F32, kind="ExternalOutput").ap()
    io["q_scr"] = nc.dram_tensor("q_scr", [NJ * 128, L], BF16).ap()
    io["lam_scr"] = nc.dram_tensor("lam_scr", [NHC, L], F32).ap()
    io["cc_in"] = nc.dram_tensor("cc_in", [2, D, TH], F32)
    io["cc_out"] = nc.dram_tensor("cc_out", [D, TH], F32)
    with ExitStack() as stack:
        block = stack.enter_context(nc.Block())
        k = K(nc, block, stack)
        emit_l0attn(k, c, io)
    return nc


def rope_tables_l0(L):
    half = 64
    inv = 1.0 / (10000.0 ** (np.arange(half, dtype=np.float64) * 2.0 / 128.0))
    freqs = np.arange(L, dtype=np.float64)[None, :] * inv[:, None]
    emb = np.concatenate([freqs, freqs], axis=0)
    return (np.cos(emb).astype(np.float32).copy(),
            np.sin(emb).astype(np.float32).copy())


def causal_masks():
    m = np.zeros((4, 128, 512), dtype=BF)
    qq = np.arange(512)[None, :]
    kk = np.arange(128)[:, None]
    for off in range(4):
        m[off] = (qq >= kk + 128 * off).astype(BF)
    return m


def prep_l0attn_inputs(cfg, inp):
    c = derive(cfg)
    L, D, NHC, NJ, TH = c["L"], c["D"], c["NHC"], c["NJ"], c["TH"]
    x = np.asarray(inp["L0_attn_x"], dtype=np.float32)
    Wq = np.asarray(inp["L0_attn_Wq"], dtype=np.float32)
    Wk = np.asarray(inp["L0_attn_Wk"], dtype=np.float32)
    Wv = np.asarray(inp["L0_attn_Wv"], dtype=np.float32)
    Wl = np.asarray(inp["L0_attn_Wl"], dtype=np.float32)
    blv = np.asarray(inp["L0_attn_bl"], dtype=np.float32)
    Wo = np.asarray(inp["L0_attn_Wo"], dtype=np.float32)
    cosk, sink = rope_tables_l0(L)
    msk = causal_masks()
    maps = []
    for core in range(8):
        b, g = core // 2, core % 2
        xT = np.ascontiguousarray(x[b].T)
        xTh = np.ascontiguousarray(x[b, g * TH:(g + 1) * TH, :].T)
        j0 = g * NJ * 128
        maps.append(dict(
            xT=xT, xTh=xTh,
            wq=np.ascontiguousarray(Wq[:, j0:j0 + NJ * 128]),
            wk=np.ascontiguousarray(Wk[:, g * NHC * 128:(g + 1) * NHC * 128]),
            wv=np.ascontiguousarray(Wv[:, g * NHC * 128:(g + 1) * NHC * 128]),
            wl=np.ascontiguousarray(Wl[:, g * NHC:(g + 1) * NHC]),
            bl=np.ascontiguousarray(blv[g * NHC:(g + 1) * NHC]),
            wo=np.ascontiguousarray(Wo[g * NHC * 128:(g + 1) * NHC * 128, :]),
            cosk=cosk, sink=sink, masks=msk,
        ))
    return maps


_PROG_CACHE = {}


def _get_prog(name, builder, cfg):
    key = (name, tuple(sorted(cfg.items())))
    if key not in _PROG_CACHE:
        _PROG_CACHE[key] = builder(cfg)
    return _PROG_CACHE[key]


def kernel(**inputs):
    cfg = FULL
    c = derive(cfg)
    TH, D, L = c["TH"], c["D"], c["L"]
    cores = list(range(8))
    nc1 = _get_prog("l0attn", build_l0attn, cfg)
    res1 = run_bass_kernel_spmd(nc1, prep_l0attn_inputs(cfg, inputs), core_ids=cores)
    streams = [np.ascontiguousarray(res1.results[i]["stream_out"]) for i in range(8)]
    nc2 = _get_prog("l0ffn", build_l0ffn, cfg)
    res2 = run_bass_kernel_spmd(nc2, prep_l0ffn_inputs(cfg, inputs, streams), core_ids=cores)
    streams = [np.ascontiguousarray(res2.results[i]["stream_out"]) for i in range(8)]
    nc3 = _get_prog("l1attn", build_l1attn, cfg)
    res3 = run_bass_kernel_spmd(nc3, prep_l1attn_inputs(cfg, inputs, streams), core_ids=cores)
    streams = [np.ascontiguousarray(res3.results[i]["stream_out"]) for i in range(8)]
    nc4 = _get_prog("l1ffn", build_l1ffn, cfg)
    res4 = run_bass_kernel_spmd(nc4, prep_l1ffn_inputs(cfg, inputs, streams), core_ids=cores)
    streams = [res4.results[i]["stream_out"] for i in range(8)]
    out = np.zeros((4, L, D), dtype=np.float32)
    for core in range(8):
        b, g = core // 2, core % 2
        out[b, g * TH:(g + 1) * TH, :] = streams[core].T
    return out
def emit_l0ffn(k, c, io):
    nc, block = k.nc, k.block
    L, D, DC, NHC = c["L"], c["D"], c["DC"], c["NHC"]
    TH, NQL = c["TH"], c["NQL"]
    I0 = c["I0"]
    NG = I0 // 128
    W0 = D // 8
    G0 = D // 128
    W1 = I0 // 8
    G1 = I0 // 128
    NHW = max(1, D // 512)
    HB = min(D, 512)
    NTCH = TH // 512

    ones_row_f = k.sb("ones_row_f", [1, 128], F32)
    ones_col_f = k.sb("ones_col_f", [128, 1], F32)
    identb = k.sb("identb", [128, 128], BF16)
    ps8 = k.ps("ps8", [128, 8, 512])
    psbf = ps8.bitcast(BF16)

    m_id = k.sem("m_id")
    m_pk = k.sem("m_pk")
    m_sc = k.sem("m_sc")
    m_sg = k.sem("m_sg")
    m_sd = k.sem("m_sd")
    m_xbf = k.sem("m_xbf")
    m_wgu = [k.sem("m_wgu0"), k.sem("m_wgu1")]
    m_wdn = [k.sem("m_wdn0"), k.sem("m_wdn1")]
    m_str = [k.sem("m_str0"), k.sem("m_str1")]
    m_out = k.sem("m_out")

    s_dq = k.sem("s_dq")
    s_fn = k.sem("s_fn")
    s_tr = k.sem("s_tr")
    s_ev = k.sem("s_ev")
    s_gu = k.sem("s_gu")
    s_si = k.sem("s_si")
    s_hb = k.sem("s_hb")
    s_dn = k.sem("s_dn")
    s_ye = k.sem("s_ye")
    s_sq = k.sem("s_sq")
    s_st2 = k.sem("s_st2")
    s_row = k.sem("s_row")
    s_bc = k.sem("s_bc")
    s_rs = k.sem("s_rs")
    s_nrm = k.sem("s_nrm")

    NT1 = 2 * NG + DC
    if "dbg_val" in io:
        io["_dbgf"] = k.sem("m_dbgf")

    def tinfo(ti):
        if ti < 2 * NG:
            return ("gu", ti)
        return ("dn", ti - 2 * NG)

    def ntr_of(ti):
        return DC if tinfo(ti)[0] == "gu" else NG

    trbase = [0]
    for ti in range(NT1):
        trbase.append(trbase[-1] + ntr_of(ti))

    with ExitStack() as sf:
        kf = K(nc, block, sf)
        pk_sb = kf.sb("F_pk", [128, max(W0, W1)], I32)
        nib = kf.sb("F_nib", [128, 2, max(W0, W1)], I32)
        ge8 = kf.sb("F_ge8", [128, 2, max(W0, W1)], F32)
        bits = kf.sb("F_bits", [128, 2, max(W0, W1)], U32)
        val = kf.sb("F_val", [128, max(8 * W0, 8 * W1)], BF16)
        scT = kf.sb("F_scT", [128, max(G0, G1)], F32)
        scP = kf.sb("F_scP", [128, max(G0, G1)], F32)
        stg = kf.sb("F_stg", [128, max(DC, NG), 128], BF16)

        @block.gpsimd
        def _(g):
            for ti in range(NT1):
                kind, idx = tinfo(ti)
                if ti >= 1:
                    g.wait_ge(s_dq.h, ti)
                if kind == "gu":
                    g.dma_start(out=pk_sb[:, 0:W0],
                                in_=io["gup"][idx * 128:(idx + 1) * 128, :]).then_inc(m_pk.h, 16)
                    g.dma_start(out=scT[:, 0:G0],
                                in_=io["gusT"][idx * 128:(idx + 1) * 128, :]).then_inc(m_sc.h, 16)
                else:
                    g.dma_start(out=pk_sb[:, 0:W1],
                                in_=io["dnp"][idx * 128:(idx + 1) * 128, :]).then_inc(m_pk.h, 16)
                    g.dma_start(out=scT[:, 0:G1],
                                in_=io["dnsT"][idx * 128:(idx + 1) * 128, :]).then_inc(m_sc.h, 16)

        @block.vector
        def _(v):
            v.memset(ones_row_f[:, :], 1.0)
            v.memset(ones_col_f[:, :], 1.0)
            ngu_st = 0
            ndn_st = 0
            for ti in range(NT1):
                kind, idx = tinfo(ti)
                nw = W0 if kind == "gu" else W1
                ngr = G0 if kind == "gu" else G1
                ntr = ntr_of(ti)
                v.wait_ge(m_pk.h, 16 * (ti + 1))
                v.wait_ge(m_sc.h, 16 * (ti + 1))
                v.tensor_scalar(out=scP[:, 0:ngr], in0=scT[:, 0:ngr],
                                scalar1=float(2.0 ** 126), scalar2=None, op0=OP.mult)
                for kp in range(0, 8, 2):
                    for q_ in (0, 1):
                        op = v.tensor_scalar(out=nib[:, q_, 0:nw], in0=pk_sb[:, 0:nw],
                                             scalar1=4 * (kp + q_), scalar2=15,
                                             op0=OP.logical_shift_right, op1=OP.bitwise_and)
                    op.then_inc(s_fn.h, 1)
                    s_fn.n += 1
                    v.wait_ge(s_fn.h, s_fn.n)
                    for q_ in (0, 1):
                        op = v.tensor_scalar(out=ge8[:, q_, 0:nw], in0=nib[:, q_, 0:nw],
                                             scalar1=8.0, scalar2=float(2 ** 31 - 2 ** 25),
                                             op0=OP.is_ge, op1=OP.mult)
                    op.then_inc(s_fn.h, 1)
                    s_fn.n += 1
                    v.wait_ge(s_fn.h, s_fn.n)
                    for q_ in (0, 1):
                        op = v.scalar_tensor_tensor(out=bits[:, q_, 0:nw], in0=nib[:, q_, 0:nw],
                                                    scalar=float(2 ** 22), in1=ge8[:, q_, 0:nw],
                                                    op0=OP.mult, op1=OP.add)
                    op.then_inc(s_fn.h, 1)
                    s_fn.n += 1
                    v.wait_ge(s_fn.h, s_fn.n)
                    for q_ in (0, 1):
                        op = v.tensor_tensor(
                            out=val[:, kp + q_:8 * nw:8],
                            in0=bits[:, q_, 0:nw].bitcast(F32),
                            in1=scP[:, 0:ngr, None].broadcast_to([128, ngr, 16]),
                            op=OP.mult)
                op.then_inc(s_dq.h, 1)
                s_dq.n += 1
                for tr in range(ntr):
                    gi = trbase[ti] + tr
                    v.wait_ge(s_tr.h, gi + 1)
                    if tr == 0 and ti >= 1:
                        pkind, _ = tinfo(ti - 1)
                        if pkind == "gu":
                            v.wait_ge(m_sg.h, 16 * ngu_st)
                        else:
                            v.wait_ge(m_sd.h, 16 * ndn_st)
                    op = v.tensor_copy(stg[:, tr, :], psbf[:, gi % 4, 0:128])
                    op.then_inc(s_ev.h, 1)
                    s_ev.n += 1
                if kind == "gu":
                    ngu_st += 1
                else:
                    ndn_st += 1

        @block.tensor
        def _(t):
            t.wait_ge(m_id.h, 16)
            for ti in range(NT1):
                ntr = ntr_of(ti)
                t.wait_ge(s_dq.h, ti + 1)
                for tr in range(ntr):
                    gi = trbase[ti] + tr
                    if gi >= 4:
                        t.wait_ge(s_ev.h, gi - 3)
                    mm = t.transpose(psbf[:, gi % 4, 0:128],
                                     val[:, tr * 128:(tr + 1) * 128], identb[:, :])
                    mm.then_inc(s_tr.h, 1)
                    s_tr.n += 1

        @block.sync
        def _(s):
            s.dma_start(out=identb[:, :], in_=io["ident"]).then_inc(m_id.h, 16)
            for ti in range(NT1):
                kind, idx = tinfo(ti)
                if "dbg_val" in io and ti == io["dbg_ti"]:
                    s.wait_ge(s_dq.h, ti + 1)
                    s.dma_start(out=io["dbg_val"], in_=val[:, 0:8*W0]).then_inc(io["_dbgf"].h, 16)
                s.wait_ge(s_ev.h, trbase[ti + 1])
                if kind == "gu":
                    s.dma_start(out=io["wguT_scr"][idx].rearrange("c p o -> p c o"),
                                in_=stg[:, 0:DC, :]).then_inc(m_sg.h, 16)
                else:
                    hw_ = idx // (HB // 128)
                    hc_ = idx % (HB // 128)
                    s.dma_start(
                        out=io["wdnT_scr"][:, hw_, :, hc_ * 128:(hc_ + 1) * 128]
                        .rearrange("a pi h -> pi a h"),
                        in_=stg[:, 0:NG, :]).then_inc(m_sd.h, 16)

    with ExitStack() as sf2:
        kf2 = K(nc, block, sf2)
        xbf = kf2.sb("G_xbf", [128, DC, 512], BF16)
        wgu = kf2.sb("G_wgu", [128, 2, 2, DC, 128], BF16)
        wdn = kf2.sb("G_wdn", [128, 2, NG, 128], BF16)
        hbuf = kf2.sb("G_hbuf", [128, NG, 512], BF16)
        gsil = kf2.sb("G_gsil", [128, 2, 512], BF16)
        ybuf = kf2.sb("G_ybuf", [128, DC, 512], F32)
        strsl = kf2.sb("G_strsl", [128, 2, 512], F32)
        scr2 = kf2.sb("G_scr2", [128, 2, 512], F32)
        rowbuf = kf2.sb("G_rows", [1, 8, 512], F32)
        rb2 = kf2.sb("G_rb2", [128, 2, 512], F32)
        rmsb = kf2.sb("G_rmsb", [128, 512], F32)
        atmp = kf2.sb("G_atmp", [128, 512], F32)

        @block.gpsimd
        def _(g):
            g.wait_ge(s_ev.h, trbase[NT1])
            g.wait_ge(m_sg.h, 16 * 2 * NG)
            g.wait_ge(m_sd.h, 16 * DC)
            for tc in range(NTCH):
                t0 = tc * 512
                if tc >= 1:
                    g.wait_ge(s_gu.h, NG * tc)
                g.dma_start(out=xbf[:, :, :],
                            in_=io["stream_in"].rearrange("(c p) t -> p c t", p=128)[:, :, t0:t0 + 512]
                            ).then_inc(m_xbf.h, 16)

        @block.sync
        def _(s):
            s.wait_ge(m_sg.h, 16 * 2 * NG)
            s.wait_ge(m_sd.h, 16 * DC)
            for tc in range(NTCH):
                t0 = tc * 512
                for it in range(NG):
                    gi = tc * NG + it
                    if gi >= 2:
                        s.wait_ge(s_gu.h, gi - 1)
                    s.dma_start(out=wgu[:, it % 2, 0, :, :],
                                in_=io["wguT_scr"][it].rearrange("c p o -> p c o")
                                ).then_inc(m_wgu[it % 2].h, 16)
                    s.dma_start(out=wgu[:, it % 2, 1, :, :],
                                in_=io["wguT_scr"][NG + it].rearrange("c p o -> p c o")
                                ).then_inc(m_wgu[it % 2].h, 16)
                for ho in range(DC):
                    g2 = tc * DC + ho
                    if g2 >= 2:
                        s.wait_ge(s_dn.h, g2 - 1)
                    s.dma_start(out=wdn[:, ho % 2, :, :],
                                in_=io["wdnT_scr"][:, ho // 4, :, (ho % 4) * 128:(ho % 4) * 128 + 128]
                                .rearrange("c p o -> p c o")).then_inc(m_wdn[ho % 2].h, 16)
                for ho in range(DC):
                    g2 = tc * DC + ho
                    if g2 >= 2:
                        s.wait_ge(s_rs.h, g2 - 1)
                    s.dma_start(out=strsl[:, g2 % 2, :],
                                in_=io["stream_in"].rearrange("(c p) t -> p c t", p=128)[:, ho, t0:t0 + 512]
                                ).then_inc(m_str[g2 % 2].h, 16)
                s.wait_ge(s_nrm.h, tc + 1)
                s.dma_start(out=io["out"].rearrange("(c p) t -> p c t", p=128)[:, :, t0:t0 + 512],
                            in_=ybuf[:, :, :]).then_inc(m_out.h, 16)
            s.wait_ge(m_out.h, 16 * NTCH)

        @block.tensor
        def _(t):
            for tc in range(NTCH):
                t.wait_ge(m_xbf.h, 16 * (tc + 1))
                for it in range(NG):
                    gi = tc * NG + it
                    t.wait_ge(m_wgu[it % 2].h, 32 * (gi // 2 + 1))
                    if gi >= 2:
                        t.wait_ge(s_hb.h, gi - 1)
                    if it == 0 and tc >= 1:
                        t.wait_ge(s_nrm.h, tc)
                    for kc in range(DC):
                        mm = t.matmul(ps8[:, 2 * (gi % 2), :], wgu[:, it % 2, 0, kc, :],
                                      xbf[:, kc, :], start=(kc == 0), stop=(kc == DC - 1))
                    for kc in range(DC):
                        mm2 = t.matmul(ps8[:, 2 * (gi % 2) + 1, :], wgu[:, it % 2, 1, kc, :],
                                       xbf[:, kc, :], start=(kc == 0), stop=(kc == DC - 1))
                    mm2.then_inc(s_gu.h, 1)
                    s_gu.n += 1
                for ho in range(DC):
                    g2 = tc * DC + ho
                    t.wait_ge(m_wdn[ho % 2].h, 16 * (g2 // 2 + 1))
                    if g2 >= 2:
                        t.wait_ge(s_ye.h, g2 - 1)
                    for ic in range(NG):
                        t.wait_ge(s_hb.h, tc * NG + ic + 1)
                        mm = t.matmul(ps8[:, 4 + g2 % 2, :], wdn[:, ho % 2, ic, :],
                                      hbuf[:, ic, :], start=(ic == 0), stop=(ic == NG - 1))
                    mm.then_inc(s_dn.h, 1)
                    s_dn.n += 1
                for ho in range(DC):
                    t.wait_ge(s_sq.h, (2 * DC + 2) * tc + ho + 1)
                    if ho == 0 and tc >= 1:
                        t.wait_ge(s_row.h, 4 * tc - 1)
                    mm = t.matmul(ps8[0:1, 6, :], ones_col_f[:, :], scr2[:, ho % 2, :],
                                  start=(ho == 0), stop=(ho == DC - 1))
                    mm.then_inc(s_st2.h, 1)
                    s_st2.n += 1
                t.wait_ge(s_row.h, 4 * tc + 2)
                mm = t.matmul(ps8[:, 7, :], ones_row_f[:, :], rowbuf[:, 4, :],
                              start=True, stop=True)
                mm.then_inc(s_bc.h, 1)
                s_bc.n += 1
                for ho in range(DC):
                    t.wait_ge(s_sq.h, (2 * DC + 2) * tc + DC + 1 + ho + 1)
                    if ho == 0:
                        t.wait_ge(s_row.h, 4 * tc + 1)
                    mm = t.matmul(ps8[0:1, 0, :], ones_col_f[:, :], ybuf[:, ho, :],
                                  start=(ho == 0), stop=(ho == DC - 1))
                    mm2 = t.matmul(ps8[0:1, 6, :], ones_col_f[:, :], scr2[:, ho % 2, :],
                                   start=(ho == 0), stop=(ho == DC - 1))
                    mm2.then_inc(s_st2.h, 1)
                    s_st2.n += 1
                t.wait_ge(s_row.h, 4 * tc + 4)
                mm = t.matmul(ps8[:, 7, :], ones_row_f[:, :], rowbuf[:, 4, :],
                              start=True, stop=True)
                mm2 = t.matmul(ps8[:, 1, :], ones_row_f[:, :], rowbuf[:, 5, :],
                               start=True, stop=True)
                mm2.then_inc(s_bc.h, 1)
                s_bc.n += 1

        @block.scalar
        def _(sc):
            for tc in range(NTCH):
                for it in range(NG):
                    gi = tc * NG + it
                    sc.wait_ge(s_gu.h, gi + 1)
                    if gi >= 2:
                        sc.wait_ge(s_hb.h, gi - 1)
                    op = sc.activation(out=gsil[:, gi % 2, :], in_=ps8[:, 2 * (gi % 2), :],
                                       func=AF.Silu)
                    op.then_inc(s_si.h, 1)
                    s_si.n += 1
                for ho in range(DC):
                    g2 = tc * DC + ho
                    sc.wait_ge(s_ye.h, g2 + 1)
                    if ho >= 2:
                        sc.wait_ge(s_st2.h, 2 * DC * tc + ho - 1)
                    op = sc.activation(out=scr2[:, ho % 2, :], in_=ybuf[:, ho, :],
                                       func=AF.Square)
                    op.then_inc(s_sq.h, 1)
                    s_sq.n += 1
                sc.wait_ge(s_row.h, 4 * tc + 1)
                op = sc.activation(out=rowbuf[:, 6, :], in_=rowbuf[:, 1, :], func=AF.Sqrt)
                op.then_inc(s_sq.h, 1)
                s_sq.n += 1
                for ho in range(DC):
                    g2 = tc * DC + ho
                    sc.wait_ge(s_rs.h, g2 + 1)
                    if ho >= 2:
                        sc.wait_ge(s_st2.h, 2 * DC * tc + DC + ho - 1)
                    op = sc.activation(out=scr2[:, ho % 2, :], in_=ybuf[:, ho, :],
                                       func=AF.Square)
                    op.then_inc(s_sq.h, 1)
                    s_sq.n += 1
                sc.wait_ge(s_row.h, 4 * tc + 3)
                op = sc.activation(out=rowbuf[:, 6, :], in_=rowbuf[:, 1, :], func=AF.Sqrt)
                op.then_inc(s_sq.h, 1)
                s_sq.n += 1

        @block.vector
        def _(v):
            for tc in range(NTCH):
                for it in range(NG):
                    gi = tc * NG + it
                    v.wait_ge(s_si.h, gi + 1)
                    if it == 0 and tc >= 1:
                        v.wait_ge(s_dn.h, DC * tc)
                    op = v.tensor_tensor(out=hbuf[:, it, :], in0=gsil[:, gi % 2, :],
                                         in1=ps8[:, 2 * (gi % 2) + 1, :], op=OP.mult)
                    op.then_inc(s_hb.h, 1)
                    s_hb.n += 1
                for ho in range(DC):
                    g2 = tc * DC + ho
                    v.wait_ge(s_dn.h, g2 + 1)
                    if ho == 0:
                        v.wait_ge(m_out.h, 16 * tc)
                    op = v.tensor_copy(ybuf[:, ho, :], ps8[:, 4 + g2 % 2, :])
                    op.then_inc(s_ye.h, 1)
                    s_ye.n += 1
                v.wait_ge(s_st2.h, 2 * DC * tc + DC)
                v.tensor_scalar(out=rowbuf[:, 0, :], in0=ps8[0:1, 6, :],
                                scalar1=1.0 / D, scalar2=None, op0=OP.mult)
                op = v.tensor_scalar(out=rowbuf[:, 1, :], in0=rowbuf[:, 0, :],
                                     scalar1=1e-6, scalar2=None, op0=OP.add)
                op.then_inc(s_row.h, 1)
                s_row.n += 1
                v.wait_ge(s_sq.h, (2 * DC + 2) * tc + DC + 1)
                op = v.reciprocal(rowbuf[:, 4, :], rowbuf[:, 6, :])
                op.then_inc(s_row.h, 1)
                s_row.n += 1
                v.wait_ge(s_bc.h, 2 * tc + 1)
                v.tensor_copy(rmsb[:, :], ps8[:, 7, :])
                for ho in range(DC):
                    g2 = tc * DC + ho
                    v.wait_ge(m_str[g2 % 2].h, 16 * (g2 // 2 + 1))
                    v.tensor_tensor(out=atmp[:, :], in0=ybuf[:, ho, :],
                                    in1=rmsb[:, :], op=OP.mult)
                    op = v.tensor_tensor(out=ybuf[:, ho, :], in0=atmp[:, :],
                                         in1=strsl[:, g2 % 2, :], op=OP.add)
                    op.then_inc(s_rs.h, 1)
                    s_rs.n += 1
                v.wait_ge(s_st2.h, 2 * DC * (tc + 1))
                v.tensor_scalar(out=rowbuf[:, 2, :], in0=ps8[0:1, 0, :],
                                scalar1=1.0 / D, scalar2=None, op0=OP.mult)
                v.tensor_scalar(out=rowbuf[:, 3, :], in0=ps8[0:1, 6, :],
                                scalar1=1.0 / D, scalar2=None, op0=OP.mult)
                v.tensor_tensor(out=rowbuf[:, 0, :], in0=rowbuf[:, 2, :],
                                in1=rowbuf[:, 2, :], op=OP.mult)
                v.tensor_tensor(out=rowbuf[:, 3, :], in0=rowbuf[:, 3, :],
                                in1=rowbuf[:, 0, :], op=OP.subtract)
                op = v.tensor_scalar(out=rowbuf[:, 1, :], in0=rowbuf[:, 3, :],
                                     scalar1=1e-5, scalar2=None, op0=OP.add)
                op.then_inc(s_row.h, 1)
                s_row.n += 1
                v.wait_ge(s_sq.h, (2 * DC + 2) * (tc + 1))
                v.reciprocal(rowbuf[:, 4, :], rowbuf[:, 6, :])
                op = v.tensor_tensor(out=rowbuf[:, 5, :], in0=rowbuf[:, 2, :],
                                     in1=rowbuf[:, 4, :], op=OP.mult)
                op.then_inc(s_row.h, 1)
                s_row.n += 1
                v.wait_ge(s_bc.h, 2 * tc + 2)
                v.tensor_copy(rb2[:, 0, :], ps8[:, 7, :])
                v.tensor_copy(rb2[:, 1, :], ps8[:, 1, :])
                for ho in range(DC):
                    v.tensor_tensor(out=atmp[:, :], in0=ybuf[:, ho, :],
                                    in1=rb2[:, 0, :], op=OP.mult)
                    op = v.tensor_tensor(out=ybuf[:, ho, :], in0=atmp[:, :],
                                         in1=rb2[:, 1, :], op=OP.subtract)
                op.then_inc(s_nrm.h, 1)
                s_nrm.n += 1


def build_l0ffn(cfg):
    c = derive(cfg)
    D, TH, I0, DC = c["D"], c["TH"], c["I0"], c["DC"]
    NG = I0 // 128
    NHW = max(1, D // 512)
    HB = min(D, 512)
    nc = bass.Bass("TRN2", target_bir_lowering=False, num_devices=8,
                   detect_race_conditions=False)
    io = {}
    io["stream_in"] = nc.dram_tensor("stream_in", [D, TH], F32, kind="ExternalInput").ap()
    io["gup"] = nc.dram_tensor("gup", [2 * I0, D // 8], I32, kind="ExternalInput").ap()
    io["gusT"] = nc.dram_tensor("gusT", [2 * I0, D // 128], F16, kind="ExternalInput").ap()
    io["dnp"] = nc.dram_tensor("dnp", [D, I0 // 8], I32, kind="ExternalInput").ap()
    io["dnsT"] = nc.dram_tensor("dnsT", [D, I0 // 128], F16, kind="ExternalInput").ap()
    io["ident"] = nc.dram_tensor("ident", [128, 128], BF16, kind="ExternalInput").ap()
    io["out"] = nc.dram_tensor("stream_out", [D, TH], F32, kind="ExternalOutput").ap()
    io["wguT_scr"] = nc.dram_tensor("wguT_scr", [2 * NG, DC, 128, 128], BF16).ap()
    io["wdnT_scr"] = nc.dram_tensor("wdnT_scr", [NG, NHW, 128, HB], BF16).ap()
    with ExitStack() as stack:
        block = stack.enter_context(nc.Block())
        k = K(nc, block, stack)
        emit_l0ffn(k, c, io)
    return nc


def prep_l0ffn_inputs(cfg, inp, streams):
    c = derive(cfg)
    gup = np.ascontiguousarray(np.asarray(inp["L0_ffn_gate_up_packed"], dtype=np.int32))
    gusT = np.ascontiguousarray(np.asarray(inp["L0_ffn_gate_up_scales"]).T)
    dnp = np.ascontiguousarray(np.asarray(inp["L0_ffn_down_packed"], dtype=np.int32))
    dnsT = np.ascontiguousarray(np.asarray(inp["L0_ffn_down_scales"]).T)
    ident = np.eye(128, dtype=BF)
    maps = []
    for core in range(8):
        maps.append(dict(stream_in=streams[core], gup=gup, gusT=gusT,
                         dnp=dnp, dnsT=dnsT, ident=ident))
    return maps
def emit_l1attn(k, c, io):
    nc, block = k.nc, k.block
    D, DC = c["D"], c["DC"]
    TH = c["TH"]
    E3 = 3 * DC
    RH = DC // 4
    NTCH = TH // 512
    RPT = 2 * RH + DC

    ones_row_f = k.sb("ones_row_f", [1, 128], F32)
    ones_col_f = k.sb("ones_col_f", [128, 1], F32)
    identf = k.sb("identf_sb", [128, 128], F32)
    binT = k.sb("binT_sb", [128, E3], F32)
    boutT = k.sb("boutT_sb", [128, DC], F32)
    aacc = k.sb("aacc", [128, DC], F32)
    aA = k.sb("aA", [128, DC], F32)
    art = k.sb("art", [128, 128], F32)
    ps8 = k.ps("ps8", [128, 8, 512])

    m_cst = k.sem("m_cst")
    m_xbf = k.sem("m_xbf")
    m_win = [k.sem("m_win0"), k.sem("m_win1")]
    m_cs = k.sem("m_cs")
    m_qst = [k.sem("m_qst0"), k.sem("m_qst1")]
    m_ain = k.sem("m_ain")
    m_aout = k.sem("m_aout")
    m_qld = k.sem("m_qld")
    m_wout = [k.sem("m_wout0"), k.sem("m_wout1")]
    m_str = [k.sem("m_str0"), k.sem("m_str1")]
    m_out = k.sem("m_out")

    s_qkv = k.sem("s_qkv")
    s_ev1 = k.sem("s_ev1")
    s_rp = k.sem("s_rp")
    s_kst = k.sem("s_kst")
    s_nrw = k.sem("s_nrw")
    s_sqr = k.sem("s_sqr")
    s_bc1 = k.sem("s_bc1")
    s_ap = k.sem("s_ap")
    s_tra = k.sem("s_tra")
    s_aev = k.sem("s_aev")
    s_red = k.sem("s_red")
    s_cc = k.sem("s_cc")
    s_aok = k.sem("s_aok")
    s_yb = k.sem("s_yb")
    s_op = k.sem("s_op")
    s_ev2 = k.sem("s_ev2")
    s_rs = k.sem("s_rs")
    s_sq = k.sem("s_sq")
    s_st2 = k.sem("s_st2")
    s_row = k.sem("s_row")
    s_bc = k.sem("s_bc")
    s_nrm = k.sem("s_nrm")

    io["_m_ain"] = m_ain
    io["_sccn"] = s_cc
    if "dbg_k" in io:
        io["_dbg"] = k.sem("m_dbg")
    with ExitStack() as s1:
        k1 = K(nc, block, s1)
        xbf = k1.sb("H_xbf", [128, DC, 512], BF16)
        winb = k1.sb("H_win", [128, 2, DC, 128], BF16)
        kfull = k1.sb("H_k", [128, DC, 512], F32)
        vstg = k1.sb("H_v", [128, DC, 512], F32)
        qstg = k1.sb("H_q", [128, DC, 512], F32)
        cosb = k1.sb("H_cos", [128, RH, 512], F32)
        sinb = k1.sb("H_sin", [128, RH, 512], F32)
        rt1 = k1.sb("H_rt1", [128, 512], F32)
        rt2 = k1.sb("H_rt2", [128, 512], F32)
        rt3 = k1.sb("H_rt3", [128, 512], F32)
        qbf_st = k1.sb("H_qbf", [128, 2, 512], BF16)
        rrow = k1.sb("H_rrow", [1, 512], F32)
        rbb = k1.sb("H_rbb", [128, 512], F32)
        scr1 = k1.sb("H_scr1", [128, 2, 512], F32)

        @block.gpsimd
        def _(g):
            for tc in range(NTCH):
                t0 = tc * 512
                if tc >= 1:
                    g.wait_ge(s_qkv.h, E3 * tc)
                g.dma_start(out=xbf[:, :, :],
                            in_=io["stream_in"].rearrange("(c p) t -> p c t", p=128)[:, :, t0:t0 + 512]
                            ).then_inc(m_xbf.h, 16)
                for et in range(E3):
                    gi = tc * E3 + et
                    if gi >= 2:
                        g.wait_ge(s_qkv.h, gi - 1)
                    g.dma_start(out=winb[:, et % 2, :, :],
                                in_=io["w_inT"].rearrange("(c p) o -> p c o", p=128)[:, :, et * 128:(et + 1) * 128]
                                ).then_inc(m_win[et % 2].h, 16)
            g.wait_ge(m_ain.h, 16)
            g.collective_compute("AllReduce", OP.add,
                                 replica_groups=[[0, 1], [2, 3], [4, 5], [6, 7]],
                                 ins=[io["cc_a_in"].ap().opt()],
                                 outs=[io["cc_a_out"].ap().opt()]).then_inc(s_cc.h, 1)
            s_cc.n += 1

        @block.sync
        def _(s):
            s.dma_start(out=binT[:, :], in_=io["binT"]).then_inc(m_cst.h, 16)
            s.dma_start(out=boutT[:, :], in_=io["boutT"]).then_inc(m_cst.h, 16)
            s.dma_start(out=identf[:, :], in_=io["identf"]).then_inc(m_cst.h, 16)
            for tc in range(NTCH):
                t0 = tc * 512
                if tc >= 1:
                    s.wait_ge(s_rp.h, RPT * tc)
                s.dma_start(out=cosb[:, :, :],
                            in_=io["cosT1"].rearrange("(c p) t -> p c t", p=128)[:, :, t0:t0 + 512]
                            ).then_inc(m_cs.h, 16)
                s.dma_start(out=sinb[:, :, :],
                            in_=io["sinT1"].rearrange("(c p) t -> p c t", p=128)[:, :, t0:t0 + 512]
                            ).then_inc(m_cs.h, 16)
                for qi in range(DC):
                    s.wait_ge(s_rp.h, RPT * tc + 2 * RH + qi + 1)
                    s.dma_start(out=io["q_scr"][qi * 128:(qi + 1) * 128, t0:t0 + 512],
                                in_=qbf_st[:, qi % 2, :]).then_inc(m_qst[qi % 2].h, 16)
                if tc == 0 and "dbg_k" in io:
                    s.wait_ge(s_ap.h, 1)
                    s.dma_start(out=io["dbg_k"], in_=kfull[:, :, :]).then_inc(io["_dbg"].h, 16)
                    s.dma_start(out=io["dbg_v"], in_=vstg[:, :, :]).then_inc(io["_dbg"].h, 16)
                    s.dma_start(out=io["dbg_r"], in_=rbb[:, :]).then_inc(io["_dbg"].h, 16)
                    s.dma_start(out=io["dbg_a0"], in_=aacc[:, :]).then_inc(io["_dbg"].h, 16)
                    io["_dbg"].n += 64
            s.wait_ge(s_ap.h, NTCH)
            s.dma_start(out=io["cc_a_in"].ap(), in_=aacc[:, :]).then_inc(m_ain.h, 16)
            s.wait_ge(s_cc.h, 1)
            s.dma_start(out=aA[:, :], in_=io["cc_a_out"].ap()).then_inc(m_aout.h, 16)

        @block.tensor
        def _(t):
            t.wait_ge(m_cst.h, 48)
            for tc in range(NTCH):
                t.wait_ge(m_xbf.h, 16 * (tc + 1))
                for et in range(E3):
                    gi = tc * E3 + et
                    t.wait_ge(m_win[et % 2].h, 16 * (gi // 2 + 1))
                    if gi >= 4:
                        t.wait_ge(s_ev1.h, gi - 3)
                    for kc in range(DC):
                        mm = t.matmul(ps8[:, gi % 4, :], winb[:, et % 2, kc, :],
                                      xbf[:, kc, :], start=(kc == 0), stop=(kc == DC - 1))
                    mm.then_inc(s_qkv.h, 1)
                    s_qkv.n += 1
                for kc in range(DC):
                    t.wait_ge(s_sqr.h, (DC + 1) * tc + kc + 1)
                    if kc == 0 and tc >= 1:
                        t.wait_ge(s_nrw.h, 2 * tc)
                    mm = t.matmul(ps8[0:1, 6, :], ones_col_f[:, :], scr1[:, kc % 2, :],
                                  start=(kc == 0), stop=(kc == DC - 1))
                    mm.then_inc(s_kst.h, 1)
                    s_kst.n += 1
                t.wait_ge(s_nrw.h, 2 * tc + 2)
                mm = t.matmul(ps8[:, 7, :], ones_row_f[:, :], rrow[:, :],
                              start=True, stop=True)
                mm.then_inc(s_bc1.h, 1)
                s_bc1.n += 1

        @block.scalar
        def _(sc):
            sc.wait_ge(m_cst.h, 48)
            for tc in range(NTCH):
                for et in range(E3):
                    gi = tc * E3 + et
                    sc.wait_ge(s_qkv.h, gi + 1)
                    if et == 0 and tc >= 1:
                        sc.wait_ge(s_ap.h, tc)
                    bias = binT[:, et:et + 1]
                    if et < DC:
                        dst = qstg[:, et, :]
                    elif et < 2 * DC:
                        dst = kfull[:, et - DC, :]
                    else:
                        dst = vstg[:, et - 2 * DC, :]
                    op = sc.activation(out=dst, in_=ps8[:, gi % 4, :],
                                       func=AF.Identity, bias=bias)
                    op.then_inc(s_ev1.h, 1)
                    s_ev1.n += 1
                for kc in range(DC):
                    if kc < 2 * RH:
                        sc.wait_ge(s_rp.h, RPT * tc + RH)
                    if kc >= 2:
                        sc.wait_ge(s_kst.h, DC * tc + kc - 1)
                    op = sc.activation(out=scr1[:, kc % 2, :], in_=kfull[:, kc, :],
                                       func=AF.Square)
                    op.then_inc(s_sqr.h, 1)
                    s_sqr.n += 1
                sc.wait_ge(s_nrw.h, 2 * tc + 1)
                op = sc.activation(out=rrow[:, :], in_=rrow[:, :], func=AF.Sqrt)
                op.then_inc(s_sqr.h, 1)
                s_sqr.n += 1

        @block.vector
        def _(v):
            v.memset(ones_row_f[:, :], 1.0)
            v.memset(ones_col_f[:, :], 1.0)
            v.memset(aacc[:, :], 0.0)
            for tc in range(NTCH):
                v.wait_ge(m_cs.h, 32 * (tc + 1))
                for cp in range(RH):
                    v.wait_ge(s_ev1.h, tc * E3 + DC + RH + cp + 1)
                    lo = kfull[:, cp, :]
                    hi = kfull[:, cp + RH, :]
                    v.tensor_tensor(out=rt1[:, :], in0=lo, in1=cosb[:, cp, :], op=OP.mult)
                    v.tensor_tensor(out=rt2[:, :], in0=hi, in1=sinb[:, cp, :], op=OP.mult)
                    v.tensor_tensor(out=rt3[:, :], in0=rt1[:, :], in1=rt2[:, :], op=OP.subtract)
                    v.tensor_tensor(out=rt1[:, :], in0=hi, in1=cosb[:, cp, :], op=OP.mult)
                    v.tensor_tensor(out=rt2[:, :], in0=lo, in1=sinb[:, cp, :], op=OP.mult)
                    v.tensor_tensor(out=hi, in0=rt1[:, :], in1=rt2[:, :], op=OP.add)
                    op = v.tensor_copy(lo, rt3[:, :])
                    op.then_inc(s_rp.h, 1)
                    s_rp.n += 1
                for cp in range(RH):
                    v.wait_ge(s_ev1.h, tc * E3 + RH + cp + 1)
                    lo = qstg[:, cp, :]
                    hi = qstg[:, cp + RH, :]
                    v.tensor_tensor(out=rt1[:, :], in0=lo, in1=cosb[:, cp, :], op=OP.mult)
                    v.tensor_tensor(out=rt2[:, :], in0=hi, in1=sinb[:, cp, :], op=OP.mult)
                    v.tensor_tensor(out=rt3[:, :], in0=rt1[:, :], in1=rt2[:, :], op=OP.subtract)
                    v.tensor_tensor(out=rt1[:, :], in0=hi, in1=cosb[:, cp, :], op=OP.mult)
                    v.tensor_tensor(out=rt2[:, :], in0=lo, in1=sinb[:, cp, :], op=OP.mult)
                    v.tensor_tensor(out=hi, in0=rt1[:, :], in1=rt2[:, :], op=OP.add)
                    op = v.tensor_copy(lo, rt3[:, :])
                    op.then_inc(s_rp.h, 1)
                    s_rp.n += 1
                for qi in range(DC):
                    gq_i = tc * DC + qi
                    if gq_i >= 2:
                        v.wait_ge(m_qst[gq_i % 2].h, 16 * (gq_i // 2))
                    if qi >= 2 * RH:
                        v.wait_ge(s_ev1.h, tc * E3 + qi + 1)
                    op = v.tensor_copy(qbf_st[:, qi % 2, :], qstg[:, qi, :])
                    op.then_inc(s_rp.h, 1)
                    s_rp.n += 1
                v.wait_ge(s_kst.h, DC * (tc + 1))
                op = v.tensor_copy(rrow[:, :], ps8[0:1, 6, :])
                op.then_inc(s_nrw.h, 1)
                s_nrw.n += 1
                v.wait_ge(s_sqr.h, (DC + 1) * (tc + 1))
                v.tensor_scalar(out=rrow[:, :], in0=rrow[:, :],
                                scalar1=1e-12, scalar2=None, op0=OP.max)
                op = v.reciprocal(rrow[:, :], rrow[:, :])
                op.then_inc(s_nrw.h, 1)
                s_nrw.n += 1
                v.wait_ge(s_bc1.h, tc + 1)
                v.tensor_copy(rbb[:, :], ps8[:, 7, :])
                v.wait_ge(s_ev1.h, (tc + 1) * E3)
                for kc in range(DC):
                    v.tensor_tensor(out=rt1[:, :], in0=kfull[:, kc, :],
                                    in1=vstg[:, kc, :], op=OP.mult)
                    v.tensor_tensor(out=rt2[:, :], in0=rt1[:, :], in1=rbb[:, :], op=OP.mult)
                    op = v.reduce_sum(rt3[:, 0:1], rt2[:, :], axis=AX.X)
                    op.then_inc(s_red.h, 1)
                    s_red.n += 1
                    v.wait_ge(s_red.h, s_red.n)
                    op = v.tensor_tensor(out=aacc[:, kc:kc + 1], in0=aacc[:, kc:kc + 1],
                                         in1=rt3[:, 0:1], op=OP.add)
                op.then_inc(s_ap.h, 1)
                s_ap.n += 1
            v.wait_ge(m_aout.h, 16)
            op = v.tensor_copy(rt1[:, 0:1], aA[:, 0:1])
            op.then_inc(s_aok.h, 1)
            s_aok.n += 1

    with ExitStack() as s2:
        k2 = K(nc, block, s2)
        qbf = k2.sb("I_qbf", [128, DC, 512], BF16)
        ybf = k2.sb("I_ybf", [128, DC, 512], BF16)
        woutb = k2.sb("I_wout", [128, 2, DC, 128], BF16)
        sD = k2.sb("I_sD", [128, DC, 512], F32)
        strsl = k2.sb("I_strsl", [128, 2, 512], F32)
        evst = k2.sb("I_evst", [128, 2, 512], F32)
        scr2 = k2.sb("I_scr2", [128, 2, 512], F32)
        rowbuf = k2.sb("I_rows", [1, 8, 512], F32)
        rb2 = k2.sb("I_rb2", [128, 2, 512], F32)

        @block.gpsimd
        def _(g):
            g.wait_ge(s_aok.h, 1)
            for tc in range(NTCH):
                for ot in range(DC):
                    gi = tc * DC + ot
                    if gi >= 2:
                        g.wait_ge(s_op.h, gi - 1)
                    g.dma_start(out=woutb[:, ot % 2, :, :],
                                in_=io["w_outT"].rearrange("(c p) o -> p c o", p=128)[:, :, ot * 128:(ot + 1) * 128]
                                ).then_inc(m_wout[ot % 2].h, 16)

        @block.sync
        def _(s):
            s.wait_ge(s_aok.h, 1)
            for tc in range(NTCH):
                t0 = tc * 512
                if tc >= 1:
                    s.wait_ge(s_yb.h, DC * tc)
                s.dma_start(out=qbf[:, :, :],
                            in_=io["q_scr"].rearrange("(c p) t -> p c t", p=128)[:, :, t0:t0 + 512]
                            ).then_inc(m_qld.h, 16)
                for ot in range(DC):
                    gi = tc * DC + ot
                    if gi >= 2:
                        s.wait_ge(s_rs.h, gi - 1)
                    s.dma_start(out=strsl[:, gi % 2, :],
                                in_=io["stream_in"].rearrange("(c p) t -> p c t", p=128)[:, ot, t0:t0 + 512]
                                ).then_inc(m_str[gi % 2].h, 16)
                s.wait_ge(s_nrm.h, tc + 1)
                s.dma_start(out=io["out"].rearrange("(c p) t -> p c t", p=128)[:, :, t0:t0 + 512],
                            in_=sD[:, :, :]).then_inc(m_out.h, 16)
            s.wait_ge(m_out.h, 16 * NTCH)

        @block.tensor
        def _(t):
            for tc in range(NTCH):
                for ot in range(DC):
                    gi = tc * DC + ot
                    t.wait_ge(m_wout[ot % 2].h, 16 * (gi // 2 + 1))
                    t.wait_ge(s_yb.h, DC * tc + DC)
                    if gi >= 2:
                        t.wait_ge(s_ev2.h, gi - 1)
                    for dc in range(DC):
                        mm = t.matmul(ps8[:, 4 + gi % 2, :], woutb[:, ot % 2, dc, :],
                                      ybf[:, dc, :], start=(dc == 0), stop=(dc == DC - 1))
                    mm.then_inc(s_op.h, 1)
                    s_op.n += 1
                for ot in range(DC):
                    t.wait_ge(s_sq.h, (DC + 1) * tc + ot + 1)
                    if ot == 0 and tc >= 1:
                        t.wait_ge(s_row.h, 2 * tc - 1)
                    mm = t.matmul(ps8[0:1, 0, :], ones_col_f[:, :], sD[:, ot, :],
                                  start=(ot == 0), stop=(ot == DC - 1))
                    mm2 = t.matmul(ps8[0:1, 6, :], ones_col_f[:, :], scr2[:, ot % 2, :],
                                   start=(ot == 0), stop=(ot == DC - 1))
                    mm2.then_inc(s_st2.h, 1)
                    s_st2.n += 1
                t.wait_ge(s_row.h, 2 * tc + 2)
                if tc >= 1:
                    t.wait_ge(s_nrm.h, tc)
                mm = t.matmul(ps8[:, 7, :], ones_row_f[:, :], rowbuf[:, 4, :],
                              start=True, stop=True)
                mm2 = t.matmul(ps8[:, 1, :], ones_row_f[:, :], rowbuf[:, 5, :],
                               start=True, stop=True)
                mm2.then_inc(s_bc.h, 1)
                s_bc.n += 1

        @block.scalar
        def _(sc):
            for tc in range(NTCH):
                for ot in range(DC):
                    gi = tc * DC + ot
                    sc.wait_ge(s_op.h, gi + 1)
                    if gi >= 2:
                        sc.wait_ge(s_rs.h, gi - 1)
                    op = sc.activation(out=evst[:, gi % 2, :], in_=ps8[:, 4 + gi % 2, :],
                                       func=AF.Identity, bias=boutT[:, ot:ot + 1])
                    op.then_inc(s_ev2.h, 1)
                    s_ev2.n += 1
                for ot in range(DC):
                    gi = tc * DC + ot
                    sc.wait_ge(s_rs.h, gi + 1)
                    if ot >= 2:
                        sc.wait_ge(s_st2.h, DC * tc + ot - 1)
                    op = sc.activation(out=scr2[:, ot % 2, :], in_=sD[:, ot, :],
                                       func=AF.Square)
                    op.then_inc(s_sq.h, 1)
                    s_sq.n += 1
                sc.wait_ge(s_row.h, 2 * tc + 1)
                op = sc.activation(out=rowbuf[:, 6, :], in_=rowbuf[:, 1, :], func=AF.Sqrt)
                op.then_inc(s_sq.h, 1)
                s_sq.n += 1

        @block.vector
        def _(v):
            for tc in range(NTCH):
                v.wait_ge(m_qld.h, 16 * (tc + 1))
                for dc in range(DC):
                    if tc >= 1 and dc == 0:
                        v.wait_ge(s_op.h, DC * tc)
                    op = v.tensor_scalar(out=ybf[:, dc, :], in0=qbf[:, dc, :],
                                         scalar1=aA[:, dc:dc + 1], scalar2=None,
                                         op0=OP.mult)
                    op.then_inc(s_yb.h, 1)
                    s_yb.n += 1
                for ot in range(DC):
                    gi = tc * DC + ot
                    v.wait_ge(s_ev2.h, gi + 1)
                    v.wait_ge(m_str[gi % 2].h, 16 * (gi // 2 + 1))
                    if ot == 0:
                        v.wait_ge(m_out.h, 16 * tc)
                    op = v.scalar_tensor_tensor(out=sD[:, ot, :], in0=strsl[:, gi % 2, :],
                                                scalar=2.0, in1=evst[:, gi % 2, :],
                                                op0=OP.mult, op1=OP.add)
                    op.then_inc(s_rs.h, 1)
                    s_rs.n += 1
                v.wait_ge(s_st2.h, DC * (tc + 1))
                v.tensor_scalar(out=rowbuf[:, 2, :], in0=ps8[0:1, 0, :],
                                scalar1=1.0 / D, scalar2=None, op0=OP.mult)
                v.tensor_scalar(out=rowbuf[:, 3, :], in0=ps8[0:1, 6, :],
                                scalar1=1.0 / D, scalar2=None, op0=OP.mult)
                v.tensor_tensor(out=rowbuf[:, 0, :], in0=rowbuf[:, 2, :],
                                in1=rowbuf[:, 2, :], op=OP.mult)
                v.tensor_tensor(out=rowbuf[:, 3, :], in0=rowbuf[:, 3, :],
                                in1=rowbuf[:, 0, :], op=OP.subtract)
                op = v.tensor_scalar(out=rowbuf[:, 1, :], in0=rowbuf[:, 3, :],
                                     scalar1=1e-5, scalar2=None, op0=OP.add)
                op.then_inc(s_row.h, 1)
                s_row.n += 1
                v.wait_ge(s_sq.h, (DC + 1) * (tc + 1))
                v.reciprocal(rowbuf[:, 4, :], rowbuf[:, 6, :])
                op = v.tensor_tensor(out=rowbuf[:, 5, :], in0=rowbuf[:, 2, :],
                                     in1=rowbuf[:, 4, :], op=OP.mult)
                op.then_inc(s_row.h, 1)
                s_row.n += 1
                v.wait_ge(s_bc.h, tc + 1)
                v.tensor_copy(rb2[:, 0, :], ps8[:, 7, :])
                v.tensor_copy(rb2[:, 1, :], ps8[:, 1, :])
                for ot in range(DC):
                    v.tensor_tensor(out=scr2[:, ot % 2, :], in0=sD[:, ot, :],
                                    in1=rb2[:, 0, :], op=OP.mult)
                    op = v.tensor_tensor(out=sD[:, ot, :], in0=scr2[:, ot % 2, :],
                                         in1=rb2[:, 1, :], op=OP.subtract)
                op.then_inc(s_nrm.h, 1)
                s_nrm.n += 1


def qkv_perm(D):
    rot = D // 2
    p = np.arange(D)
    half = rot // 2
    p[:half] = np.arange(0, rot, 2)
    p[half:rot] = np.arange(1, rot, 2)
    return p


def build_l1attn(cfg):
    c = derive(cfg)
    D, TH, DC = c["D"], c["TH"], c["DC"]
    E = 3 * D
    nc = bass.Bass("TRN2", target_bir_lowering=False, num_devices=8,
                   detect_race_conditions=False)
    io = {}
    io["stream_in"] = nc.dram_tensor("stream_in", [D, TH], F32, kind="ExternalInput").ap()
    io["w_inT"] = nc.dram_tensor("w_inT", [D, E], F32, kind="ExternalInput").ap()
    io["binT"] = nc.dram_tensor("binT", [128, 3 * DC], F32, kind="ExternalInput").ap()
    io["w_outT"] = nc.dram_tensor("w_outT", [D, D], F32, kind="ExternalInput").ap()
    io["boutT"] = nc.dram_tensor("boutT", [128, DC], F32, kind="ExternalInput").ap()
    io["cosT1"] = nc.dram_tensor("cosT1", [D // 4, TH], F32, kind="ExternalInput").ap()
    io["sinT1"] = nc.dram_tensor("sinT1", [D // 4, TH], F32, kind="ExternalInput").ap()
    io["identf"] = nc.dram_tensor("identf", [128, 128], F32, kind="ExternalInput").ap()
    io["out"] = nc.dram_tensor("stream_out", [D, TH], F32, kind="ExternalOutput").ap()
    io["q_scr"] = nc.dram_tensor("q_scr", [D, TH], BF16).ap()
    io["cc_a_in"] = nc.dram_tensor("cc_a_in", [128, DC], F32)
    io["cc_a_out"] = nc.dram_tensor("cc_a_out", [128, DC], F32)
    with ExitStack() as stack:
        block = stack.enter_context(nc.Block())
        k = K(nc, block, stack)
        emit_l1attn(k, c, io)
    return nc


def prep_l1attn_inputs(cfg, inp, streams):
    c = derive(cfg)
    D, TH, DC = c["D"], c["TH"], c["DC"]
    rot = D // 2
    W_in = np.asarray(inp["L1_attn_W_in"], dtype=np.float32)
    b_in = np.asarray(inp["L1_attn_b_in"], dtype=np.float32)
    W_out = np.asarray(inp["L1_attn_W_out"], dtype=np.float32)
    b_out = np.asarray(inp["L1_attn_b_out"], dtype=np.float32)
    p = qkv_perm(D)
    permE = np.concatenate([p, D + p, 2 * D + p])
    w_inT = np.ascontiguousarray(W_in.T[:, permE])
    binP = b_in[permE]
    binT = np.ascontiguousarray(binP.reshape(3 * DC, 128).T)
    w_outT = np.ascontiguousarray(W_out.T[p, :])
    boutT = np.ascontiguousarray(b_out.reshape(DC, 128).T)
    identf = np.eye(128, dtype=np.float32)
    half = rot // 2
    inv = (10000.0 ** (-2.0 * np.arange(half) / rot))
    maps = []
    for core in range(8):
        b, g = core // 2, core % 2
        pos = np.arange(g * TH, (g + 1) * TH, dtype=np.float64)
        fr = pos[None, :] * inv[:, None]
        maps.append(dict(stream_in=streams[core], w_inT=w_inT, binT=binT,
                         w_outT=w_outT, boutT=boutT,
                         cosT1=np.cos(fr).astype(np.float32),
                         sinT1=np.sin(fr).astype(np.float32),
                         identf=identf))
    return maps
def emit_l1ffn(k, c, io):
    nc, block = k.nc, k.block
    D, DC = c["D"], c["DC"]
    TH = c["TH"]
    I1 = c["I1"]
    NG = I1 // 128
    NTCH = TH // 512

    ones_row_f = k.sb("ones_row_f", [1, 128], F32)
    ones_col_f = k.sb("ones_col_f", [128, 1], F32)
    ps8 = k.ps("ps8", [128, 8, 512])

    m_xbf = k.sem("m_xbf")
    m_wgu = [k.sem("m_wgu0"), k.sem("m_wgu1")]
    m_wdn = [k.sem("m_wdn0"), k.sem("m_wdn1")]
    m_str = [k.sem("m_str0"), k.sem("m_str1")]
    m_out = k.sem("m_out")

    s_gu = k.sem("s_gu")
    s_si = k.sem("s_si")
    s_hb = k.sem("s_hb")
    s_dn = k.sem("s_dn")
    s_ye = k.sem("s_ye")
    s_rs = k.sem("s_rs")
    s_sq = k.sem("s_sq")
    s_st2 = k.sem("s_st2")
    s_row = k.sem("s_row")
    s_bc = k.sem("s_bc")
    s_nrm = k.sem("s_nrm")

    with ExitStack() as sf2:
        kf2 = K(nc, block, sf2)
        xbf = kf2.sb("J_xbf", [128, DC, 512], BF16)
        wgu = kf2.sb("J_wgu", [128, 2, 2, DC, 128], BF16)
        wdn = kf2.sb("J_wdn", [128, 2, NG, 128], BF16)
        hbuf = kf2.sb("J_hbuf", [128, NG, 512], BF16)
        gsil = kf2.sb("J_gsil", [128, 2, 512], BF16)
        ybuf = kf2.sb("J_ybuf", [128, DC, 512], F32)
        strsl = kf2.sb("J_strsl", [128, 2, 512], F32)
        scr2 = kf2.sb("J_scr2", [128, 2, 512], F32)
        rowbuf = kf2.sb("J_rows", [1, 8, 512], F32)
        rb2 = kf2.sb("J_rb2", [128, 2, 512], F32)
        atmp = kf2.sb("J_atmp", [128, 512], F32)

        @block.gpsimd
        def _(g):
            for tc in range(NTCH):
                t0 = tc * 512
                if tc >= 1:
                    g.wait_ge(s_gu.h, NG * tc)
                g.dma_start(out=xbf[:, :, :],
                            in_=io["stream_in"].rearrange("(c p) t -> p c t", p=128)[:, :, t0:t0 + 512]
                            ).then_inc(m_xbf.h, 16)
                for it in range(NG):
                    gi = tc * NG + it
                    if gi >= 2:
                        g.wait_ge(s_gu.h, gi - 1)
                    g.dma_start(out=wgu[:, it % 2, 0, :, :],
                                in_=io["w_guT"].rearrange("(c p) o -> p c o", p=128)[:, :, it * 128:(it + 1) * 128]
                                ).then_inc(m_wgu[it % 2].h, 16)
                    g.dma_start(out=wgu[:, it % 2, 1, :, :],
                                in_=io["w_guT"].rearrange("(c p) o -> p c o", p=128)[:, :, I1 + it * 128:I1 + (it + 1) * 128]
                                ).then_inc(m_wgu[it % 2].h, 16)
                for ho in range(DC):
                    g2 = tc * DC + ho
                    if g2 >= 2:
                        g.wait_ge(s_dn.h, g2 - 1)
                    g.dma_start(out=wdn[:, ho % 2, :, :],
                                in_=io["w_dnT"].rearrange("(c p) o -> p c o", p=128)[:, :, ho * 128:(ho + 1) * 128]
                                ).then_inc(m_wdn[ho % 2].h, 16)

        @block.sync
        def _(s):
            for tc in range(NTCH):
                t0 = tc * 512
                for ho in range(DC):
                    g2 = tc * DC + ho
                    if g2 >= 2:
                        s.wait_ge(s_ye.h, g2 - 1)
                    s.dma_start(out=strsl[:, g2 % 2, :],
                                in_=io["stream_in"].rearrange("(c p) t -> p c t", p=128)[:, ho, t0:t0 + 512]
                                ).then_inc(m_str[g2 % 2].h, 16)
                s.wait_ge(s_nrm.h, tc + 1)
                s.dma_start(out=io["out"].rearrange("(c p) t -> p c t", p=128)[:, :, t0:t0 + 512],
                            in_=ybuf[:, :, :]).then_inc(m_out.h, 16)
            s.wait_ge(m_out.h, 16 * NTCH)

        @block.tensor
        def _(t):
            for tc in range(NTCH):
                t.wait_ge(m_xbf.h, 16 * (tc + 1))
                for it in range(NG):
                    gi = tc * NG + it
                    t.wait_ge(m_wgu[it % 2].h, 32 * (gi // 2 + 1))
                    if gi >= 2:
                        t.wait_ge(s_hb.h, gi - 1)
                    if it == 0 and tc >= 1:
                        t.wait_ge(s_nrm.h, tc)
                    for kc in range(DC):
                        mm = t.matmul(ps8[:, 2 * (gi % 2), :], wgu[:, it % 2, 0, kc, :],
                                      xbf[:, kc, :], start=(kc == 0), stop=(kc == DC - 1))
                    for kc in range(DC):
                        mm2 = t.matmul(ps8[:, 2 * (gi % 2) + 1, :], wgu[:, it % 2, 1, kc, :],
                                       xbf[:, kc, :], start=(kc == 0), stop=(kc == DC - 1))
                    mm2.then_inc(s_gu.h, 1)
                    s_gu.n += 1
                for ho in range(DC):
                    g2 = tc * DC + ho
                    t.wait_ge(m_wdn[ho % 2].h, 16 * (g2 // 2 + 1))
                    if g2 >= 2:
                        t.wait_ge(s_ye.h, g2 - 1)
                    for ic in range(NG):
                        t.wait_ge(s_hb.h, tc * NG + ic + 1)
                        mm = t.matmul(ps8[:, 4 + g2 % 2, :], wdn[:, ho % 2, ic, :],
                                      hbuf[:, ic, :], start=(ic == 0), stop=(ic == NG - 1))
                    mm.then_inc(s_dn.h, 1)
                    s_dn.n += 1
                for ho in range(DC):
                    t.wait_ge(s_sq.h, (DC + 1) * tc + ho + 1)
                    if ho == 0 and tc >= 1:
                        t.wait_ge(s_row.h, 2 * tc - 1)
                    mm = t.matmul(ps8[0:1, 0, :], ones_col_f[:, :], ybuf[:, ho, :],
                                  start=(ho == 0), stop=(ho == DC - 1))
                    mm2 = t.matmul(ps8[0:1, 6, :], ones_col_f[:, :], scr2[:, ho % 2, :],
                                   start=(ho == 0), stop=(ho == DC - 1))
                    mm2.then_inc(s_st2.h, 1)
                    s_st2.n += 1
                t.wait_ge(s_row.h, 2 * tc + 2)
                if tc >= 1:
                    t.wait_ge(s_nrm.h, tc)
                mm = t.matmul(ps8[:, 7, :], ones_row_f[:, :], rowbuf[:, 4, :],
                              start=True, stop=True)
                mm2 = t.matmul(ps8[:, 1, :], ones_row_f[:, :], rowbuf[:, 5, :],
                               start=True, stop=True)
                mm2.then_inc(s_bc.h, 1)
                s_bc.n += 1

        @block.scalar
        def _(sc):
            for tc in range(NTCH):
                for it in range(NG):
                    gi = tc * NG + it
                    sc.wait_ge(s_gu.h, gi + 1)
                    if gi >= 2:
                        sc.wait_ge(s_hb.h, gi - 1)
                    op = sc.activation(out=gsil[:, gi % 2, :], in_=ps8[:, 2 * (gi % 2), :],
                                       func=AF.Silu)
                    op.then_inc(s_si.h, 1)
                    s_si.n += 1
                for ho in range(DC):
                    g2 = tc * DC + ho
                    sc.wait_ge(s_ye.h, g2 + 1)
                    if ho >= 2:
                        sc.wait_ge(s_st2.h, DC * tc + ho - 1)
                    op = sc.activation(out=scr2[:, ho % 2, :], in_=ybuf[:, ho, :],
                                       func=AF.Square)
                    op.then_inc(s_sq.h, 1)
                    s_sq.n += 1
                sc.wait_ge(s_row.h, 2 * tc + 1)
                op = sc.activation(out=rowbuf[:, 6, :], in_=rowbuf[:, 1, :], func=AF.Sqrt)
                op.then_inc(s_sq.h, 1)
                s_sq.n += 1

        @block.vector
        def _(v):
            v.memset(ones_row_f[:, :], 1.0)
            v.memset(ones_col_f[:, :], 1.0)
            for tc in range(NTCH):
                for it in range(NG):
                    gi = tc * NG + it
                    v.wait_ge(s_si.h, gi + 1)
                    if it == 0 and tc >= 1:
                        v.wait_ge(s_dn.h, DC * tc)
                    op = v.tensor_tensor(out=hbuf[:, it, :], in0=gsil[:, gi % 2, :],
                                         in1=ps8[:, 2 * (gi % 2) + 1, :], op=OP.mult)
                    op.then_inc(s_hb.h, 1)
                    s_hb.n += 1
                for ho in range(DC):
                    g2 = tc * DC + ho
                    v.wait_ge(s_dn.h, g2 + 1)
                    v.wait_ge(m_str[g2 % 2].h, 16 * (g2 // 2 + 1))
                    if ho == 0:
                        v.wait_ge(m_out.h, 16 * tc)
                    op = v.tensor_tensor(out=ybuf[:, ho, :], in0=ps8[:, 4 + g2 % 2, :],
                                         in1=strsl[:, g2 % 2, :], op=OP.add)
                    op.then_inc(s_ye.h, 1)
                    s_ye.n += 1
                v.wait_ge(s_st2.h, DC * (tc + 1))
                v.tensor_scalar(out=rowbuf[:, 2, :], in0=ps8[0:1, 0, :],
                                scalar1=1.0 / D, scalar2=None, op0=OP.mult)
                v.tensor_scalar(out=rowbuf[:, 3, :], in0=ps8[0:1, 6, :],
                                scalar1=1.0 / D, scalar2=None, op0=OP.mult)
                v.tensor_tensor(out=rowbuf[:, 0, :], in0=rowbuf[:, 2, :],
                                in1=rowbuf[:, 2, :], op=OP.mult)
                v.tensor_tensor(out=rowbuf[:, 3, :], in0=rowbuf[:, 3, :],
                                in1=rowbuf[:, 0, :], op=OP.subtract)
                op = v.tensor_scalar(out=rowbuf[:, 1, :], in0=rowbuf[:, 3, :],
                                     scalar1=1e-5, scalar2=None, op0=OP.add)
                op.then_inc(s_row.h, 1)
                s_row.n += 1
                v.wait_ge(s_sq.h, (DC + 1) * (tc + 1))
                v.reciprocal(rowbuf[:, 4, :], rowbuf[:, 6, :])
                op = v.tensor_tensor(out=rowbuf[:, 5, :], in0=rowbuf[:, 2, :],
                                     in1=rowbuf[:, 4, :], op=OP.mult)
                op.then_inc(s_row.h, 1)
                s_row.n += 1
                v.wait_ge(s_bc.h, tc + 1)
                v.tensor_copy(rb2[:, 0, :], ps8[:, 7, :])
                v.tensor_copy(rb2[:, 1, :], ps8[:, 1, :])
                for ho in range(DC):
                    v.tensor_tensor(out=atmp[:, :], in0=ybuf[:, ho, :],
                                    in1=rb2[:, 0, :], op=OP.mult)
                    op = v.tensor_tensor(out=ybuf[:, ho, :], in0=atmp[:, :],
                                         in1=rb2[:, 1, :], op=OP.subtract)
                op.then_inc(s_nrm.h, 1)
                s_nrm.n += 1


def build_l1ffn(cfg):
    c = derive(cfg)
    D, TH, I1 = c["D"], c["TH"], c["I1"]
    nc = bass.Bass("TRN2", target_bir_lowering=False, num_devices=8,
                   detect_race_conditions=False)
    io = {}
    io["stream_in"] = nc.dram_tensor("stream_in", [D, TH], F32, kind="ExternalInput").ap()
    io["w_guT"] = nc.dram_tensor("w_guT", [D, 2 * I1], F32, kind="ExternalInput").ap()
    io["w_dnT"] = nc.dram_tensor("w_dnT", [I1, D], F32, kind="ExternalInput").ap()
    io["out"] = nc.dram_tensor("stream_out", [D, TH], F32, kind="ExternalOutput").ap()
    with ExitStack() as stack:
        block = stack.enter_context(nc.Block())
        k = K(nc, block, stack)
        emit_l1ffn(k, c, io)
    return nc


def prep_l1ffn_inputs(cfg, inp, streams):
    w_guT = np.ascontiguousarray(np.asarray(inp["L1_ffn_w_gate_up"], dtype=np.float32).T)
    w_dnT = np.ascontiguousarray(np.asarray(inp["L1_ffn_w_down"], dtype=np.float32).T)
    maps = []
    for core in range(8):
        maps.append(dict(stream_in=streams[core], w_guT=w_guT, w_dnT=w_dnT))
    return maps


_PFX = [""]


class GatedBlock:
    def __init__(self, block, gate):
        self._b = block
        self._gate = gate

    def _wrap(self, dec):
        gate = self._gate

        def deco(fn):
            def wrapped(eng):
                if gate is not None:
                    eng.wait_ge(gate[0], gate[1])
                fn(eng)
            return dec(wrapped)
        return deco

    @property
    def gpsimd(self):
        return self._wrap(self._b.gpsimd)

    @property
    def sync(self):
        return self._wrap(self._b.sync)

    @property
    def tensor(self):
        return self._wrap(self._b.tensor)

    @property
    def vector(self):
        return self._wrap(self._b.vector)

    @property
    def scalar(self):
        return self._wrap(self._b.scalar)


def _mangle(name):
    return _PFX[0] + name


_orig_sb = K.sb
_orig_ps = K.ps
_orig_sem = K.sem


def _sb(self, name, shape, dtype):
    return _orig_sb(self, _mangle(name), shape, dtype)


def _ps(self, name, shape, dtype=F32):
    return _orig_ps(self, _mangle(name), shape, dtype)


def _sem(self, name):
    return _orig_sem(self, _mangle(name))


K.sb = _sb
K.ps = _ps
K.sem = _sem


def build_fused(cfg):
    c = derive(cfg)
    L, D, DC, NHC, NJ, TH = c["L"], c["D"], c["DC"], c["NHC"], c["NJ"], c["TH"]
    I0, I1 = c["I0"], c["I1"]
    HD = NHC * 128
    NG0 = I0 // 128
    NHW = max(1, D // 512)
    HB = min(D, 512)
    E = 3 * D
    nc = bass.Bass("TRN2", target_bir_lowering=False, num_devices=8,
                   detect_race_conditions=False)
    T = lambda n, s, d, **kw: nc.dram_tensor(n, s, d, **kw)
    io1 = {}
    io1["xT"] = T("xT", [D, L], F32, kind="ExternalInput").ap()
    io1["xTh"] = T("xTh", [D, TH], F32, kind="ExternalInput").ap()
    io1["wq"] = T("wq", [D, NJ * 128], F32, kind="ExternalInput").ap()
    io1["wk"] = T("wk", [D, HD], F32, kind="ExternalInput").ap()
    io1["wv"] = T("wv", [D, HD], F32, kind="ExternalInput").ap()
    io1["wl"] = T("wl", [D, NHC], F32, kind="ExternalInput").ap()
    io1["bl"] = T("bl", [NHC], F32, kind="ExternalInput").ap()
    io1["wo"] = T("wo", [HD, D], F32, kind="ExternalInput").ap()
    io1["cosk"] = T("cosk", [128, L], F32, kind="ExternalInput").ap()
    io1["sink"] = T("sink", [128, L], F32, kind="ExternalInput").ap()
    io1["masks"] = T("masks", [4, 128, 512], BF16, kind="ExternalInput").ap()
    io1["q_scr"] = T("f1_q_scr", [NJ * 128, L], BF16).ap()
    io1["lam_scr"] = T("f1_lam_scr", [NHC, L], F32).ap()
    io1["cc_in"] = T("f1_cc_in", [2, D, TH], F32)
    io1["cc_out"] = T("f1_cc_out", [D, TH], F32)
    s1 = T("f_s1", [D, TH], F32).ap()
    s2 = T("f_s2", [D, TH], F32).ap()
    s3 = T("f_s3", [D, TH], F32).ap()
    io1["out"] = s1
    io2 = {}
    io2["stream_in"] = s1
    io2["gup"] = T("gup", [2 * I0, D // 8], I32, kind="ExternalInput").ap()
    io2["gusT"] = T("gusT", [2 * I0, D // 128], F16, kind="ExternalInput").ap()
    io2["dnp"] = T("dnp", [D, I0 // 8], I32, kind="ExternalInput").ap()
    io2["dnsT"] = T("dnsT", [D, I0 // 128], F16, kind="ExternalInput").ap()
    io2["ident"] = T("ident", [128, 128], BF16, kind="ExternalInput").ap()
    io2["wguT_scr"] = T("f2_wguT", [2 * NG0, DC, 128, 128], BF16).ap()
    io2["wdnT_scr"] = T("f2_wdnT", [NG0, NHW, 128, HB], BF16).ap()
    io2["out"] = s2
    io3 = {}
    io3["stream_in"] = s2
    io3["w_inT"] = T("w_inT", [D, E], F32, kind="ExternalInput").ap()
    io3["binT"] = T("binT", [128, 3 * DC], F32, kind="ExternalInput").ap()
    io3["w_outT"] = T("w_outT", [D, D], F32, kind="ExternalInput").ap()
    io3["boutT"] = T("boutT", [128, DC], F32, kind="ExternalInput").ap()
    io3["cosT1"] = T("cosT1", [D // 4, TH], F32, kind="ExternalInput").ap()
    io3["sinT1"] = T("sinT1", [D // 4, TH], F32, kind="ExternalInput").ap()
    io3["identf"] = T("identf", [128, 128], F32, kind="ExternalInput").ap()
    io3["q_scr"] = T("f3_q_scr", [D, TH], BF16).ap()
    io3["cc_a_in"] = T("f3_cc_a_in", [128, DC], F32)
    io3["cc_a_out"] = T("f3_cc_a_out", [128, DC], F32)
    io3["out"] = s3
    io4 = {}
    io4["stream_in"] = s3
    io4["w_guT"] = T("w_guT", [D, 2 * I1], F32, kind="ExternalInput").ap()
    io4["w_dnT"] = T("w_dnT", [I1, D], F32, kind="ExternalInput").ap()
    io4["out"] = T("stream_out", [D, TH], F32, kind="ExternalOutput").ap()

    with ExitStack() as stack:
        block = stack.enter_context(nc.Block())
        bars = [stack.enter_context(nc.semaphore("fbar%d" % i)) for i in range(3)]
        emits = [(emit_l0attn, io1, "f1_"), (emit_l0ffn, io2, "f2_"),
                 (emit_l1attn, io3, "f3_"), (emit_l1ffn, io4, "f4_")]
        gate = None
        for idx, (emit, io, pfx) in enumerate(emits):
            _PFX[0] = pfx
            gb = GatedBlock(block, gate)
            _SEM_LOG.clear()
            with ExitStack() as bstack:
                k = K(nc, gb, bstack)
                emit(k, c, io)
            blocksems = sorted(set(_SEM_LOG))
            if idx < 3:
                bar = bars[idx]

                @block.sync
                def _(s, _bar=bar, _g=gate):
                    if _g is not None:
                        s.wait_ge(_g[0], _g[1])
                    s.nop().then_inc(_bar, 1)

                for engname in ("tensor", "vector", "scalar"):
                    dec = getattr(block, engname)

                    @dec
                    def _(e, _bar=bar, _g=gate):
                        if _g is not None:
                            e.wait_ge(_g[0], _g[1])
                        e.wait_ge(_bar, 1)
                        e.nop().then_inc(_bar, 1)

                @block.gpsimd
                def _(g, _bar=bar, _g=gate, _sems=blocksems):
                    if _g is not None:
                        g.wait_ge(_g[0], _g[1])
                    g.wait_ge(_bar, 4)
                    g.dma_reset(None)
                    for snum in _sems:
                        g.sem_clear(range(snum, snum + 1))
                    g.nop().then_inc(_bar, 1)
                gate = (bar, 5)
        _PFX[0] = ""
    return nc


def prep_fused_inputs(cfg, inputs):
    m1 = prep_l0attn_inputs(cfg, inputs)
    m2 = prep_l0ffn_inputs(cfg, inputs, [None] * 8)
    m3 = prep_l1attn_inputs(cfg, inputs, [None] * 8)
    m4 = prep_l1ffn_inputs(cfg, inputs, [None] * 8)
    maps = []
    for core in range(8):
        m = {}
        for mm_ in (m1[core], m2[core], m3[core], m4[core]):
            for kk, vv in mm_.items():
                if kk != "stream_in":
                    m[kk] = vv
        maps.append(m)
    return maps


FUSED = True


def kernel_fused(**inputs):
    cfg = FULL
    c = derive(cfg)
    TH, D, L = c["TH"], c["D"], c["L"]
    nc = _get_prog("fused", build_fused, cfg)
    maps = prep_fused_inputs(cfg, inputs)
    res = run_bass_kernel_spmd(nc, maps, core_ids=list(range(8)))
    out = np.zeros((4, L, D), dtype=np.float32)
    for core in range(8):
        b, g = core // 2, core % 2
        out[b, g * TH:(g + 1) * TH, :] = res.results[core]["stream_out"].T
    return out


_kernel_unfused = kernel


def kernel(**inputs):
    if FUSED:
        return kernel_fused(**inputs)
    return _kernel_unfused(**inputs)


# version 04 (final)
# speedup vs baseline: 1.1128x; 1.1128x over previous
import math
from contextlib import ExitStack

import numpy as np
import ml_dtypes

import concourse.bass as bass
import concourse.mybir as mybir
from concourse.bass_utils import run_bass_kernel_spmd
from concourse.alu_op_type import AluOpType as OP

F32 = mybir.dt.float32
BF16 = mybir.dt.bfloat16
F16 = mybir.dt.float16
I32 = mybir.dt.int32
U32 = mybir.dt.uint32
AF = mybir.ActivationFunctionType
AX = mybir.AxisListType
BF = ml_dtypes.bfloat16

FULL = dict(L=4096, D=2048, NHC=8, I0=5632, I1=8192, NSH=8,
            CCG=[[0, 1, 2, 3, 4, 5, 6, 7]])
MINI = dict(L=2048, D=512, NHC=2, I0=768, I1=1024, NSH=4,
            CCG=[[0, 1, 2, 3], [4, 5, 6, 7]])


def derive(cfg):
    c = dict(cfg)
    c["DC"] = c["D"] // 128
    c["NTC"] = c["L"] // 512
    c["NQC"] = c["L"] // 512
    c["NTT"] = c["L"] // 128
    c["TH"] = c["L"] // 2
    c["NQL"] = c["TH"] // 512
    c["NJ"] = 2 * c["NHC"]
    c["E"] = 3 * c["D"]
    return c


_SEM_LOG = []


class Sem:
    def __init__(self, k, name):
        self.h = k.stack.enter_context(k.nc.semaphore(name))
        self.n = 0
        _SEM_LOG.append(self.h.num)


class K:
    def __init__(self, nc, block, stack):
        self.nc = nc
        self.block = block
        self.stack = stack

    def sem(self, name):
        return Sem(self, name)

    def sb(self, name, shape, dtype):
        return self.stack.enter_context(self.nc.sbuf_tensor(name, shape, dtype))

    def ps(self, name, shape, dtype=F32):
        return self.stack.enter_context(self.nc.psum_tensor(name, shape, dtype))


def emit_l0attn(k, c, io):
    nc, block = k.nc, k.block
    L, D, DC, NHC, NJ = c["L"], c["D"], c["DC"], c["NHC"], c["NJ"]
    NTC, NQC, NTT, NQL = c["NTC"], c["NQC"], c["NTT"], c["NQL"]
    TH = c["TH"]
    HD = NHC * 128
    NVC = max(1, HD // 256)
    VW = min(HD, 256)
    NVT = (HD // VW) * 4
    ESC = 1.0 / math.sqrt(128.0)
    assert DC == 2 * NHC

    nkt = [4 * qc + 4 for qc in range(NQC)]
    KTT = sum(nkt)
    cumk = [0]
    for n in nkt:
        cumk.append(cumk[-1] + n)

    def tile0(h, qc, p):
        return 2 * KTT * h + 2 * cumk[qc] + p * nkt[qc]

    ones_col = k.sb("ones_col", [128, 1], BF16)
    ones_row_f = k.sb("ones_row_f", [1, 128], F32)
    ones_col_f = k.sb("ones_col_f", [128, 1], F32)
    bl_sb = k.sb("bl_sb", [8, 1], F32)
    tmp1 = k.sb("tmp1", [128, 512], F32)
    tmp2 = k.sb("tmp2", [128, 512], F32)
    ps8 = k.ps("ps8", [128, 8, 512])

    m_cstg = k.sem("m_cstg")
    m_csts = k.sem("m_csts")
    m_x = k.sem("m_x")
    m_wk = [k.sem("m_wk0"), k.sem("m_wk1")]
    m_wq = [k.sem("m_wq0"), k.sem("m_wq1")]
    m_wv = k.sem("m_wv")
    m_cs = k.sem("m_cs")
    m_qst = [k.sem("m_qst0"), k.sem("m_qst1")]
    m_lam = k.sem("m_lam")
    m_qb = k.sem("m_qb")
    m_lr = k.sem("m_lr")
    m_wo = k.sem("m_wo")
    m_ao = k.sem("m_ao")
    m_dld = k.sem("m_dld")
    m_out = k.sem("m_out")

    s_peK = k.sem("s_peK")
    s_peV = k.sem("s_peV")
    s_peQ = k.sem("s_peQ")
    s_peL = k.sem("s_peL")
    s_dvK = k.sem("s_dvK")
    s_dvV = k.sem("s_dvV")
    s_dvQ = k.sem("s_dvQ")
    s_acL = k.sem("s_acL")
    s_st = k.sem("s_st")
    s_exp = k.sem("s_exp")
    s_msk = k.sem("s_msk")
    s_sums = k.sem("s_sums")
    s_ctx = k.sem("s_ctx")
    s_dvS = k.sem("s_dvS")
    s_peB = k.sem("s_peB")
    s_cmb = k.sem("s_cmb")
    s_cc = k.sem("s_cc")
    s_peD = k.sem("s_peD")
    s_exC = k.sem("s_exC")
    s_ex = k.sem("s_ex")
    s_sq2 = k.sem("s_sq2")
    s_stat = k.sem("s_stat")
    s_row = k.sem("s_row")
    s_peB2 = k.sem("s_peB2")
    s_nrm = k.sem("s_nrm")

    AT = 2 * NHC * KTT
    vslot = [NHC if h == 0 else h - 1 for h in range(NHC)]

    with ExitStack() as sv:
        kv = K(nc, block, sv)
        Vt = kv.sb("Vt", [128, NTT, NHC + 1, 128], BF16)

        with ExitStack() as sk:
            kf_ = K(nc, block, sk)
            Kf = kf_.sb("Kf", [128, NHC, L], BF16)

            with ExitStack() as sa:
                ka = K(nc, block, sa)
                xbf = ka.sb("A_xbf", [128, DC, 512], BF16)
                wkb = ka.sb("A_wkb", [128, 2, DC, 128], BF16)
                wqb = ka.sb("A_wqb", [128, 2, DC, 128], BF16)
                wvb = ka.sb("A_wvb", [128, DC, VW], BF16)
                wlb = ka.sb("A_wlb", [128, DC, NHC], BF16)
                cosb = ka.sb("A_cosb", [128, 512], F32)
                sinb = ka.sb("A_sinb", [128, 512], F32)
                stg = ka.sb("A_stg", [128, 2, 512], BF16)
                lam_st = ka.sb("A_lam", [8, 512], F32)

                @block.gpsimd
                def _(g):
                    g.dma_start(out=wlb[:, :, :],
                                in_=io["wl"].rearrange("(c p) h -> p c h", p=128)
                                ).then_inc(m_cstg.h, 16)
                    for tc in range(NTC):
                        t0 = tc * 512
                        if tc >= 1:
                            g.wait_ge(s_peK.h, NHC * tc)
                            g.wait_ge(s_peV.h, NVT * tc)
                            g.wait_ge(s_peQ.h, NJ * tc)
                            g.wait_ge(s_peL.h, tc)
                        g.dma_start(out=xbf[:, :, :],
                                    in_=io["xT"].rearrange("(c p) t -> p c t", p=128)[:, :, t0:t0 + 512]
                                    ).then_inc(m_x.h, 16)
                        for h in range(NHC):
                            gidx = tc * NHC + h
                            if gidx >= 2:
                                g.wait_ge(s_peK.h, gidx - 1)
                            g.dma_start(out=wkb[:, h % 2, :, :],
                                        in_=io["wk"].rearrange("(c p) o -> p c o", p=128)[:, :, h * 128:(h + 1) * 128]
                                        ).then_inc(m_wk[h % 2].h, 16)
                        for v in range(NVC):
                            gidx = tc * NVC + v
                            if gidx >= 1:
                                g.wait_ge(s_peV.h, 4 * gidx)
                            g.dma_start(out=wvb[:, :, :],
                                        in_=io["wv"].rearrange("(c p) o -> p c o", p=128)[:, :, v * VW:(v + 1) * VW]
                                        ).then_inc(m_wv.h, 16)
                        for j in range(NJ):
                            gidx = tc * NJ + j
                            if gidx >= 2:
                                g.wait_ge(s_peQ.h, gidx - 1)
                            g.dma_start(out=wqb[:, j % 2, :, :],
                                        in_=io["wq"].rearrange("(c p) o -> p c o", p=128)[:, :, j * 128:(j + 1) * 128]
                                        ).then_inc(m_wq[j % 2].h, 16)

                @block.sync
                def _(s):
                    s.dma_start(out=bl_sb[0:NHC, :], in_=io["bl"][:, None]).then_inc(m_csts.h, 16)
                    for tc in range(NTC):
                        t0 = tc * 512
                        if tc >= 1:
                            s.wait_ge(s_dvK.h, NHC * tc)
                            s.wait_ge(s_dvQ.h, NJ * tc)
                        s.dma_start(out=cosb[:, :], in_=io["cosk"][:, t0:t0 + 512]).then_inc(m_cs.h, 16)
                        s.dma_start(out=sinb[:, :], in_=io["sink"][:, t0:t0 + 512]).then_inc(m_cs.h, 16)
                        for j in range(NJ):
                            s.wait_ge(s_dvQ.h, tc * NJ + j + 1)
                            s.dma_start(out=io["q_scr"][j * 128:(j + 1) * 128, t0:t0 + 512],
                                        in_=stg[:, j % 2, :]).then_inc(m_qst[j % 2].h, 16)
                        s.wait_ge(s_acL.h, tc + 1)
                        s.dma_start(out=io["lam_scr"][0:NHC, t0:t0 + 512],
                                    in_=lam_st[0:NHC, :]).then_inc(m_lam.h, 16)

                @block.tensor
                def _(t):
                    for tc in range(NTC):
                        t.wait_ge(m_x.h, 16 * (tc + 1))
                        for h in range(NHC):
                            gidx = tc * NHC + h
                            t.wait_ge(m_wk[h % 2].h, 16 * (gidx // 2 + 1))
                            if gidx >= 2:
                                t.wait_ge(s_dvK.h, gidx - 1)
                            for kc in range(DC):
                                mm = t.matmul(ps8[:, gidx % 2, :], wkb[:, h % 2, kc, :],
                                              xbf[:, kc, :], start=(kc == 0), stop=(kc == DC - 1))
                            mm.then_inc(s_peK.h, 1)
                            s_peK.n += 1
                        for v in range(NVC):
                            t.wait_ge(m_wv.h, 16 * (tc * NVC + v + 1))
                            for tt in range(4):
                                gidx = tc * NVT + v * 4 + tt
                                if gidx >= 2:
                                    t.wait_ge(s_dvV.h, gidx - 1)
                                for kc in range(DC):
                                    mm = t.matmul(ps8[:, 2 + gidx % 2, 0:VW],
                                                  xbf[:, kc, tt * 128:(tt + 1) * 128],
                                                  wvb[:, kc, :], start=(kc == 0), stop=(kc == DC - 1))
                                mm.then_inc(s_peV.h, 1)
                                s_peV.n += 1
                        for j in range(NJ):
                            gidx = tc * NJ + j
                            t.wait_ge(m_wq[j % 2].h, 16 * (gidx // 2 + 1))
                            if gidx >= 2:
                                t.wait_ge(s_dvQ.h, gidx - 1)
                            for kc in range(DC):
                                mm = t.matmul(ps8[:, 4 + gidx % 2, :], wqb[:, j % 2, kc, :],
                                              xbf[:, kc, :], start=(kc == 0), stop=(kc == DC - 1))
                            mm.then_inc(s_peQ.h, 1)
                            s_peQ.n += 1
                        if tc == 0:
                            t.wait_ge(m_cstg.h, 16)
                        if tc >= 1:
                            t.wait_ge(s_acL.h, tc)
                        for kc in range(DC):
                            mm = t.matmul(ps8[0:NHC, 6, :], wlb[:, kc, 0:NHC], xbf[:, kc, :],
                                          start=(kc == 0), stop=(kc == DC - 1))
                        mm.then_inc(s_peL.h, 1)
                        s_peL.n += 1

                @block.scalar
                def _(sc):
                    sc.wait_ge(m_csts.h, 16)
                    for tc in range(NTC):
                        sc.wait_ge(s_peL.h, tc + 1)
                        if tc >= 1:
                            sc.wait_ge(m_lam.h, 16 * tc)
                        op = sc.activation(out=lam_st[0:NHC, :], in_=ps8[0:NHC, 6, :],
                                           func=AF.Sigmoid, bias=bl_sb[0:NHC, :])
                        op.then_inc(s_acL.h, 1)
                        s_acL.n += 1

                @block.vector
                def _(v):
                    v.memset(ones_col[:, :], 1.0)
                    v.memset(ones_row_f[:, :], 1.0)
                    v.memset(ones_col_f[:, :], 1.0)
                    for tc in range(NTC):
                        v.wait_ge(m_cs.h, 32 * (tc + 1))
                        for h in range(NHC):
                            gidx = tc * NHC + h
                            v.wait_ge(s_peK.h, gidx + 1)
                            pk = ps8[:, gidx % 2, :]
                            v.tensor_tensor(out=tmp1[:, :], in0=pk, in1=cosb[:, :], op=OP.mult)
                            v.scalar_tensor_tensor(out=tmp2[0:64, :], in0=pk[64:128, :],
                                                   scalar=-1.0, in1=sinb[0:64, :],
                                                   op0=OP.mult, op1=OP.mult)
                            v.tensor_tensor(out=tmp2[64:128, :], in0=pk[0:64, :],
                                            in1=sinb[64:128, :], op=OP.mult)
                            op = v.tensor_tensor(out=Kf[:, h, tc * 512:tc * 512 + 512],
                                                 in0=tmp1[:, :], in1=tmp2[:, :], op=OP.add)
                            op.then_inc(s_dvK.h, 1)
                            s_dvK.n += 1
                        for vi in range(NVC):
                            for tt in range(4):
                                gidx = tc * NVT + vi * 4 + tt
                                v.wait_ge(s_peV.h, gidx + 1)
                                dst = Vt[:, tc * 4 + tt, vi * (VW // 128):(vi + 1) * (VW // 128), :]
                                op = v.tensor_copy(dst, ps8[:, 2 + gidx % 2, 0:VW])
                                op.then_inc(s_dvV.h, 1)
                                s_dvV.n += 1
                        for j in range(NJ):
                            gidx = tc * NJ + j
                            v.wait_ge(s_peQ.h, gidx + 1)
                            if gidx >= 2:
                                v.wait_ge(m_qst[gidx % 2].h, 16 * (gidx // 2))
                            pq_ = ps8[:, 4 + gidx % 2, :]
                            v.tensor_tensor(out=tmp1[:, :], in0=pq_, in1=cosb[:, :], op=OP.mult)
                            v.scalar_tensor_tensor(out=tmp2[0:64, :], in0=pq_[64:128, :],
                                                   scalar=-1.0, in1=sinb[0:64, :],
                                                   op0=OP.mult, op1=OP.mult)
                            v.tensor_tensor(out=tmp2[64:128, :], in0=pq_[0:64, :],
                                            in1=sinb[64:128, :], op=OP.mult)
                            op = v.tensor_tensor(out=stg[:, j % 2, :], in0=tmp1[:, :],
                                                 in1=tmp2[:, :], op=OP.add)
                            op.then_inc(s_dvQ.h, 1)
                            s_dvQ.n += 1

            with ExitStack() as sb_:
                kb = K(nc, block, sb_)
                qb = kb.sb("B_qb", [128, 2, L], BF16)
                ptb = kb.sb("B_pt", [128, 2, 512], BF16)
                mask_sb = kb.sb("B_mask", [128, 4, 512], BF16)
                sums_sb = kb.sb("B_sums", [1, 2, 512], F32)
                rec_sb = kb.sb("B_rec", [1, 2, 512], F32)
                lamr = kb.sb("B_lamr", [1, 512], F32)
                lr1 = kb.sb("B_lr1", [1, 512], F32)
                rb_sb = kb.sb("B_rb", [128, 2, 512], F32)
                ct0 = kb.sb("B_ct0", [128, 512], F32)

                @block.sync
                def _(s):
                    s.wait_ge(s_peL.h, NTC)
                    s.wait_ge(s_dvQ.h, NJ * NTC)
                    s.wait_ge(s_acL.h, NTC)
                    s.dma_start(out=mask_sb[:, :, :],
                                in_=io["masks"].rearrange("m p q -> p m q")).then_inc(m_csts.h, 16)
                    for h in range(NHC):
                        if h >= 1:
                            s.wait_ge(s_st.h, 2 * KTT * h)
                        s.dma_start(out=qb[:, 0, :],
                                    in_=io["q_scr"][(2 * h) * 128:(2 * h + 1) * 128, :]).then_inc(m_qb.h, 16)
                        s.dma_start(out=qb[:, 1, :],
                                    in_=io["q_scr"][(2 * h + 1) * 128:(2 * h + 2) * 128, :]).then_inc(m_qb.h, 16)
                        for qc in range(NQC):
                            hq = h * NQC + qc
                            if hq >= 1:
                                s.wait_ge(s_dvS.h, 2 * hq)
                            s.dma_start(out=lamr[:, :],
                                        in_=io["lam_scr"][h:h + 1, qc * 512:(qc + 1) * 512]
                                        ).then_inc(m_lr.h, 16)

                @block.tensor
                def _(t):
                    t.wait_ge(s_dvK.h, NHC * NTC)
                    t.wait_ge(s_dvV.h, NVT * NTC)
                    nst = 0
                    ncs = 0
                    nmsk = 0
                    for h in range(NHC):
                        t.wait_ge(m_qb.h, 32 * (h + 1))
                        for qc in range(NQC):
                            hq = h * NQC + qc
                            for p in range(2):
                                pq = 2 * hq + p
                                n = nkt[qc]
                                for kt in range(n + 1):
                                    if kt < n:
                                        i = nst
                                        if i >= 2:
                                            t.wait_ge(s_exp.h, i - 1)
                                        mm = t.matmul(ps8[:, i % 2, :],
                                                      Kf[:, h, kt * 128:(kt + 1) * 128],
                                                      qb[:, p, qc * 512:(qc + 1) * 512],
                                                      start=True, stop=True)
                                        mm.then_inc(s_st.h, 1)
                                        nst += 1
                                        s_st.n += 1
                                    if kt >= 1:
                                        ck = kt - 1
                                        i = ncs
                                        diag = ck >= 4 * qc
                                        if diag:
                                            t.wait_ge(s_msk.h, nmsk + (ck - 4 * qc) + 1)
                                        else:
                                            t.wait_ge(s_exp.h, i + 1)
                                        if ck == 0:
                                            if pq >= 2:
                                                t.wait_ge(s_dvS.h, pq - 1)
                                            if hq >= 1:
                                                t.wait_ge(s_cmb.h, hq)
                                        mm = t.matmul(ps8[0:1, 2 + pq % 2, :], ones_col[:, :],
                                                      ptb[:, i % 2, :],
                                                      start=(ck == 0), stop=(ck == n - 1))
                                        mm.then_inc(s_sums.h, 1)
                                        s_sums.n += 1
                                        mm = t.matmul(ps8[:, 4 + p, :], Vt[:, ck, h, :],
                                                      ptb[:, i % 2, :],
                                                      start=(ck == 0), stop=(ck == n - 1))
                                        mm.then_inc(s_ctx.h, 1)
                                        ncs += 1
                                        s_ctx.n += 1
                                nmsk += 4
                            t.wait_ge(s_dvS.h, 2 * hq + 2)
                            if hq >= 1:
                                t.wait_ge(s_cmb.h, hq)
                            mm = t.matmul(ps8[:, 6, :], ones_row_f[:, :], rec_sb[:, 0, :],
                                          start=True, stop=True)
                            mm = t.matmul(ps8[:, 7, :], ones_row_f[:, :], lr1[:, :],
                                          start=True, stop=True)
                            mm.then_inc(s_peB.h, 1)
                            s_peB.n += 1

                @block.scalar
                def _(sc):
                    for i in range(AT):
                        sc.wait_ge(s_st.h, i + 1)
                        if i >= 2:
                            sc.wait_ge(s_ctx.h, i - 1)
                        op = sc.activation(out=ptb[:, i % 2, :], in_=ps8[:, i % 2, :],
                                           func=AF.Exp, scale=ESC)
                        op.then_inc(s_exp.h, 1)
                        s_exp.n += 1

                @block.vector
                def _(v):
                    v.wait_ge(m_csts.h, 32)
                    for h in range(NHC):
                        for qc in range(NQC):
                            hq = h * NQC + qc
                            for p in range(2):
                                pq = 2 * hq + p
                                n = nkt[qc]
                                base = tile0(h, qc, p)
                                for m in range(4):
                                    i = base + 4 * qc + m
                                    v.wait_ge(s_exp.h, i + 1)
                                    op = v.tensor_tensor(out=ptb[:, i % 2, :], in0=ptb[:, i % 2, :],
                                                         in1=mask_sb[:, m, :], op=OP.mult)
                                    op.then_inc(s_msk.h, 1)
                                    s_msk.n += 1
                                v.wait_ge(s_sums.h, base + n)
                                v.tensor_copy(sums_sb[:, pq % 2, :], ps8[0:1, 2 + pq % 2, :])
                                if p == 0:
                                    op = v.reciprocal(rec_sb[:, pq % 2, :], sums_sb[:, pq % 2, :])
                                    op.then_inc(s_dvS.h, 1)
                                    s_dvS.n += 1
                                else:
                                    v.reciprocal(rec_sb[:, pq % 2, :], sums_sb[:, pq % 2, :])
                                    v.wait_ge(m_lr.h, 16 * (hq + 1))
                                    op = v.tensor_tensor(out=lr1[:, :], in0=lamr[:, :],
                                                         in1=rec_sb[:, pq % 2, :], op=OP.mult)
                                    op.then_inc(s_dvS.h, 1)
                                    s_dvS.n += 1
                            v.wait_ge(s_peB.h, hq + 1)
                            v.tensor_copy(rb_sb[:, 0, :], ps8[:, 6, :])
                            v.tensor_copy(rb_sb[:, 1, :], ps8[:, 7, :])
                            v.wait_ge(s_ctx.h, tile0(h, qc, 1) + nkt[qc])
                            v.tensor_tensor(out=ct0[:, :], in0=ps8[:, 4, :], in1=rb_sb[:, 0, :],
                                            op=OP.mult)
                            v.tensor_tensor(out=tmp1[:, :], in0=ps8[:, 5, :], in1=rb_sb[:, 1, :],
                                            op=OP.mult)
                            dst = Vt[:, qc * 4:(qc + 1) * 4, vslot[h], :]
                            op = v.tensor_tensor(out=dst, in0=ct0[:, :], in1=tmp1[:, :],
                                                 op=OP.subtract)
                            op.then_inc(s_cmb.h, 1)
                            s_cmb.n += 1

        with ExitStack() as sc2:
            kc2 = K(nc, block, sc2)
            wob = kc2.sb("C_wob", [128, NHC, D], BF16)
            sA = kc2.sb("C_sA", [128, DC, 512], F32)

            @block.gpsimd
            def _(g):
                g.wait_ge(s_cmb.h, NHC * NQC)
                g.dma_start(out=wob[:, :, :],
                            in_=io["wo"].rearrange("(c p) o -> p c o", p=128)
                            ).then_inc(m_wo.h, 16)
                g.wait_ge(m_ao.h, 16 * NQC)
                g.collective_compute("ReduceScatter", OP.add,
                                     replica_groups=[[0, 1], [2, 3], [4, 5], [6, 7]],
                                     ins=[io["cc_in"].ap().opt()],
                                     outs=[io["cc_out"].ap().opt()]).then_inc(s_cc.h, 1)
                s_cc.n += 1

            @block.tensor
            def _(t):
                t.wait_ge(m_wo.h, 16)
                for qc8 in range(NQC):
                    t.wait_ge(s_cmb.h, (NHC - 1) * NQC + qc8 + 1)
                    for oc in range(DC):
                        gi = qc8 * DC + oc
                        if gi >= 2:
                            t.wait_ge(s_exC.h, gi - 1)
                        for dc in range(NHC):
                            mm = t.matmul(ps8[:, gi % 2, :], wob[:, dc, oc * 128:(oc + 1) * 128],
                                          Vt[:, qc8 * 4:(qc8 + 1) * 4, vslot[dc], :],
                                          start=(dc == 0), stop=(dc == NHC - 1))
                        mm.then_inc(s_peD.h, 1)
                        s_peD.n += 1

            @block.vector
            def _(v):
                for qc8 in range(NQC):
                    for oc in range(DC):
                        gi = qc8 * DC + oc
                        v.wait_ge(s_peD.h, gi + 1)
                        if oc == 0 and qc8 >= 1:
                            v.wait_ge(m_ao.h, 16 * qc8)
                        op = v.tensor_copy(sA[:, oc, :], ps8[:, gi % 2, :])
                        op.then_inc(s_exC.h, 1)
                        s_exC.n += 1

            @block.sync
            def _(s):
                for qc8 in range(NQC):
                    r = qc8 // NQL
                    q0 = (qc8 % NQL) * 512
                    s.wait_ge(s_exC.h, DC * (qc8 + 1))
                    s.dma_start(out=io["cc_in"].ap()[r].rearrange("(c p) t -> p c t", p=128)[:, :, q0:q0 + 512],
                                in_=sA[:, :, :]).then_inc(m_ao.h, 16)

    with ExitStack() as sd:
        kd = K(nc, block, sd)
        aoslab = kd.sb("D_ao", [128, DC, 512], F32)
        xslab = kd.sb("D_x", [128, DC, 512], F32)
        sD = kd.sb("D_s", [128, DC, 512], F32)
        scr2 = kd.sb("D_scr2", [128, 2, 512], F32)
        rowbuf = kd.sb("D_rows", [1, 8, 512], F32)
        rb2 = kd.sb("D_rb2", [128, 2, 512], F32)

        @block.sync
        def _(s):
            for ql in range(NQL):
                q0 = ql * 512
                if ql == 0:
                    s.wait_ge(s_cc.h, 1)
                else:
                    s.wait_ge(s_ex.h, DC * ql)
                s.dma_start(out=aoslab[:, :, :],
                            in_=io["cc_out"].ap().rearrange("(c p) t -> p c t", p=128)[:, :, q0:q0 + 512]
                            ).then_inc(m_dld.h, 16)
                s.dma_start(out=xslab[:, :, :],
                            in_=io["xTh"].rearrange("(c p) t -> p c t", p=128)[:, :, q0:q0 + 512]
                            ).then_inc(m_dld.h, 16)
                s.wait_ge(s_nrm.h, ql + 1)
                s.dma_start(out=io["out"].rearrange("(c p) t -> p c t", p=128)[:, :, q0:q0 + 512],
                            in_=sD[:, :, :]).then_inc(m_out.h, 16)
            s.wait_ge(m_out.h, 16 * NQL)

        @block.tensor
        def _(t):
            for ql in range(NQL):
                for oc in range(DC):
                    gi = ql * DC + oc
                    t.wait_ge(s_ex.h, gi + 1)
                    if oc == 0 and ql >= 1:
                        t.wait_ge(s_row.h, 2 * ql - 1)
                    mm = t.matmul(ps8[0:1, 2, :], ones_col_f[:, :], sD[:, oc, :],
                                  start=(oc == 0), stop=(oc == DC - 1))
                    t.wait_ge(s_sq2.h, (DC + 1) * ql + oc + 1)
                    mm = t.matmul(ps8[0:1, 3, :], ones_col_f[:, :], scr2[:, oc % 2, :],
                                  start=(oc == 0), stop=(oc == DC - 1))
                    mm.then_inc(s_stat.h, 1)
                    s_stat.n += 1
                t.wait_ge(s_row.h, 2 * ql + 2)
                if ql >= 1:
                    t.wait_ge(s_nrm.h, ql)
                mm = t.matmul(ps8[:, 4, :], ones_row_f[:, :], rowbuf[:, 4, :],
                              start=True, stop=True)
                mm = t.matmul(ps8[:, 5, :], ones_row_f[:, :], rowbuf[:, 5, :],
                              start=True, stop=True)
                mm.then_inc(s_peB2.h, 1)
                s_peB2.n += 1

        @block.scalar
        def _(sc):
            for ql in range(NQL):
                for oc in range(DC):
                    gi = ql * DC + oc
                    sc.wait_ge(s_ex.h, gi + 1)
                    if gi >= 2:
                        sc.wait_ge(s_stat.h, gi - 1)
                    op = sc.activation(out=scr2[:, oc % 2, :], in_=sD[:, oc, :],
                                       func=AF.Square)
                    op.then_inc(s_sq2.h, 1)
                    s_sq2.n += 1
                sc.wait_ge(s_row.h, 2 * ql + 1)
                op = sc.activation(out=rowbuf[:, 6, :], in_=rowbuf[:, 1, :], func=AF.Sqrt)
                op.then_inc(s_sq2.h, 1)
                s_sq2.n += 1

        @block.vector
        def _(v):
            for ql in range(NQL):
                for oc in range(DC):
                    v.wait_ge(m_dld.h, 32 * (ql + 1))
                    if ql == 0 and oc == 0:
                        v.wait_ge(m_ao.h, 16 * NQC)
                    op = v.tensor_tensor(out=sD[:, oc, :], in0=aoslab[:, oc, :],
                                         in1=xslab[:, oc, :], op=OP.add)
                    op.then_inc(s_ex.h, 1)
                    s_ex.n += 1
                v.wait_ge(s_stat.h, DC * (ql + 1))
                v.tensor_scalar(out=rowbuf[:, 2, :], in0=ps8[0:1, 2, :],
                                scalar1=1.0 / D, scalar2=None, op0=OP.mult)
                v.tensor_scalar(out=rowbuf[:, 3, :], in0=ps8[0:1, 3, :],
                                scalar1=1.0 / D, scalar2=None, op0=OP.mult)
                v.tensor_tensor(out=rowbuf[:, 0, :], in0=rowbuf[:, 2, :],
                                in1=rowbuf[:, 2, :], op=OP.mult)
                v.tensor_tensor(out=rowbuf[:, 3, :], in0=rowbuf[:, 3, :],
                                in1=rowbuf[:, 0, :], op=OP.subtract)
                op = v.tensor_scalar(out=rowbuf[:, 1, :], in0=rowbuf[:, 3, :],
                                     scalar1=1e-5, scalar2=None, op0=OP.add)
                op.then_inc(s_row.h, 1)
                s_row.n += 1
                v.wait_ge(s_sq2.h, (DC + 1) * (ql + 1))
                v.reciprocal(rowbuf[:, 4, :], rowbuf[:, 6, :])
                op = v.tensor_tensor(out=rowbuf[:, 5, :], in0=rowbuf[:, 2, :],
                                     in1=rowbuf[:, 4, :], op=OP.mult)
                op.then_inc(s_row.h, 1)
                s_row.n += 1
                v.wait_ge(s_peB2.h, ql + 1)
                v.tensor_copy(rb2[:, 0, :], ps8[:, 4, :])
                v.tensor_copy(rb2[:, 1, :], ps8[:, 5, :])
                for oc in range(DC):
                    v.tensor_tensor(out=scr2[:, oc % 2, :], in0=sD[:, oc, :],
                                    in1=rb2[:, 0, :], op=OP.mult)
                    op = v.tensor_tensor(out=sD[:, oc, :], in0=scr2[:, oc % 2, :],
                                         in1=rb2[:, 1, :], op=OP.subtract)
                op.then_inc(s_nrm.h, 1)
                s_nrm.n += 1


def build_l0attn(cfg):
    c = derive(cfg)
    L, D, NHC, NJ, TH = c["L"], c["D"], c["NHC"], c["NJ"], c["TH"]
    HD = NHC * 128
    nc = bass.Bass("TRN2", target_bir_lowering=False, num_devices=8,
                   detect_race_conditions=False)
    io = {}
    io["xT"] = nc.dram_tensor("xT", [D, L], F32, kind="ExternalInput").ap()
    io["xTh"] = nc.dram_tensor("xTh", [D, TH], F32, kind="ExternalInput").ap()
    io["wq"] = nc.dram_tensor("wq", [D, NJ * 128], F32, kind="ExternalInput").ap()
    io["wk"] = nc.dram_tensor("wk", [D, HD], F32, kind="ExternalInput").ap()
    io["wv"] = nc.dram_tensor("wv", [D, HD], F32, kind="ExternalInput").ap()
    io["wl"] = nc.dram_tensor("wl", [D, NHC], F32, kind="ExternalInput").ap()
    io["bl"] = nc.dram_tensor("bl", [NHC], F32, kind="ExternalInput").ap()
    io["wo"] = nc.dram_tensor("wo", [NHC * 128, D], F32, kind="ExternalInput").ap()
    io["cosk"] = nc.dram_tensor("cosk", [128, L], F32, kind="ExternalInput").ap()
    io["sink"] = nc.dram_tensor("sink", [128, L], F32, kind="ExternalInput").ap()
    io["masks"] = nc.dram_tensor("masks", [4, 128, 512], BF16, kind="ExternalInput").ap()
    io["out"] = nc.dram_tensor("stream_out", [D, TH], F32, kind="ExternalOutput").ap()
    io["q_scr"] = nc.dram_tensor("q_scr", [NJ * 128, L], BF16).ap()
    io["lam_scr"] = nc.dram_tensor("lam_scr", [NHC, L], F32).ap()
    io["cc_in"] = nc.dram_tensor("cc_in", [2, D, TH], F32)
    io["cc_out"] = nc.dram_tensor("cc_out", [D, TH], F32)
    with ExitStack() as stack:
        block = stack.enter_context(nc.Block())
        k = K(nc, block, stack)
        emit_l0attn(k, c, io)
    return nc


def rope_tables_l0(L):
    half = 64
    inv = 1.0 / (10000.0 ** (np.arange(half, dtype=np.float64) * 2.0 / 128.0))
    freqs = np.arange(L, dtype=np.float64)[None, :] * inv[:, None]
    emb = np.concatenate([freqs, freqs], axis=0)
    return (np.cos(emb).astype(np.float32).copy(),
            np.sin(emb).astype(np.float32).copy())


def causal_masks():
    m = np.zeros((4, 128, 512), dtype=BF)
    qq = np.arange(512)[None, :]
    kk = np.arange(128)[:, None]
    for off in range(4):
        m[off] = (qq >= kk + 128 * off).astype(BF)
    return m


def prep_l0attn_inputs(cfg, inp):
    c = derive(cfg)
    L, D, NHC, NJ, TH = c["L"], c["D"], c["NHC"], c["NJ"], c["TH"]
    x = np.asarray(inp["L0_attn_x"], dtype=np.float32)
    Wq = np.asarray(inp["L0_attn_Wq"], dtype=np.float32)
    Wk = np.asarray(inp["L0_attn_Wk"], dtype=np.float32)
    Wv = np.asarray(inp["L0_attn_Wv"], dtype=np.float32)
    Wl = np.asarray(inp["L0_attn_Wl"], dtype=np.float32)
    blv = np.asarray(inp["L0_attn_bl"], dtype=np.float32)
    Wo = np.asarray(inp["L0_attn_Wo"], dtype=np.float32)
    cosk, sink = rope_tables_l0(L)
    msk = causal_masks()
    maps = []
    for core in range(8):
        b, g = core // 2, core % 2
        xT = np.ascontiguousarray(x[b].T)
        xTh = np.ascontiguousarray(x[b, g * TH:(g + 1) * TH, :].T)
        j0 = g * NJ * 128
        maps.append(dict(
            xT=xT, xTh=xTh,
            wq=np.ascontiguousarray(Wq[:, j0:j0 + NJ * 128]),
            wk=np.ascontiguousarray(Wk[:, g * NHC * 128:(g + 1) * NHC * 128]),
            wv=np.ascontiguousarray(Wv[:, g * NHC * 128:(g + 1) * NHC * 128]),
            wl=np.ascontiguousarray(Wl[:, g * NHC:(g + 1) * NHC]),
            bl=np.ascontiguousarray(blv[g * NHC:(g + 1) * NHC]),
            wo=np.ascontiguousarray(Wo[g * NHC * 128:(g + 1) * NHC * 128, :]),
            cosk=cosk, sink=sink, masks=msk,
        ))
    return maps


_PROG_CACHE = {}


def _get_prog(name, builder, cfg):
    key = (name, repr(sorted(cfg.items())))
    if key not in _PROG_CACHE:
        _PROG_CACHE[key] = builder(cfg)
    return _PROG_CACHE[key]


def kernel(**inputs):
    cfg = FULL
    c = derive(cfg)
    TH, D, L = c["TH"], c["D"], c["L"]
    cores = list(range(8))
    nc1 = _get_prog("l0attn", build_l0attn, cfg)
    res1 = run_bass_kernel_spmd(nc1, prep_l0attn_inputs(cfg, inputs), core_ids=cores)
    streams = [np.ascontiguousarray(res1.results[i]["stream_out"]) for i in range(8)]
    nc2 = _get_prog("l0ffn", build_l0ffn, cfg)
    res2 = run_bass_kernel_spmd(nc2, prep_l0ffn_inputs(cfg, inputs, streams), core_ids=cores)
    streams = [np.ascontiguousarray(res2.results[i]["stream_out"]) for i in range(8)]
    nc3 = _get_prog("l1attn", build_l1attn, cfg)
    res3 = run_bass_kernel_spmd(nc3, prep_l1attn_inputs(cfg, inputs, streams), core_ids=cores)
    streams = [np.ascontiguousarray(res3.results[i]["stream_out"]) for i in range(8)]
    nc4 = _get_prog("l1ffn", build_l1ffn, cfg)
    res4 = run_bass_kernel_spmd(nc4, prep_l1ffn_inputs(cfg, inputs, streams), core_ids=cores)
    streams = [res4.results[i]["stream_out"] for i in range(8)]
    out = np.zeros((4, L, D), dtype=np.float32)
    for core in range(8):
        b, g = core // 2, core % 2
        out[b, g * TH:(g + 1) * TH, :] = streams[core].T
    return out
def emit_l0ffn(k, c, io):
    nc, block = k.nc, k.block
    L, D, DC, NHC = c["L"], c["D"], c["DC"], c["NHC"]
    TH, NQL = c["TH"], c["NQL"]
    I0 = c["I0"]
    NG = I0 // 128
    W0 = D // 8
    G0 = D // 128
    W1 = I0 // 8
    G1 = I0 // 128
    NHW = max(1, D // 512)
    HB = min(D, 512)
    NTCH = TH // 512

    ones_row_f = k.sb("ones_row_f", [1, 128], F32)
    ones_col_f = k.sb("ones_col_f", [128, 1], F32)
    identb = k.sb("identb", [128, 128], BF16)
    ps8 = k.ps("ps8", [128, 8, 512])
    psbf = ps8.bitcast(BF16)

    m_id = k.sem("m_id")
    m_pk = k.sem("m_pk")
    m_sc = k.sem("m_sc")
    m_sg = k.sem("m_sg")
    m_sd = k.sem("m_sd")
    m_xbf = k.sem("m_xbf")
    m_wgu = [k.sem("m_wgu0"), k.sem("m_wgu1")]
    m_wdn = [k.sem("m_wdn0"), k.sem("m_wdn1")]
    m_str = [k.sem("m_str0"), k.sem("m_str1")]
    m_out = k.sem("m_out")

    s_dq = k.sem("s_dq")
    s_fn = k.sem("s_fn")
    s_tr = k.sem("s_tr")
    s_ev = k.sem("s_ev")
    s_gu = k.sem("s_gu")
    s_si = k.sem("s_si")
    s_hb = k.sem("s_hb")
    s_dn = k.sem("s_dn")
    s_ye = k.sem("s_ye")
    s_sq = k.sem("s_sq")
    s_st2 = k.sem("s_st2")
    s_row = k.sem("s_row")
    s_bc = k.sem("s_bc")
    s_rs = k.sem("s_rs")
    s_nrm = k.sem("s_nrm")
    s_cc = k.sem("s_cc")

    NSH = c["NSH"]
    NGU_loc = 2 * NG // NSH
    NDN_loc = DC // NSH
    NT1 = NGU_loc + NDN_loc
    if "dbg_val" in io:
        io["_dbgf"] = k.sem("m_dbgf")

    def tinfo(ti):
        if ti < NGU_loc:
            return ("gu", ti)
        return ("dn", ti - NGU_loc)

    def ntr_of(ti):
        return DC if tinfo(ti)[0] == "gu" else NG

    trbase = [0]
    for ti in range(NT1):
        trbase.append(trbase[-1] + ntr_of(ti))

    with ExitStack() as sf:
        kf = K(nc, block, sf)
        pk_sb = kf.sb("F_pk", [128, max(W0, W1)], I32)
        nib = kf.sb("F_nib", [128, 2, max(W0, W1)], I32)
        ge8 = kf.sb("F_ge8", [128, 2, max(W0, W1)], F32)
        bits = kf.sb("F_bits", [128, 2, max(W0, W1)], U32)
        val = kf.sb("F_val", [128, max(8 * W0, 8 * W1)], BF16)
        scT = kf.sb("F_scT", [128, max(G0, G1)], F32)
        scP = kf.sb("F_scP", [128, max(G0, G1)], F32)
        stg = kf.sb("F_stg", [128, max(DC, NG), 128], BF16)

        @block.gpsimd
        def _(g):
            for ti in range(NT1):
                kind, idx = tinfo(ti)
                if ti >= 1:
                    g.wait_ge(s_dq.h, ti)
                if kind == "gu":
                    g.dma_start(out=pk_sb[:, 0:W0],
                                in_=io["gup"][idx * 128:(idx + 1) * 128, :]).then_inc(m_pk.h, 16)
                    g.dma_start(out=scT[:, 0:G0],
                                in_=io["gusT"][idx * 128:(idx + 1) * 128, :]).then_inc(m_sc.h, 16)
                else:
                    g.dma_start(out=pk_sb[:, 0:W1],
                                in_=io["dnp"][idx * 128:(idx + 1) * 128, :]).then_inc(m_pk.h, 16)
                    g.dma_start(out=scT[:, 0:G1],
                                in_=io["dnsT"][idx * 128:(idx + 1) * 128, :]).then_inc(m_sc.h, 16)
            g.wait_ge(m_sg.h, 16 * NGU_loc)
            g.wait_ge(m_sd.h, 16 * NDN_loc)
            g.collective_compute("AllGather", OP.bypass, replica_groups=c["CCG"],
                                 ins=[io["wgu_sh"].opt()],
                                 outs=[io["wguT_scr"].opt()]).then_inc(s_cc.h, 1)
            g.collective_compute("AllGather", OP.bypass, replica_groups=c["CCG"],
                                 ins=[io["wdn_sh"].opt()],
                                 outs=[io["wdn_all"].opt()]).then_inc(s_cc.h, 1)

        @block.vector
        def _(v):
            v.memset(ones_row_f[:, :], 1.0)
            v.memset(ones_col_f[:, :], 1.0)
            ngu_st = 0
            ndn_st = 0
            for ti in range(NT1):
                kind, idx = tinfo(ti)
                nw = W0 if kind == "gu" else W1
                ngr = G0 if kind == "gu" else G1
                ntr = ntr_of(ti)
                v.wait_ge(m_pk.h, 16 * (ti + 1))
                v.wait_ge(m_sc.h, 16 * (ti + 1))
                v.tensor_scalar(out=scP[:, 0:ngr], in0=scT[:, 0:ngr],
                                scalar1=float(2.0 ** 126), scalar2=None, op0=OP.mult)
                for kp in range(0, 8, 2):
                    for q_ in (0, 1):
                        op = v.tensor_scalar(out=nib[:, q_, 0:nw], in0=pk_sb[:, 0:nw],
                                             scalar1=4 * (kp + q_), scalar2=15,
                                             op0=OP.logical_shift_right, op1=OP.bitwise_and)
                    op.then_inc(s_fn.h, 1)
                    s_fn.n += 1
                    v.wait_ge(s_fn.h, s_fn.n)
                    for q_ in (0, 1):
                        op = v.tensor_scalar(out=ge8[:, q_, 0:nw], in0=nib[:, q_, 0:nw],
                                             scalar1=8.0, scalar2=float(2 ** 31 - 2 ** 25),
                                             op0=OP.is_ge, op1=OP.mult)
                    op.then_inc(s_fn.h, 1)
                    s_fn.n += 1
                    v.wait_ge(s_fn.h, s_fn.n)
                    for q_ in (0, 1):
                        op = v.scalar_tensor_tensor(out=bits[:, q_, 0:nw], in0=nib[:, q_, 0:nw],
                                                    scalar=float(2 ** 22), in1=ge8[:, q_, 0:nw],
                                                    op0=OP.mult, op1=OP.add)
                    op.then_inc(s_fn.h, 1)
                    s_fn.n += 1
                    v.wait_ge(s_fn.h, s_fn.n)
                    for q_ in (0, 1):
                        op = v.tensor_tensor(
                            out=val[:, kp + q_:8 * nw:8],
                            in0=bits[:, q_, 0:nw].bitcast(F32),
                            in1=scP[:, 0:ngr, None].broadcast_to([128, ngr, 16]),
                            op=OP.mult)
                op.then_inc(s_dq.h, 1)
                s_dq.n += 1
                for tr in range(ntr):
                    gi = trbase[ti] + tr
                    v.wait_ge(s_tr.h, gi + 1)
                    if tr == 0 and ti >= 1:
                        pkind, _ = tinfo(ti - 1)
                        if pkind == "gu":
                            v.wait_ge(m_sg.h, 16 * ngu_st)
                        else:
                            v.wait_ge(m_sd.h, 16 * ndn_st)
                    op = v.tensor_copy(stg[:, tr, :], psbf[:, gi % 4, 0:128])
                    op.then_inc(s_ev.h, 1)
                    s_ev.n += 1
                if kind == "gu":
                    ngu_st += 1
                else:
                    ndn_st += 1

        @block.tensor
        def _(t):
            t.wait_ge(m_id.h, 16)
            for ti in range(NT1):
                ntr = ntr_of(ti)
                t.wait_ge(s_dq.h, ti + 1)
                for tr in range(ntr):
                    gi = trbase[ti] + tr
                    if gi >= 4:
                        t.wait_ge(s_ev.h, gi - 3)
                    mm = t.transpose(psbf[:, gi % 4, 0:128],
                                     val[:, tr * 128:(tr + 1) * 128], identb[:, :])
                    mm.then_inc(s_tr.h, 1)
                    s_tr.n += 1

        @block.sync
        def _(s):
            s.dma_start(out=identb[:, :], in_=io["ident"]).then_inc(m_id.h, 16)
            for ti in range(NT1):
                kind, idx = tinfo(ti)
                if "dbg_val" in io and ti == io["dbg_ti"]:
                    s.wait_ge(s_dq.h, ti + 1)
                    s.dma_start(out=io["dbg_val"], in_=val[:, 0:8*W0]).then_inc(io["_dbgf"].h, 16)
                s.wait_ge(s_ev.h, trbase[ti + 1])
                if kind == "gu":
                    s.dma_start(out=io["wgu_sh"][idx].rearrange("c p o -> p c o"),
                                in_=stg[:, 0:DC, :]).then_inc(m_sg.h, 16)
                else:
                    s.dma_start(out=io["wdn_sh"][idx].rearrange("c p o -> p c o"),
                                in_=stg[:, 0:NG, :]).then_inc(m_sd.h, 16)

    with ExitStack() as sf2:
        kf2 = K(nc, block, sf2)
        xbf = kf2.sb("G_xbf", [128, DC, 512], BF16)
        wgu = kf2.sb("G_wgu", [128, 2, 2, DC, 128], BF16)
        wdn = kf2.sb("G_wdn", [128, 2, NG, 128], BF16)
        hbuf = kf2.sb("G_hbuf", [128, NG, 512], BF16)
        gsil = kf2.sb("G_gsil", [128, 2, 512], BF16)
        ybuf = kf2.sb("G_ybuf", [128, DC, 512], F32)
        strsl = kf2.sb("G_strsl", [128, 2, 512], F32)
        scr2 = kf2.sb("G_scr2", [128, 2, 512], F32)
        rowbuf = kf2.sb("G_rows", [1, 8, 512], F32)
        rb2 = kf2.sb("G_rb2", [128, 2, 512], F32)
        rmsb = kf2.sb("G_rmsb", [128, 512], F32)
        atmp = kf2.sb("G_atmp", [128, 512], F32)

        @block.gpsimd
        def _(g):
            g.wait_ge(s_ev.h, trbase[NT1])
            g.wait_ge(s_cc.h, 2)
            for tc in range(NTCH):
                t0 = tc * 512
                if tc >= 1:
                    g.wait_ge(s_gu.h, NG * tc)
                g.dma_start(out=xbf[:, :, :],
                            in_=io["stream_in"].rearrange("(c p) t -> p c t", p=128)[:, :, t0:t0 + 512]
                            ).then_inc(m_xbf.h, 16)

        @block.sync
        def _(s):
            s.wait_ge(s_cc.h, 2)
            for tc in range(NTCH):
                t0 = tc * 512
                for it in range(NG):
                    gi = tc * NG + it
                    if gi >= 2:
                        s.wait_ge(s_gu.h, gi - 1)
                    s.dma_start(out=wgu[:, it % 2, 0, :, :],
                                in_=io["wguT_scr"][it].rearrange("c p o -> p c o")
                                ).then_inc(m_wgu[it % 2].h, 16)
                    s.dma_start(out=wgu[:, it % 2, 1, :, :],
                                in_=io["wguT_scr"][NG + it].rearrange("c p o -> p c o")
                                ).then_inc(m_wgu[it % 2].h, 16)
                for ho in range(DC):
                    g2 = tc * DC + ho
                    if g2 >= 2:
                        s.wait_ge(s_dn.h, g2 - 1)
                    s.dma_start(out=wdn[:, ho % 2, :, :],
                                in_=io["wdn_all"][ho].rearrange("c p o -> p c o")
                                ).then_inc(m_wdn[ho % 2].h, 16)
                for ho in range(DC):
                    g2 = tc * DC + ho
                    if g2 >= 2:
                        s.wait_ge(s_rs.h, g2 - 1)
                    s.dma_start(out=strsl[:, g2 % 2, :],
                                in_=io["stream_in"].rearrange("(c p) t -> p c t", p=128)[:, ho, t0:t0 + 512]
                                ).then_inc(m_str[g2 % 2].h, 16)
                s.wait_ge(s_nrm.h, tc + 1)
                s.dma_start(out=io["out"].rearrange("(c p) t -> p c t", p=128)[:, :, t0:t0 + 512],
                            in_=ybuf[:, :, :]).then_inc(m_out.h, 16)
            s.wait_ge(m_out.h, 16 * NTCH)

        @block.tensor
        def _(t):
            for tc in range(NTCH):
                t.wait_ge(m_xbf.h, 16 * (tc + 1))
                for it in range(NG):
                    gi = tc * NG + it
                    t.wait_ge(m_wgu[it % 2].h, 32 * (gi // 2 + 1))
                    if gi >= 2:
                        t.wait_ge(s_hb.h, gi - 1)
                    if it == 0 and tc >= 1:
                        t.wait_ge(s_nrm.h, tc)
                    for kc in range(DC):
                        mm = t.matmul(ps8[:, 2 * (gi % 2), :], wgu[:, it % 2, 0, kc, :],
                                      xbf[:, kc, :], start=(kc == 0), stop=(kc == DC - 1))
                    for kc in range(DC):
                        mm2 = t.matmul(ps8[:, 2 * (gi % 2) + 1, :], wgu[:, it % 2, 1, kc, :],
                                       xbf[:, kc, :], start=(kc == 0), stop=(kc == DC - 1))
                    mm2.then_inc(s_gu.h, 1)
                    s_gu.n += 1
                for ho in range(DC):
                    g2 = tc * DC + ho
                    t.wait_ge(m_wdn[ho % 2].h, 16 * (g2 // 2 + 1))
                    if g2 >= 2:
                        t.wait_ge(s_ye.h, g2 - 1)
                    for ic in range(NG):
                        t.wait_ge(s_hb.h, tc * NG + ic + 1)
                        mm = t.matmul(ps8[:, 4 + g2 % 2, :], wdn[:, ho % 2, ic, :],
                                      hbuf[:, ic, :], start=(ic == 0), stop=(ic == NG - 1))
                    mm.then_inc(s_dn.h, 1)
                    s_dn.n += 1
                for ho in range(DC):
                    t.wait_ge(s_sq.h, (2 * DC + 2) * tc + ho + 1)
                    if ho == 0 and tc >= 1:
                        t.wait_ge(s_row.h, 4 * tc - 1)
                    mm = t.matmul(ps8[0:1, 6, :], ones_col_f[:, :], scr2[:, ho % 2, :],
                                  start=(ho == 0), stop=(ho == DC - 1))
                    mm.then_inc(s_st2.h, 1)
                    s_st2.n += 1
                t.wait_ge(s_row.h, 4 * tc + 2)
                mm = t.matmul(ps8[:, 7, :], ones_row_f[:, :], rowbuf[:, 4, :],
                              start=True, stop=True)
                mm.then_inc(s_bc.h, 1)
                s_bc.n += 1
                for ho in range(DC):
                    t.wait_ge(s_sq.h, (2 * DC + 2) * tc + DC + 1 + ho + 1)
                    if ho == 0:
                        t.wait_ge(s_row.h, 4 * tc + 1)
                    mm = t.matmul(ps8[0:1, 0, :], ones_col_f[:, :], ybuf[:, ho, :],
                                  start=(ho == 0), stop=(ho == DC - 1))
                    mm2 = t.matmul(ps8[0:1, 6, :], ones_col_f[:, :], scr2[:, ho % 2, :],
                                   start=(ho == 0), stop=(ho == DC - 1))
                    mm2.then_inc(s_st2.h, 1)
                    s_st2.n += 1
                t.wait_ge(s_row.h, 4 * tc + 4)
                mm = t.matmul(ps8[:, 7, :], ones_row_f[:, :], rowbuf[:, 4, :],
                              start=True, stop=True)
                mm2 = t.matmul(ps8[:, 1, :], ones_row_f[:, :], rowbuf[:, 5, :],
                               start=True, stop=True)
                mm2.then_inc(s_bc.h, 1)
                s_bc.n += 1

        @block.scalar
        def _(sc):
            for tc in range(NTCH):
                for it in range(NG):
                    gi = tc * NG + it
                    sc.wait_ge(s_gu.h, gi + 1)
                    if gi >= 2:
                        sc.wait_ge(s_hb.h, gi - 1)
                    op = sc.activation(out=gsil[:, gi % 2, :], in_=ps8[:, 2 * (gi % 2), :],
                                       func=AF.Silu)
                    op.then_inc(s_si.h, 1)
                    s_si.n += 1
                for ho in range(DC):
                    g2 = tc * DC + ho
                    sc.wait_ge(s_ye.h, g2 + 1)
                    if ho >= 2:
                        sc.wait_ge(s_st2.h, 2 * DC * tc + ho - 1)
                    op = sc.activation(out=scr2[:, ho % 2, :], in_=ybuf[:, ho, :],
                                       func=AF.Square)
                    op.then_inc(s_sq.h, 1)
                    s_sq.n += 1
                sc.wait_ge(s_row.h, 4 * tc + 1)
                op = sc.activation(out=rowbuf[:, 6, :], in_=rowbuf[:, 1, :], func=AF.Sqrt)
                op.then_inc(s_sq.h, 1)
                s_sq.n += 1
                for ho in range(DC):
                    g2 = tc * DC + ho
                    sc.wait_ge(s_rs.h, g2 + 1)
                    if ho >= 2:
                        sc.wait_ge(s_st2.h, 2 * DC * tc + DC + ho - 1)
                    op = sc.activation(out=scr2[:, ho % 2, :], in_=ybuf[:, ho, :],
                                       func=AF.Square)
                    op.then_inc(s_sq.h, 1)
                    s_sq.n += 1
                sc.wait_ge(s_row.h, 4 * tc + 3)
                op = sc.activation(out=rowbuf[:, 6, :], in_=rowbuf[:, 1, :], func=AF.Sqrt)
                op.then_inc(s_sq.h, 1)
                s_sq.n += 1

        @block.vector
        def _(v):
            for tc in range(NTCH):
                for it in range(NG):
                    gi = tc * NG + it
                    v.wait_ge(s_si.h, gi + 1)
                    if it == 0 and tc >= 1:
                        v.wait_ge(s_dn.h, DC * tc)
                    op = v.tensor_tensor(out=hbuf[:, it, :], in0=gsil[:, gi % 2, :],
                                         in1=ps8[:, 2 * (gi % 2) + 1, :], op=OP.mult)
                    op.then_inc(s_hb.h, 1)
                    s_hb.n += 1
                for ho in range(DC):
                    g2 = tc * DC + ho
                    v.wait_ge(s_dn.h, g2 + 1)
                    if ho == 0:
                        v.wait_ge(m_out.h, 16 * tc)
                    op = v.tensor_copy(ybuf[:, ho, :], ps8[:, 4 + g2 % 2, :])
                    op.then_inc(s_ye.h, 1)
                    s_ye.n += 1
                v.wait_ge(s_st2.h, 2 * DC * tc + DC)
                v.tensor_scalar(out=rowbuf[:, 0, :], in0=ps8[0:1, 6, :],
                                scalar1=1.0 / D, scalar2=None, op0=OP.mult)
                op = v.tensor_scalar(out=rowbuf[:, 1, :], in0=rowbuf[:, 0, :],
                                     scalar1=1e-6, scalar2=None, op0=OP.add)
                op.then_inc(s_row.h, 1)
                s_row.n += 1
                v.wait_ge(s_sq.h, (2 * DC + 2) * tc + DC + 1)
                op = v.reciprocal(rowbuf[:, 4, :], rowbuf[:, 6, :])
                op.then_inc(s_row.h, 1)
                s_row.n += 1
                v.wait_ge(s_bc.h, 2 * tc + 1)
                v.tensor_copy(rmsb[:, :], ps8[:, 7, :])
                for ho in range(DC):
                    g2 = tc * DC + ho
                    v.wait_ge(m_str[g2 % 2].h, 16 * (g2 // 2 + 1))
                    v.tensor_tensor(out=atmp[:, :], in0=ybuf[:, ho, :],
                                    in1=rmsb[:, :], op=OP.mult)
                    op = v.tensor_tensor(out=ybuf[:, ho, :], in0=atmp[:, :],
                                         in1=strsl[:, g2 % 2, :], op=OP.add)
                    op.then_inc(s_rs.h, 1)
                    s_rs.n += 1
                v.wait_ge(s_st2.h, 2 * DC * (tc + 1))
                v.tensor_scalar(out=rowbuf[:, 2, :], in0=ps8[0:1, 0, :],
                                scalar1=1.0 / D, scalar2=None, op0=OP.mult)
                v.tensor_scalar(out=rowbuf[:, 3, :], in0=ps8[0:1, 6, :],
                                scalar1=1.0 / D, scalar2=None, op0=OP.mult)
                v.tensor_tensor(out=rowbuf[:, 0, :], in0=rowbuf[:, 2, :],
                                in1=rowbuf[:, 2, :], op=OP.mult)
                v.tensor_tensor(out=rowbuf[:, 3, :], in0=rowbuf[:, 3, :],
                                in1=rowbuf[:, 0, :], op=OP.subtract)
                op = v.tensor_scalar(out=rowbuf[:, 1, :], in0=rowbuf[:, 3, :],
                                     scalar1=1e-5, scalar2=None, op0=OP.add)
                op.then_inc(s_row.h, 1)
                s_row.n += 1
                v.wait_ge(s_sq.h, (2 * DC + 2) * (tc + 1))
                v.reciprocal(rowbuf[:, 4, :], rowbuf[:, 6, :])
                op = v.tensor_tensor(out=rowbuf[:, 5, :], in0=rowbuf[:, 2, :],
                                     in1=rowbuf[:, 4, :], op=OP.mult)
                op.then_inc(s_row.h, 1)
                s_row.n += 1
                v.wait_ge(s_bc.h, 2 * tc + 2)
                v.tensor_copy(rb2[:, 0, :], ps8[:, 7, :])
                v.tensor_copy(rb2[:, 1, :], ps8[:, 1, :])
                for ho in range(DC):
                    v.tensor_tensor(out=atmp[:, :], in0=ybuf[:, ho, :],
                                    in1=rb2[:, 0, :], op=OP.mult)
                    op = v.tensor_tensor(out=ybuf[:, ho, :], in0=atmp[:, :],
                                         in1=rb2[:, 1, :], op=OP.subtract)
                op.then_inc(s_nrm.h, 1)
                s_nrm.n += 1


def build_l0ffn(cfg):
    c = derive(cfg)
    D, TH, I0, DC = c["D"], c["TH"], c["I0"], c["DC"]
    NG = I0 // 128
    NHW = max(1, D // 512)
    HB = min(D, 512)
    nc = bass.Bass("TRN2", target_bir_lowering=False, num_devices=8,
                   detect_race_conditions=False)
    io = {}
    NSH = c["NSH"]
    NGU_loc = 2 * NG // NSH
    NDN_loc = DC // NSH
    io["stream_in"] = nc.dram_tensor("stream_in", [D, TH], F32, kind="ExternalInput").ap()
    io["gup"] = nc.dram_tensor("gup", [NGU_loc * 128, D // 8], I32, kind="ExternalInput").ap()
    io["gusT"] = nc.dram_tensor("gusT", [NGU_loc * 128, D // 128], F16, kind="ExternalInput").ap()
    io["dnp"] = nc.dram_tensor("dnp", [NDN_loc * 128, I0 // 8], I32, kind="ExternalInput").ap()
    io["dnsT"] = nc.dram_tensor("dnsT", [NDN_loc * 128, I0 // 128], F16, kind="ExternalInput").ap()
    io["ident"] = nc.dram_tensor("ident", [128, 128], BF16, kind="ExternalInput").ap()
    io["out"] = nc.dram_tensor("stream_out", [D, TH], F32, kind="ExternalOutput").ap()
    io["wgu_sh"] = nc.dram_tensor("wgu_sh", [NGU_loc, DC, 128, 128], BF16).ap()
    io["wdn_sh"] = nc.dram_tensor("wdn_sh", [NDN_loc, NG, 128, 128], BF16).ap()
    io["wguT_scr"] = nc.dram_tensor("wguT_scr", [2 * NG, DC, 128, 128], BF16).ap()
    io["wdn_all"] = nc.dram_tensor("wdn_all", [DC, NG, 128, 128], BF16).ap()
    with ExitStack() as stack:
        block = stack.enter_context(nc.Block())
        k = K(nc, block, stack)
        emit_l0ffn(k, c, io)
    return nc


def prep_l0ffn_inputs(cfg, inp, streams):
    c = derive(cfg)
    gup = np.ascontiguousarray(np.asarray(inp["L0_ffn_gate_up_packed"], dtype=np.int32))
    gusT = np.ascontiguousarray(np.asarray(inp["L0_ffn_gate_up_scales"]).T)
    dnp = np.ascontiguousarray(np.asarray(inp["L0_ffn_down_packed"], dtype=np.int32))
    dnsT = np.ascontiguousarray(np.asarray(inp["L0_ffn_down_scales"]).T)
    ident = np.eye(128, dtype=BF)
    NSH = c["NSH"]
    RGU = gup.shape[0] // NSH
    RDN = dnp.shape[0] // NSH
    maps = []
    for core in range(8):
        sh = core % NSH
        maps.append(dict(
            stream_in=streams[core],
            gup=np.ascontiguousarray(gup[sh * RGU:(sh + 1) * RGU]),
            gusT=np.ascontiguousarray(gusT[sh * RGU:(sh + 1) * RGU]),
            dnp=np.ascontiguousarray(dnp[sh * RDN:(sh + 1) * RDN]),
            dnsT=np.ascontiguousarray(dnsT[sh * RDN:(sh + 1) * RDN]),
            ident=ident))
    return maps
def emit_l1attn(k, c, io):
    nc, block = k.nc, k.block
    D, DC = c["D"], c["DC"]
    TH = c["TH"]
    E3 = 3 * DC
    RH = DC // 4
    NTCH = TH // 512
    RPT = 2 * RH + DC

    ones_row_f = k.sb("ones_row_f", [1, 128], F32)
    ones_col_f = k.sb("ones_col_f", [128, 1], F32)
    identf = k.sb("identf_sb", [128, 128], F32)
    binT = k.sb("binT_sb", [128, E3], F32)
    boutT = k.sb("boutT_sb", [128, DC], F32)
    aacc = k.sb("aacc", [128, DC], F32)
    aA = k.sb("aA", [128, DC], F32)
    art = k.sb("art", [128, 128], F32)
    ps8 = k.ps("ps8", [128, 8, 512])

    m_cst = k.sem("m_cst")
    m_xbf = k.sem("m_xbf")
    m_win = [k.sem("m_win0"), k.sem("m_win1")]
    m_cs = k.sem("m_cs")
    m_qst = [k.sem("m_qst0"), k.sem("m_qst1")]
    m_ain = k.sem("m_ain")
    m_aout = k.sem("m_aout")
    m_qld = k.sem("m_qld")
    m_wout = [k.sem("m_wout0"), k.sem("m_wout1")]
    m_str = [k.sem("m_str0"), k.sem("m_str1")]
    m_out = k.sem("m_out")

    s_qkv = k.sem("s_qkv")
    s_ev1 = k.sem("s_ev1")
    s_rp = k.sem("s_rp")
    s_kst = k.sem("s_kst")
    s_nrw = k.sem("s_nrw")
    s_sqr = k.sem("s_sqr")
    s_bc1 = k.sem("s_bc1")
    s_ap = k.sem("s_ap")
    s_tra = k.sem("s_tra")
    s_aev = k.sem("s_aev")
    s_red = k.sem("s_red")
    s_cc = k.sem("s_cc")
    s_aok = k.sem("s_aok")
    s_yb = k.sem("s_yb")
    s_op = k.sem("s_op")
    s_ev2 = k.sem("s_ev2")
    s_rs = k.sem("s_rs")
    s_sq = k.sem("s_sq")
    s_st2 = k.sem("s_st2")
    s_row = k.sem("s_row")
    s_bc = k.sem("s_bc")
    s_nrm = k.sem("s_nrm")

    io["_m_ain"] = m_ain
    io["_sccn"] = s_cc
    if "dbg_k" in io:
        io["_dbg"] = k.sem("m_dbg")
    with ExitStack() as s1:
        k1 = K(nc, block, s1)
        xbf = k1.sb("H_xbf", [128, DC, 512], BF16)
        winb = k1.sb("H_win", [128, 2, DC, 128], BF16)
        kfull = k1.sb("H_k", [128, DC, 512], F32)
        vstg = k1.sb("H_v", [128, DC, 512], F32)
        qstg = k1.sb("H_q", [128, DC, 512], F32)
        cosb = k1.sb("H_cos", [128, RH, 512], F32)
        sinb = k1.sb("H_sin", [128, RH, 512], F32)
        rt1 = k1.sb("H_rt1", [128, 512], F32)
        rt2 = k1.sb("H_rt2", [128, 512], F32)
        rt3 = k1.sb("H_rt3", [128, 512], F32)
        qbf_st = k1.sb("H_qbf", [128, 2, 512], BF16)
        rrow = k1.sb("H_rrow", [1, 512], F32)
        rbb = k1.sb("H_rbb", [128, 512], F32)
        scr1 = k1.sb("H_scr1", [128, 2, 512], F32)

        @block.gpsimd
        def _(g):
            for tc in range(NTCH):
                t0 = tc * 512
                if tc >= 1:
                    g.wait_ge(s_qkv.h, E3 * tc)
                g.dma_start(out=xbf[:, :, :],
                            in_=io["stream_in"].rearrange("(c p) t -> p c t", p=128)[:, :, t0:t0 + 512]
                            ).then_inc(m_xbf.h, 16)
                for et in range(E3):
                    gi = tc * E3 + et
                    if gi >= 2:
                        g.wait_ge(s_qkv.h, gi - 1)
                    g.dma_start(out=winb[:, et % 2, :, :],
                                in_=io["w_inT"].rearrange("(c p) o -> p c o", p=128)[:, :, et * 128:(et + 1) * 128]
                                ).then_inc(m_win[et % 2].h, 16)
            g.wait_ge(m_ain.h, 16)
            g.collective_compute("AllReduce", OP.add,
                                 replica_groups=[[0, 1], [2, 3], [4, 5], [6, 7]],
                                 ins=[io["cc_a_in"].ap().opt()],
                                 outs=[io["cc_a_out"].ap().opt()]).then_inc(s_cc.h, 1)
            s_cc.n += 1

        @block.sync
        def _(s):
            s.dma_start(out=binT[:, :], in_=io["binT"]).then_inc(m_cst.h, 16)
            s.dma_start(out=boutT[:, :], in_=io["boutT"]).then_inc(m_cst.h, 16)
            s.dma_start(out=identf[:, :], in_=io["identf"]).then_inc(m_cst.h, 16)
            for tc in range(NTCH):
                t0 = tc * 512
                if tc >= 1:
                    s.wait_ge(s_rp.h, RPT * tc)
                s.dma_start(out=cosb[:, :, :],
                            in_=io["cosT1"].rearrange("(c p) t -> p c t", p=128)[:, :, t0:t0 + 512]
                            ).then_inc(m_cs.h, 16)
                s.dma_start(out=sinb[:, :, :],
                            in_=io["sinT1"].rearrange("(c p) t -> p c t", p=128)[:, :, t0:t0 + 512]
                            ).then_inc(m_cs.h, 16)
                for qi in range(DC):
                    s.wait_ge(s_rp.h, RPT * tc + 2 * RH + qi + 1)
                    s.dma_start(out=io["q_scr"][qi * 128:(qi + 1) * 128, t0:t0 + 512],
                                in_=qbf_st[:, qi % 2, :]).then_inc(m_qst[qi % 2].h, 16)
                if tc == 0 and "dbg_k" in io:
                    s.wait_ge(s_ap.h, 1)
                    s.dma_start(out=io["dbg_k"], in_=kfull[:, :, :]).then_inc(io["_dbg"].h, 16)
                    s.dma_start(out=io["dbg_v"], in_=vstg[:, :, :]).then_inc(io["_dbg"].h, 16)
                    s.dma_start(out=io["dbg_r"], in_=rbb[:, :]).then_inc(io["_dbg"].h, 16)
                    s.dma_start(out=io["dbg_a0"], in_=aacc[:, :]).then_inc(io["_dbg"].h, 16)
                    io["_dbg"].n += 64
            s.wait_ge(s_ap.h, NTCH)
            s.dma_start(out=io["cc_a_in"].ap(), in_=aacc[:, :]).then_inc(m_ain.h, 16)
            s.wait_ge(s_cc.h, 1)
            s.dma_start(out=aA[:, :], in_=io["cc_a_out"].ap()).then_inc(m_aout.h, 16)

        @block.tensor
        def _(t):
            t.wait_ge(m_cst.h, 48)
            for tc in range(NTCH):
                t.wait_ge(m_xbf.h, 16 * (tc + 1))
                for et in range(E3):
                    gi = tc * E3 + et
                    t.wait_ge(m_win[et % 2].h, 16 * (gi // 2 + 1))
                    if gi >= 4:
                        t.wait_ge(s_ev1.h, gi - 3)
                    for kc in range(DC):
                        mm = t.matmul(ps8[:, gi % 4, :], winb[:, et % 2, kc, :],
                                      xbf[:, kc, :], start=(kc == 0), stop=(kc == DC - 1))
                    mm.then_inc(s_qkv.h, 1)
                    s_qkv.n += 1
                for kc in range(DC):
                    t.wait_ge(s_sqr.h, (DC + 1) * tc + kc + 1)
                    if kc == 0 and tc >= 1:
                        t.wait_ge(s_nrw.h, 2 * tc)
                    mm = t.matmul(ps8[0:1, 6, :], ones_col_f[:, :], scr1[:, kc % 2, :],
                                  start=(kc == 0), stop=(kc == DC - 1))
                    mm.then_inc(s_kst.h, 1)
                    s_kst.n += 1
                t.wait_ge(s_nrw.h, 2 * tc + 2)
                mm = t.matmul(ps8[:, 7, :], ones_row_f[:, :], rrow[:, :],
                              start=True, stop=True)
                mm.then_inc(s_bc1.h, 1)
                s_bc1.n += 1

        @block.scalar
        def _(sc):
            sc.wait_ge(m_cst.h, 48)
            for tc in range(NTCH):
                for et in range(E3):
                    gi = tc * E3 + et
                    sc.wait_ge(s_qkv.h, gi + 1)
                    if et == 0 and tc >= 1:
                        sc.wait_ge(s_ap.h, tc)
                    bias = binT[:, et:et + 1]
                    if et < DC:
                        dst = qstg[:, et, :]
                    elif et < 2 * DC:
                        dst = kfull[:, et - DC, :]
                    else:
                        dst = vstg[:, et - 2 * DC, :]
                    op = sc.activation(out=dst, in_=ps8[:, gi % 4, :],
                                       func=AF.Identity, bias=bias)
                    op.then_inc(s_ev1.h, 1)
                    s_ev1.n += 1
                for kc in range(DC):
                    if kc < 2 * RH:
                        sc.wait_ge(s_rp.h, RPT * tc + RH)
                    if kc >= 2:
                        sc.wait_ge(s_kst.h, DC * tc + kc - 1)
                    op = sc.activation(out=scr1[:, kc % 2, :], in_=kfull[:, kc, :],
                                       func=AF.Square)
                    op.then_inc(s_sqr.h, 1)
                    s_sqr.n += 1
                sc.wait_ge(s_nrw.h, 2 * tc + 1)
                op = sc.activation(out=rrow[:, :], in_=rrow[:, :], func=AF.Sqrt)
                op.then_inc(s_sqr.h, 1)
                s_sqr.n += 1

        @block.vector
        def _(v):
            v.memset(ones_row_f[:, :], 1.0)
            v.memset(ones_col_f[:, :], 1.0)
            v.memset(aacc[:, :], 0.0)
            for tc in range(NTCH):
                v.wait_ge(m_cs.h, 32 * (tc + 1))
                for cp in range(RH):
                    v.wait_ge(s_ev1.h, tc * E3 + DC + RH + cp + 1)
                    lo = kfull[:, cp, :]
                    hi = kfull[:, cp + RH, :]
                    v.tensor_tensor(out=rt1[:, :], in0=lo, in1=cosb[:, cp, :], op=OP.mult)
                    v.tensor_tensor(out=rt2[:, :], in0=hi, in1=sinb[:, cp, :], op=OP.mult)
                    v.tensor_tensor(out=rt3[:, :], in0=rt1[:, :], in1=rt2[:, :], op=OP.subtract)
                    v.tensor_tensor(out=rt1[:, :], in0=hi, in1=cosb[:, cp, :], op=OP.mult)
                    v.tensor_tensor(out=rt2[:, :], in0=lo, in1=sinb[:, cp, :], op=OP.mult)
                    v.tensor_tensor(out=hi, in0=rt1[:, :], in1=rt2[:, :], op=OP.add)
                    op = v.tensor_copy(lo, rt3[:, :])
                    op.then_inc(s_rp.h, 1)
                    s_rp.n += 1
                for cp in range(RH):
                    v.wait_ge(s_ev1.h, tc * E3 + RH + cp + 1)
                    lo = qstg[:, cp, :]
                    hi = qstg[:, cp + RH, :]
                    v.tensor_tensor(out=rt1[:, :], in0=lo, in1=cosb[:, cp, :], op=OP.mult)
                    v.tensor_tensor(out=rt2[:, :], in0=hi, in1=sinb[:, cp, :], op=OP.mult)
                    v.tensor_tensor(out=rt3[:, :], in0=rt1[:, :], in1=rt2[:, :], op=OP.subtract)
                    v.tensor_tensor(out=rt1[:, :], in0=hi, in1=cosb[:, cp, :], op=OP.mult)
                    v.tensor_tensor(out=rt2[:, :], in0=lo, in1=sinb[:, cp, :], op=OP.mult)
                    v.tensor_tensor(out=hi, in0=rt1[:, :], in1=rt2[:, :], op=OP.add)
                    op = v.tensor_copy(lo, rt3[:, :])
                    op.then_inc(s_rp.h, 1)
                    s_rp.n += 1
                for qi in range(DC):
                    gq_i = tc * DC + qi
                    if gq_i >= 2:
                        v.wait_ge(m_qst[gq_i % 2].h, 16 * (gq_i // 2))
                    if qi >= 2 * RH:
                        v.wait_ge(s_ev1.h, tc * E3 + qi + 1)
                    op = v.tensor_copy(qbf_st[:, qi % 2, :], qstg[:, qi, :])
                    op.then_inc(s_rp.h, 1)
                    s_rp.n += 1
                v.wait_ge(s_kst.h, DC * (tc + 1))
                op = v.tensor_copy(rrow[:, :], ps8[0:1, 6, :])
                op.then_inc(s_nrw.h, 1)
                s_nrw.n += 1
                v.wait_ge(s_sqr.h, (DC + 1) * (tc + 1))
                v.tensor_scalar(out=rrow[:, :], in0=rrow[:, :],
                                scalar1=1e-12, scalar2=None, op0=OP.max)
                op = v.reciprocal(rrow[:, :], rrow[:, :])
                op.then_inc(s_nrw.h, 1)
                s_nrw.n += 1
                v.wait_ge(s_bc1.h, tc + 1)
                v.tensor_copy(rbb[:, :], ps8[:, 7, :])
                v.wait_ge(s_ev1.h, (tc + 1) * E3)
                for kc in range(DC):
                    v.tensor_tensor(out=rt1[:, :], in0=kfull[:, kc, :],
                                    in1=vstg[:, kc, :], op=OP.mult)
                    v.tensor_tensor(out=rt2[:, :], in0=rt1[:, :], in1=rbb[:, :], op=OP.mult)
                    op = v.reduce_sum(rt3[:, 0:1], rt2[:, :], axis=AX.X)
                    op.then_inc(s_red.h, 1)
                    s_red.n += 1
                    v.wait_ge(s_red.h, s_red.n)
                    op = v.tensor_tensor(out=aacc[:, kc:kc + 1], in0=aacc[:, kc:kc + 1],
                                         in1=rt3[:, 0:1], op=OP.add)
                op.then_inc(s_ap.h, 1)
                s_ap.n += 1
            v.wait_ge(m_aout.h, 16)
            op = v.tensor_copy(rt1[:, 0:1], aA[:, 0:1])
            op.then_inc(s_aok.h, 1)
            s_aok.n += 1

    with ExitStack() as s2:
        k2 = K(nc, block, s2)
        qbf = k2.sb("I_qbf", [128, DC, 512], BF16)
        ybf = k2.sb("I_ybf", [128, DC, 512], BF16)
        woutb = k2.sb("I_wout", [128, 2, DC, 128], BF16)
        sD = k2.sb("I_sD", [128, DC, 512], F32)
        strsl = k2.sb("I_strsl", [128, 2, 512], F32)
        evst = k2.sb("I_evst", [128, 2, 512], F32)
        scr2 = k2.sb("I_scr2", [128, 2, 512], F32)
        rowbuf = k2.sb("I_rows", [1, 8, 512], F32)
        rb2 = k2.sb("I_rb2", [128, 2, 512], F32)

        @block.gpsimd
        def _(g):
            g.wait_ge(s_aok.h, 1)
            for tc in range(NTCH):
                for ot in range(DC):
                    gi = tc * DC + ot
                    if gi >= 2:
                        g.wait_ge(s_op.h, gi - 1)
                    g.dma_start(out=woutb[:, ot % 2, :, :],
                                in_=io["w_outT"].rearrange("(c p) o -> p c o", p=128)[:, :, ot * 128:(ot + 1) * 128]
                                ).then_inc(m_wout[ot % 2].h, 16)

        @block.sync
        def _(s):
            s.wait_ge(s_aok.h, 1)
            for tc in range(NTCH):
                t0 = tc * 512
                if tc >= 1:
                    s.wait_ge(s_yb.h, DC * tc)
                s.dma_start(out=qbf[:, :, :],
                            in_=io["q_scr"].rearrange("(c p) t -> p c t", p=128)[:, :, t0:t0 + 512]
                            ).then_inc(m_qld.h, 16)
                for ot in range(DC):
                    gi = tc * DC + ot
                    if gi >= 2:
                        s.wait_ge(s_rs.h, gi - 1)
                    s.dma_start(out=strsl[:, gi % 2, :],
                                in_=io["stream_in"].rearrange("(c p) t -> p c t", p=128)[:, ot, t0:t0 + 512]
                                ).then_inc(m_str[gi % 2].h, 16)
                s.wait_ge(s_nrm.h, tc + 1)
                s.dma_start(out=io["out"].rearrange("(c p) t -> p c t", p=128)[:, :, t0:t0 + 512],
                            in_=sD[:, :, :]).then_inc(m_out.h, 16)
            s.wait_ge(m_out.h, 16 * NTCH)

        @block.tensor
        def _(t):
            for tc in range(NTCH):
                for ot in range(DC):
                    gi = tc * DC + ot
                    t.wait_ge(m_wout[ot % 2].h, 16 * (gi // 2 + 1))
                    t.wait_ge(s_yb.h, DC * tc + DC)
                    if gi >= 2:
                        t.wait_ge(s_ev2.h, gi - 1)
                    for dc in range(DC):
                        mm = t.matmul(ps8[:, 4 + gi % 2, :], woutb[:, ot % 2, dc, :],
                                      ybf[:, dc, :], start=(dc == 0), stop=(dc == DC - 1))
                    mm.then_inc(s_op.h, 1)
                    s_op.n += 1
                for ot in range(DC):
                    t.wait_ge(s_sq.h, (DC + 1) * tc + ot + 1)
                    if ot == 0 and tc >= 1:
                        t.wait_ge(s_row.h, 2 * tc - 1)
                    mm = t.matmul(ps8[0:1, 0, :], ones_col_f[:, :], sD[:, ot, :],
                                  start=(ot == 0), stop=(ot == DC - 1))
                    mm2 = t.matmul(ps8[0:1, 6, :], ones_col_f[:, :], scr2[:, ot % 2, :],
                                   start=(ot == 0), stop=(ot == DC - 1))
                    mm2.then_inc(s_st2.h, 1)
                    s_st2.n += 1
                t.wait_ge(s_row.h, 2 * tc + 2)
                if tc >= 1:
                    t.wait_ge(s_nrm.h, tc)
                mm = t.matmul(ps8[:, 7, :], ones_row_f[:, :], rowbuf[:, 4, :],
                              start=True, stop=True)
                mm2 = t.matmul(ps8[:, 1, :], ones_row_f[:, :], rowbuf[:, 5, :],
                               start=True, stop=True)
                mm2.then_inc(s_bc.h, 1)
                s_bc.n += 1

        @block.scalar
        def _(sc):
            for tc in range(NTCH):
                for ot in range(DC):
                    gi = tc * DC + ot
                    sc.wait_ge(s_op.h, gi + 1)
                    if gi >= 2:
                        sc.wait_ge(s_rs.h, gi - 1)
                    op = sc.activation(out=evst[:, gi % 2, :], in_=ps8[:, 4 + gi % 2, :],
                                       func=AF.Identity, bias=boutT[:, ot:ot + 1])
                    op.then_inc(s_ev2.h, 1)
                    s_ev2.n += 1
                for ot in range(DC):
                    gi = tc * DC + ot
                    sc.wait_ge(s_rs.h, gi + 1)
                    if ot >= 2:
                        sc.wait_ge(s_st2.h, DC * tc + ot - 1)
                    op = sc.activation(out=scr2[:, ot % 2, :], in_=sD[:, ot, :],
                                       func=AF.Square)
                    op.then_inc(s_sq.h, 1)
                    s_sq.n += 1
                sc.wait_ge(s_row.h, 2 * tc + 1)
                op = sc.activation(out=rowbuf[:, 6, :], in_=rowbuf[:, 1, :], func=AF.Sqrt)
                op.then_inc(s_sq.h, 1)
                s_sq.n += 1

        @block.vector
        def _(v):
            for tc in range(NTCH):
                v.wait_ge(m_qld.h, 16 * (tc + 1))
                for dc in range(DC):
                    if tc >= 1 and dc == 0:
                        v.wait_ge(s_op.h, DC * tc)
                    op = v.tensor_scalar(out=ybf[:, dc, :], in0=qbf[:, dc, :],
                                         scalar1=aA[:, dc:dc + 1], scalar2=None,
                                         op0=OP.mult)
                    op.then_inc(s_yb.h, 1)
                    s_yb.n += 1
                for ot in range(DC):
                    gi = tc * DC + ot
                    v.wait_ge(s_ev2.h, gi + 1)
                    v.wait_ge(m_str[gi % 2].h, 16 * (gi // 2 + 1))
                    if ot == 0:
                        v.wait_ge(m_out.h, 16 * tc)
                    op = v.scalar_tensor_tensor(out=sD[:, ot, :], in0=strsl[:, gi % 2, :],
                                                scalar=2.0, in1=evst[:, gi % 2, :],
                                                op0=OP.mult, op1=OP.add)
                    op.then_inc(s_rs.h, 1)
                    s_rs.n += 1
                v.wait_ge(s_st2.h, DC * (tc + 1))
                v.tensor_scalar(out=rowbuf[:, 2, :], in0=ps8[0:1, 0, :],
                                scalar1=1.0 / D, scalar2=None, op0=OP.mult)
                v.tensor_scalar(out=rowbuf[:, 3, :], in0=ps8[0:1, 6, :],
                                scalar1=1.0 / D, scalar2=None, op0=OP.mult)
                v.tensor_tensor(out=rowbuf[:, 0, :], in0=rowbuf[:, 2, :],
                                in1=rowbuf[:, 2, :], op=OP.mult)
                v.tensor_tensor(out=rowbuf[:, 3, :], in0=rowbuf[:, 3, :],
                                in1=rowbuf[:, 0, :], op=OP.subtract)
                op = v.tensor_scalar(out=rowbuf[:, 1, :], in0=rowbuf[:, 3, :],
                                     scalar1=1e-5, scalar2=None, op0=OP.add)
                op.then_inc(s_row.h, 1)
                s_row.n += 1
                v.wait_ge(s_sq.h, (DC + 1) * (tc + 1))
                v.reciprocal(rowbuf[:, 4, :], rowbuf[:, 6, :])
                op = v.tensor_tensor(out=rowbuf[:, 5, :], in0=rowbuf[:, 2, :],
                                     in1=rowbuf[:, 4, :], op=OP.mult)
                op.then_inc(s_row.h, 1)
                s_row.n += 1
                v.wait_ge(s_bc.h, tc + 1)
                v.tensor_copy(rb2[:, 0, :], ps8[:, 7, :])
                v.tensor_copy(rb2[:, 1, :], ps8[:, 1, :])
                for ot in range(DC):
                    v.tensor_tensor(out=scr2[:, ot % 2, :], in0=sD[:, ot, :],
                                    in1=rb2[:, 0, :], op=OP.mult)
                    op = v.tensor_tensor(out=sD[:, ot, :], in0=scr2[:, ot % 2, :],
                                         in1=rb2[:, 1, :], op=OP.subtract)
                op.then_inc(s_nrm.h, 1)
                s_nrm.n += 1


def qkv_perm(D):
    rot = D // 2
    p = np.arange(D)
    half = rot // 2
    p[:half] = np.arange(0, rot, 2)
    p[half:rot] = np.arange(1, rot, 2)
    return p


def build_l1attn(cfg):
    c = derive(cfg)
    D, TH, DC = c["D"], c["TH"], c["DC"]
    E = 3 * D
    nc = bass.Bass("TRN2", target_bir_lowering=False, num_devices=8,
                   detect_race_conditions=False)
    io = {}
    io["stream_in"] = nc.dram_tensor("stream_in", [D, TH], F32, kind="ExternalInput").ap()
    io["w_inT"] = nc.dram_tensor("w_inT", [D, E], F32, kind="ExternalInput").ap()
    io["binT"] = nc.dram_tensor("binT", [128, 3 * DC], F32, kind="ExternalInput").ap()
    io["w_outT"] = nc.dram_tensor("w_outT", [D, D], F32, kind="ExternalInput").ap()
    io["boutT"] = nc.dram_tensor("boutT", [128, DC], F32, kind="ExternalInput").ap()
    io["cosT1"] = nc.dram_tensor("cosT1", [D // 4, TH], F32, kind="ExternalInput").ap()
    io["sinT1"] = nc.dram_tensor("sinT1", [D // 4, TH], F32, kind="ExternalInput").ap()
    io["identf"] = nc.dram_tensor("identf", [128, 128], F32, kind="ExternalInput").ap()
    io["out"] = nc.dram_tensor("stream_out", [D, TH], F32, kind="ExternalOutput").ap()
    io["q_scr"] = nc.dram_tensor("q_scr", [D, TH], BF16).ap()
    io["cc_a_in"] = nc.dram_tensor("cc_a_in", [128, DC], F32)
    io["cc_a_out"] = nc.dram_tensor("cc_a_out", [128, DC], F32)
    with ExitStack() as stack:
        block = stack.enter_context(nc.Block())
        k = K(nc, block, stack)
        emit_l1attn(k, c, io)
    return nc


def prep_l1attn_inputs(cfg, inp, streams):
    c = derive(cfg)
    D, TH, DC = c["D"], c["TH"], c["DC"]
    rot = D // 2
    W_in = np.asarray(inp["L1_attn_W_in"], dtype=np.float32)
    b_in = np.asarray(inp["L1_attn_b_in"], dtype=np.float32)
    W_out = np.asarray(inp["L1_attn_W_out"], dtype=np.float32)
    b_out = np.asarray(inp["L1_attn_b_out"], dtype=np.float32)
    p = qkv_perm(D)
    permE = np.concatenate([p, D + p, 2 * D + p])
    w_inT = np.ascontiguousarray(W_in.T[:, permE])
    binP = b_in[permE]
    binT = np.ascontiguousarray(binP.reshape(3 * DC, 128).T)
    w_outT = np.ascontiguousarray(W_out.T[p, :])
    boutT = np.ascontiguousarray(b_out.reshape(DC, 128).T)
    identf = np.eye(128, dtype=np.float32)
    half = rot // 2
    inv = (10000.0 ** (-2.0 * np.arange(half) / rot))
    maps = []
    for core in range(8):
        b, g = core // 2, core % 2
        pos = np.arange(g * TH, (g + 1) * TH, dtype=np.float64)
        fr = pos[None, :] * inv[:, None]
        maps.append(dict(stream_in=streams[core], w_inT=w_inT, binT=binT,
                         w_outT=w_outT, boutT=boutT,
                         cosT1=np.cos(fr).astype(np.float32),
                         sinT1=np.sin(fr).astype(np.float32),
                         identf=identf))
    return maps
def emit_l1ffn(k, c, io):
    nc, block = k.nc, k.block
    D, DC = c["D"], c["DC"]
    TH = c["TH"]
    I1 = c["I1"]
    NG = I1 // 128
    NTCH = TH // 512

    ones_row_f = k.sb("ones_row_f", [1, 128], F32)
    ones_col_f = k.sb("ones_col_f", [128, 1], F32)
    ps8 = k.ps("ps8", [128, 8, 512])

    m_xbf = k.sem("m_xbf")
    m_wgu = [k.sem("m_wgu0"), k.sem("m_wgu1")]
    m_wdn = [k.sem("m_wdn0"), k.sem("m_wdn1")]
    m_str = [k.sem("m_str0"), k.sem("m_str1")]
    m_out = k.sem("m_out")

    s_gu = k.sem("s_gu")
    s_si = k.sem("s_si")
    s_hb = k.sem("s_hb")
    s_dn = k.sem("s_dn")
    s_ye = k.sem("s_ye")
    s_rs = k.sem("s_rs")
    s_sq = k.sem("s_sq")
    s_st2 = k.sem("s_st2")
    s_row = k.sem("s_row")
    s_bc = k.sem("s_bc")
    s_nrm = k.sem("s_nrm")

    with ExitStack() as sf2:
        kf2 = K(nc, block, sf2)
        xbf = kf2.sb("J_xbf", [128, DC, 512], BF16)
        wgu = kf2.sb("J_wgu", [128, 2, 2, DC, 128], BF16)
        wdn = kf2.sb("J_wdn", [128, 2, NG, 128], BF16)
        hbuf = kf2.sb("J_hbuf", [128, NG, 512], BF16)
        gsil = kf2.sb("J_gsil", [128, 2, 512], BF16)
        ybuf = kf2.sb("J_ybuf", [128, DC, 512], F32)
        strsl = kf2.sb("J_strsl", [128, 2, 512], F32)
        scr2 = kf2.sb("J_scr2", [128, 2, 512], F32)
        rowbuf = kf2.sb("J_rows", [1, 8, 512], F32)
        rb2 = kf2.sb("J_rb2", [128, 2, 512], F32)
        atmp = kf2.sb("J_atmp", [128, 512], F32)

        @block.gpsimd
        def _(g):
            for tc in range(NTCH):
                t0 = tc * 512
                if tc >= 1:
                    g.wait_ge(s_gu.h, NG * tc)
                g.dma_start(out=xbf[:, :, :],
                            in_=io["stream_in"].rearrange("(c p) t -> p c t", p=128)[:, :, t0:t0 + 512]
                            ).then_inc(m_xbf.h, 16)
                for it in range(NG):
                    gi = tc * NG + it
                    if gi >= 2:
                        g.wait_ge(s_gu.h, gi - 1)
                    g.dma_start(out=wgu[:, it % 2, 0, :, :],
                                in_=io["w_guT"].rearrange("(c p) o -> p c o", p=128)[:, :, it * 128:(it + 1) * 128]
                                ).then_inc(m_wgu[it % 2].h, 16)
                    g.dma_start(out=wgu[:, it % 2, 1, :, :],
                                in_=io["w_guT"].rearrange("(c p) o -> p c o", p=128)[:, :, I1 + it * 128:I1 + (it + 1) * 128]
                                ).then_inc(m_wgu[it % 2].h, 16)
                for ho in range(DC):
                    g2 = tc * DC + ho
                    if g2 >= 2:
                        g.wait_ge(s_dn.h, g2 - 1)
                    g.dma_start(out=wdn[:, ho % 2, :, :],
                                in_=io["w_dnT"].rearrange("(c p) o -> p c o", p=128)[:, :, ho * 128:(ho + 1) * 128]
                                ).then_inc(m_wdn[ho % 2].h, 16)

        @block.sync
        def _(s):
            for tc in range(NTCH):
                t0 = tc * 512
                for ho in range(DC):
                    g2 = tc * DC + ho
                    if g2 >= 2:
                        s.wait_ge(s_ye.h, g2 - 1)
                    s.dma_start(out=strsl[:, g2 % 2, :],
                                in_=io["stream_in"].rearrange("(c p) t -> p c t", p=128)[:, ho, t0:t0 + 512]
                                ).then_inc(m_str[g2 % 2].h, 16)
                s.wait_ge(s_nrm.h, tc + 1)
                s.dma_start(out=io["out"].rearrange("(c p) t -> p c t", p=128)[:, :, t0:t0 + 512],
                            in_=ybuf[:, :, :]).then_inc(m_out.h, 16)
            s.wait_ge(m_out.h, 16 * NTCH)

        @block.tensor
        def _(t):
            for tc in range(NTCH):
                t.wait_ge(m_xbf.h, 16 * (tc + 1))
                for it in range(NG):
                    gi = tc * NG + it
                    t.wait_ge(m_wgu[it % 2].h, 32 * (gi // 2 + 1))
                    if gi >= 2:
                        t.wait_ge(s_hb.h, gi - 1)
                    if it == 0 and tc >= 1:
                        t.wait_ge(s_nrm.h, tc)
                    for kc in range(DC):
                        mm = t.matmul(ps8[:, 2 * (gi % 2), :], wgu[:, it % 2, 0, kc, :],
                                      xbf[:, kc, :], start=(kc == 0), stop=(kc == DC - 1))
                    for kc in range(DC):
                        mm2 = t.matmul(ps8[:, 2 * (gi % 2) + 1, :], wgu[:, it % 2, 1, kc, :],
                                       xbf[:, kc, :], start=(kc == 0), stop=(kc == DC - 1))
                    mm2.then_inc(s_gu.h, 1)
                    s_gu.n += 1
                for ho in range(DC):
                    g2 = tc * DC + ho
                    t.wait_ge(m_wdn[ho % 2].h, 16 * (g2 // 2 + 1))
                    if g2 >= 2:
                        t.wait_ge(s_ye.h, g2 - 1)
                    for ic in range(NG):
                        t.wait_ge(s_hb.h, tc * NG + ic + 1)
                        mm = t.matmul(ps8[:, 4 + g2 % 2, :], wdn[:, ho % 2, ic, :],
                                      hbuf[:, ic, :], start=(ic == 0), stop=(ic == NG - 1))
                    mm.then_inc(s_dn.h, 1)
                    s_dn.n += 1
                for ho in range(DC):
                    t.wait_ge(s_sq.h, (DC + 1) * tc + ho + 1)
                    if ho == 0 and tc >= 1:
                        t.wait_ge(s_row.h, 2 * tc - 1)
                    mm = t.matmul(ps8[0:1, 0, :], ones_col_f[:, :], ybuf[:, ho, :],
                                  start=(ho == 0), stop=(ho == DC - 1))
                    mm2 = t.matmul(ps8[0:1, 6, :], ones_col_f[:, :], scr2[:, ho % 2, :],
                                   start=(ho == 0), stop=(ho == DC - 1))
                    mm2.then_inc(s_st2.h, 1)
                    s_st2.n += 1
                t.wait_ge(s_row.h, 2 * tc + 2)
                if tc >= 1:
                    t.wait_ge(s_nrm.h, tc)
                mm = t.matmul(ps8[:, 7, :], ones_row_f[:, :], rowbuf[:, 4, :],
                              start=True, stop=True)
                mm2 = t.matmul(ps8[:, 1, :], ones_row_f[:, :], rowbuf[:, 5, :],
                               start=True, stop=True)
                mm2.then_inc(s_bc.h, 1)
                s_bc.n += 1

        @block.scalar
        def _(sc):
            for tc in range(NTCH):
                for it in range(NG):
                    gi = tc * NG + it
                    sc.wait_ge(s_gu.h, gi + 1)
                    if gi >= 2:
                        sc.wait_ge(s_hb.h, gi - 1)
                    op = sc.activation(out=gsil[:, gi % 2, :], in_=ps8[:, 2 * (gi % 2), :],
                                       func=AF.Silu)
                    op.then_inc(s_si.h, 1)
                    s_si.n += 1
                for ho in range(DC):
                    g2 = tc * DC + ho
                    sc.wait_ge(s_ye.h, g2 + 1)
                    if ho >= 2:
                        sc.wait_ge(s_st2.h, DC * tc + ho - 1)
                    op = sc.activation(out=scr2[:, ho % 2, :], in_=ybuf[:, ho, :],
                                       func=AF.Square)
                    op.then_inc(s_sq.h, 1)
                    s_sq.n += 1
                sc.wait_ge(s_row.h, 2 * tc + 1)
                op = sc.activation(out=rowbuf[:, 6, :], in_=rowbuf[:, 1, :], func=AF.Sqrt)
                op.then_inc(s_sq.h, 1)
                s_sq.n += 1

        @block.vector
        def _(v):
            v.memset(ones_row_f[:, :], 1.0)
            v.memset(ones_col_f[:, :], 1.0)
            for tc in range(NTCH):
                for it in range(NG):
                    gi = tc * NG + it
                    v.wait_ge(s_si.h, gi + 1)
                    if it == 0 and tc >= 1:
                        v.wait_ge(s_dn.h, DC * tc)
                    op = v.tensor_tensor(out=hbuf[:, it, :], in0=gsil[:, gi % 2, :],
                                         in1=ps8[:, 2 * (gi % 2) + 1, :], op=OP.mult)
                    op.then_inc(s_hb.h, 1)
                    s_hb.n += 1
                for ho in range(DC):
                    g2 = tc * DC + ho
                    v.wait_ge(s_dn.h, g2 + 1)
                    v.wait_ge(m_str[g2 % 2].h, 16 * (g2 // 2 + 1))
                    if ho == 0:
                        v.wait_ge(m_out.h, 16 * tc)
                    op = v.tensor_tensor(out=ybuf[:, ho, :], in0=ps8[:, 4 + g2 % 2, :],
                                         in1=strsl[:, g2 % 2, :], op=OP.add)
                    op.then_inc(s_ye.h, 1)
                    s_ye.n += 1
                v.wait_ge(s_st2.h, DC * (tc + 1))
                v.tensor_scalar(out=rowbuf[:, 2, :], in0=ps8[0:1, 0, :],
                                scalar1=1.0 / D, scalar2=None, op0=OP.mult)
                v.tensor_scalar(out=rowbuf[:, 3, :], in0=ps8[0:1, 6, :],
                                scalar1=1.0 / D, scalar2=None, op0=OP.mult)
                v.tensor_tensor(out=rowbuf[:, 0, :], in0=rowbuf[:, 2, :],
                                in1=rowbuf[:, 2, :], op=OP.mult)
                v.tensor_tensor(out=rowbuf[:, 3, :], in0=rowbuf[:, 3, :],
                                in1=rowbuf[:, 0, :], op=OP.subtract)
                op = v.tensor_scalar(out=rowbuf[:, 1, :], in0=rowbuf[:, 3, :],
                                     scalar1=1e-5, scalar2=None, op0=OP.add)
                op.then_inc(s_row.h, 1)
                s_row.n += 1
                v.wait_ge(s_sq.h, (DC + 1) * (tc + 1))
                v.reciprocal(rowbuf[:, 4, :], rowbuf[:, 6, :])
                op = v.tensor_tensor(out=rowbuf[:, 5, :], in0=rowbuf[:, 2, :],
                                     in1=rowbuf[:, 4, :], op=OP.mult)
                op.then_inc(s_row.h, 1)
                s_row.n += 1
                v.wait_ge(s_bc.h, tc + 1)
                v.tensor_copy(rb2[:, 0, :], ps8[:, 7, :])
                v.tensor_copy(rb2[:, 1, :], ps8[:, 1, :])
                for ho in range(DC):
                    v.tensor_tensor(out=atmp[:, :], in0=ybuf[:, ho, :],
                                    in1=rb2[:, 0, :], op=OP.mult)
                    op = v.tensor_tensor(out=ybuf[:, ho, :], in0=atmp[:, :],
                                         in1=rb2[:, 1, :], op=OP.subtract)
                op.then_inc(s_nrm.h, 1)
                s_nrm.n += 1


def build_l1ffn(cfg):
    c = derive(cfg)
    D, TH, I1 = c["D"], c["TH"], c["I1"]
    nc = bass.Bass("TRN2", target_bir_lowering=False, num_devices=8,
                   detect_race_conditions=False)
    io = {}
    io["stream_in"] = nc.dram_tensor("stream_in", [D, TH], F32, kind="ExternalInput").ap()
    io["w_guT"] = nc.dram_tensor("w_guT", [D, 2 * I1], F32, kind="ExternalInput").ap()
    io["w_dnT"] = nc.dram_tensor("w_dnT", [I1, D], F32, kind="ExternalInput").ap()
    io["out"] = nc.dram_tensor("stream_out", [D, TH], F32, kind="ExternalOutput").ap()
    with ExitStack() as stack:
        block = stack.enter_context(nc.Block())
        k = K(nc, block, stack)
        emit_l1ffn(k, c, io)
    return nc


def prep_l1ffn_inputs(cfg, inp, streams):
    w_guT = np.ascontiguousarray(np.asarray(inp["L1_ffn_w_gate_up"], dtype=np.float32).T)
    w_dnT = np.ascontiguousarray(np.asarray(inp["L1_ffn_w_down"], dtype=np.float32).T)
    maps = []
    for core in range(8):
        maps.append(dict(stream_in=streams[core], w_guT=w_guT, w_dnT=w_dnT))
    return maps


_PFX = [""]


class GatedBlock:
    def __init__(self, block, gate):
        self._b = block
        self._gate = gate

    def _wrap(self, dec):
        gate = self._gate

        def deco(fn):
            def wrapped(eng):
                if gate is not None:
                    eng.wait_ge(gate[0], gate[1])
                fn(eng)
            return dec(wrapped)
        return deco

    @property
    def gpsimd(self):
        return self._wrap(self._b.gpsimd)

    @property
    def sync(self):
        return self._wrap(self._b.sync)

    @property
    def tensor(self):
        return self._wrap(self._b.tensor)

    @property
    def vector(self):
        return self._wrap(self._b.vector)

    @property
    def scalar(self):
        return self._wrap(self._b.scalar)


def _mangle(name):
    return _PFX[0] + name


_orig_sb = K.sb
_orig_ps = K.ps
_orig_sem = K.sem


def _sb(self, name, shape, dtype):
    return _orig_sb(self, _mangle(name), shape, dtype)


def _ps(self, name, shape, dtype=F32):
    return _orig_ps(self, _mangle(name), shape, dtype)


def _sem(self, name):
    return _orig_sem(self, _mangle(name))


K.sb = _sb
K.ps = _ps
K.sem = _sem


def build_fused(cfg):
    c = derive(cfg)
    L, D, DC, NHC, NJ, TH = c["L"], c["D"], c["DC"], c["NHC"], c["NJ"], c["TH"]
    I0, I1 = c["I0"], c["I1"]
    HD = NHC * 128
    NG0 = I0 // 128
    NHW = max(1, D // 512)
    HB = min(D, 512)
    E = 3 * D
    nc = bass.Bass("TRN2", target_bir_lowering=False, num_devices=8,
                   detect_race_conditions=False)
    T = lambda n, s, d, **kw: nc.dram_tensor(n, s, d, **kw)
    io1 = {}
    io1["xT"] = T("xT", [D, L], F32, kind="ExternalInput").ap()
    io1["xTh"] = T("xTh", [D, TH], F32, kind="ExternalInput").ap()
    io1["wq"] = T("wq", [D, NJ * 128], F32, kind="ExternalInput").ap()
    io1["wk"] = T("wk", [D, HD], F32, kind="ExternalInput").ap()
    io1["wv"] = T("wv", [D, HD], F32, kind="ExternalInput").ap()
    io1["wl"] = T("wl", [D, NHC], F32, kind="ExternalInput").ap()
    io1["bl"] = T("bl", [NHC], F32, kind="ExternalInput").ap()
    io1["wo"] = T("wo", [HD, D], F32, kind="ExternalInput").ap()
    io1["cosk"] = T("cosk", [128, L], F32, kind="ExternalInput").ap()
    io1["sink"] = T("sink", [128, L], F32, kind="ExternalInput").ap()
    io1["masks"] = T("masks", [4, 128, 512], BF16, kind="ExternalInput").ap()
    io1["q_scr"] = T("f1_q_scr", [NJ * 128, L], BF16).ap()
    io1["lam_scr"] = T("f1_lam_scr", [NHC, L], F32).ap()
    io1["cc_in"] = T("f1_cc_in", [2, D, TH], F32)
    io1["cc_out"] = T("f1_cc_out", [D, TH], F32)
    s1 = T("f_s1", [D, TH], F32).ap()
    s2 = T("f_s2", [D, TH], F32).ap()
    s3 = T("f_s3", [D, TH], F32).ap()
    io1["out"] = s1
    NSH = c["NSH"]
    NGU_loc = 2 * NG0 // NSH
    NDN_loc = DC // NSH
    io2 = {}
    io2["stream_in"] = s1
    io2["gup"] = T("gup", [NGU_loc * 128, D // 8], I32, kind="ExternalInput").ap()
    io2["gusT"] = T("gusT", [NGU_loc * 128, D // 128], F16, kind="ExternalInput").ap()
    io2["dnp"] = T("dnp", [NDN_loc * 128, I0 // 8], I32, kind="ExternalInput").ap()
    io2["dnsT"] = T("dnsT", [NDN_loc * 128, I0 // 128], F16, kind="ExternalInput").ap()
    io2["ident"] = T("ident", [128, 128], BF16, kind="ExternalInput").ap()
    io2["wgu_sh"] = T("f2_wgu_sh", [NGU_loc, DC, 128, 128], BF16).ap()
    io2["wdn_sh"] = T("f2_wdn_sh", [NDN_loc, NG0, 128, 128], BF16).ap()
    io2["wguT_scr"] = T("f2_wguT", [2 * NG0, DC, 128, 128], BF16).ap()
    io2["wdn_all"] = T("f2_wdn_all", [DC, NG0, 128, 128], BF16).ap()
    io2["out"] = s2
    io3 = {}
    io3["stream_in"] = s2
    io3["w_inT"] = T("w_inT", [D, E], F32, kind="ExternalInput").ap()
    io3["binT"] = T("binT", [128, 3 * DC], F32, kind="ExternalInput").ap()
    io3["w_outT"] = T("w_outT", [D, D], F32, kind="ExternalInput").ap()
    io3["boutT"] = T("boutT", [128, DC], F32, kind="ExternalInput").ap()
    io3["cosT1"] = T("cosT1", [D // 4, TH], F32, kind="ExternalInput").ap()
    io3["sinT1"] = T("sinT1", [D // 4, TH], F32, kind="ExternalInput").ap()
    io3["identf"] = T("identf", [128, 128], F32, kind="ExternalInput").ap()
    io3["q_scr"] = T("f3_q_scr", [D, TH], BF16).ap()
    io3["cc_a_in"] = T("f3_cc_a_in", [128, DC], F32)
    io3["cc_a_out"] = T("f3_cc_a_out", [128, DC], F32)
    io3["out"] = s3
    io4 = {}
    io4["stream_in"] = s3
    io4["w_guT"] = T("w_guT", [D, 2 * I1], F32, kind="ExternalInput").ap()
    io4["w_dnT"] = T("w_dnT", [I1, D], F32, kind="ExternalInput").ap()
    io4["out"] = T("stream_out", [D, TH], F32, kind="ExternalOutput").ap()

    with ExitStack() as stack:
        block = stack.enter_context(nc.Block())
        bars = [stack.enter_context(nc.semaphore("fbar%d" % i)) for i in range(3)]
        emits = [(emit_l0attn, io1, "f1_"), (emit_l0ffn, io2, "f2_"),
                 (emit_l1attn, io3, "f3_"), (emit_l1ffn, io4, "f4_")]
        gate = None
        for idx, (emit, io, pfx) in enumerate(emits):
            _PFX[0] = pfx
            gb = GatedBlock(block, gate)
            _SEM_LOG.clear()
            with ExitStack() as bstack:
                k = K(nc, gb, bstack)
                emit(k, c, io)
            blocksems = sorted(set(_SEM_LOG))
            if idx < 3:
                bar = bars[idx]

                @block.sync
                def _(s, _bar=bar, _g=gate):
                    if _g is not None:
                        s.wait_ge(_g[0], _g[1])
                    s.nop().then_inc(_bar, 1)

                for engname in ("tensor", "vector", "scalar"):
                    dec = getattr(block, engname)

                    @dec
                    def _(e, _bar=bar, _g=gate):
                        if _g is not None:
                            e.wait_ge(_g[0], _g[1])
                        e.wait_ge(_bar, 1)
                        e.nop().then_inc(_bar, 1)

                @block.gpsimd
                def _(g, _bar=bar, _g=gate, _sems=blocksems):
                    if _g is not None:
                        g.wait_ge(_g[0], _g[1])
                    g.wait_ge(_bar, 4)
                    g.dma_reset(None)
                    for snum in _sems:
                        g.sem_clear(range(snum, snum + 1))
                    g.nop().then_inc(_bar, 1)
                gate = (bar, 5)
        _PFX[0] = ""
    return nc


def prep_fused_inputs(cfg, inputs):
    m1 = prep_l0attn_inputs(cfg, inputs)
    m2 = prep_l0ffn_inputs(cfg, inputs, [None] * 8)
    m3 = prep_l1attn_inputs(cfg, inputs, [None] * 8)
    m4 = prep_l1ffn_inputs(cfg, inputs, [None] * 8)
    maps = []
    for core in range(8):
        m = {}
        for mm_ in (m1[core], m2[core], m3[core], m4[core]):
            for kk, vv in mm_.items():
                if kk != "stream_in":
                    m[kk] = vv
        maps.append(m)
    return maps


FUSED = True


def kernel_fused(**inputs):
    cfg = FULL
    c = derive(cfg)
    TH, D, L = c["TH"], c["D"], c["L"]
    nc = _get_prog("fused", build_fused, cfg)
    maps = prep_fused_inputs(cfg, inputs)
    res = run_bass_kernel_spmd(nc, maps, core_ids=list(range(8)))
    out = np.zeros((4, L, D), dtype=np.float32)
    for core in range(8):
        b, g = core // 2, core % 2
        out[b, g * TH:(g + 1) * TH, :] = res.results[core]["stream_out"].T
    return out


_kernel_unfused = kernel


def kernel(**inputs):
    if FUSED:
        return kernel_fused(**inputs)
    return _kernel_unfused(**inputs)


# version 07
# speedup vs baseline: 1.1446x; 1.0286x over previous
import math
from contextlib import ExitStack

import numpy as np
import ml_dtypes

import concourse.bass as bass
import concourse.mybir as mybir
from concourse.bass_utils import run_bass_kernel_spmd
from concourse.alu_op_type import AluOpType as OP

F32 = mybir.dt.float32
BF16 = mybir.dt.bfloat16
F16 = mybir.dt.float16
I32 = mybir.dt.int32
U32 = mybir.dt.uint32
AF = mybir.ActivationFunctionType
AX = mybir.AxisListType
BF = ml_dtypes.bfloat16

FULL = dict(L=4096, D=2048, NHC=8, I0=5632, I1=8192, NSH=8,
            CCG=[[0, 1, 2, 3, 4, 5, 6, 7]])
MINI = dict(L=2048, D=512, NHC=2, I0=768, I1=1024, NSH=4,
            CCG=[[0, 1, 2, 3], [4, 5, 6, 7]])


def derive(cfg):
    c = dict(cfg)
    c["DC"] = c["D"] // 128
    c["NTC"] = c["L"] // 512
    c["NQC"] = c["L"] // 512
    c["NTT"] = c["L"] // 128
    c["TH"] = c["L"] // 2
    c["NQL"] = c["TH"] // 512
    c["NJ"] = 2 * c["NHC"]
    c["E"] = 3 * c["D"]
    return c


_SEM_LOG = []


class Sem:
    def __init__(self, k, name):
        self.h = k.stack.enter_context(k.nc.semaphore(name))
        self.n = 0
        _SEM_LOG.append(self.h.num)


class K:
    def __init__(self, nc, block, stack):
        self.nc = nc
        self.block = block
        self.stack = stack

    def sem(self, name):
        return Sem(self, name)

    def sb(self, name, shape, dtype):
        return self.stack.enter_context(self.nc.sbuf_tensor(name, shape, dtype))

    def ps(self, name, shape, dtype=F32):
        return self.stack.enter_context(self.nc.psum_tensor(name, shape, dtype))


def emit_l0attn(k, c, io):
    nc, block = k.nc, k.block
    L, D, DC, NHC, NJ = c["L"], c["D"], c["DC"], c["NHC"], c["NJ"]
    NTC, NQC, NTT, NQL = c["NTC"], c["NQC"], c["NTT"], c["NQL"]
    TH = c["TH"]
    HD = NHC * 128
    NVC = max(1, HD // 256)
    VW = min(HD, 256)
    NVT = (HD // VW) * 4
    ESC = 1.0 / math.sqrt(128.0)
    assert DC == 2 * NHC

    nkt = [4 * qc + 4 for qc in range(NQC)]
    KTT = sum(nkt)
    cumk = [0]
    for n in nkt:
        cumk.append(cumk[-1] + n)

    def tile0(h, qc, p):
        return 2 * KTT * h + 2 * cumk[qc] + p * nkt[qc]

    ones_col = k.sb("ones_col", [128, 1], BF16)
    ones_row_f = k.sb("ones_row_f", [1, 128], F32)
    ones_col_f = k.sb("ones_col_f", [128, 1], F32)
    bl_sb = k.sb("bl_sb", [8, 1], F32)
    tmp1 = k.sb("tmp1", [128, 512], F32)
    tmp2 = k.sb("tmp2", [128, 512], F32)
    ps8 = k.ps("ps8", [128, 8, 512])

    m_cstg = k.sem("m_cstg")
    m_csts = k.sem("m_csts")
    m_x = k.sem("m_x")
    m_wk = [k.sem("m_wk0"), k.sem("m_wk1")]
    m_wq = [k.sem("m_wq0"), k.sem("m_wq1")]
    m_wv = k.sem("m_wv")
    m_cs = k.sem("m_cs")
    m_qst = [k.sem("m_qst0"), k.sem("m_qst1")]
    m_lam = k.sem("m_lam")
    m_qb = k.sem("m_qb")
    m_lr = k.sem("m_lr")
    m_wo = k.sem("m_wo")
    m_ao = k.sem("m_ao")
    m_dld = k.sem("m_dld")
    m_out = k.sem("m_out")

    s_peK = k.sem("s_peK")
    s_peV = k.sem("s_peV")
    s_peQ = k.sem("s_peQ")
    s_peL = k.sem("s_peL")
    s_dvK = k.sem("s_dvK")
    s_dvV = k.sem("s_dvV")
    s_dvQ = k.sem("s_dvQ")
    s_acL = k.sem("s_acL")
    s_st = k.sem("s_st")
    s_exp = k.sem("s_exp")
    s_msk = k.sem("s_msk")
    s_sums = k.sem("s_sums")
    s_ctx = k.sem("s_ctx")
    s_dvS = k.sem("s_dvS")
    s_peB = k.sem("s_peB")
    s_cmb = k.sem("s_cmb")
    s_cc = k.sem("s_cc")
    s_peD = k.sem("s_peD")
    s_exC = k.sem("s_exC")
    s_ex = k.sem("s_ex")
    s_sq2 = k.sem("s_sq2")
    s_stat = k.sem("s_stat")
    s_row = k.sem("s_row")
    s_peB2 = k.sem("s_peB2")
    s_nrm = k.sem("s_nrm")

    AT = 2 * NHC * KTT
    vslot = [NHC if h == 0 else h - 1 for h in range(NHC)]

    with ExitStack() as sv:
        kv = K(nc, block, sv)
        Vt = kv.sb("Vt", [128, NTT, NHC + 1, 128], BF16)

        with ExitStack() as sk:
            kf_ = K(nc, block, sk)
            Kf = kf_.sb("Kf", [128, NHC, L], BF16)

            with ExitStack() as sa:
                ka = K(nc, block, sa)
                xbf = ka.sb("A_xbf", [128, DC, 512], BF16)
                wkb = ka.sb("A_wkb", [128, 2, DC, 128], BF16)
                wqb = ka.sb("A_wqb", [128, 2, DC, 128], BF16)
                wvb = ka.sb("A_wvb", [128, DC, VW], BF16)
                wlb = ka.sb("A_wlb", [128, DC, NHC], BF16)
                cosb = ka.sb("A_cosb", [128, 512], F32)
                sinb = ka.sb("A_sinb", [128, 512], F32)
                stg = ka.sb("A_stg", [128, 2, 512], BF16)
                lam_st = ka.sb("A_lam", [8, 512], F32)

                @block.gpsimd
                def _(g):
                    g.dma_start(out=wlb[:, :, :],
                                in_=io["wl"].rearrange("(c p) h -> p c h", p=128)
                                ).then_inc(m_cstg.h, 16)
                    for tc in range(NTC):
                        t0 = tc * 512
                        if tc >= 1:
                            g.wait_ge(s_peK.h, NHC * tc)
                            g.wait_ge(s_peV.h, NVT * tc)
                            g.wait_ge(s_peQ.h, NJ * tc)
                            g.wait_ge(s_peL.h, tc)
                        g.dma_start(out=xbf[:, :, :],
                                    in_=io["xT"].rearrange("(c p) t -> p c t", p=128)[:, :, t0:t0 + 512]
                                    ).then_inc(m_x.h, 16)
                        for h in range(NHC):
                            gidx = tc * NHC + h
                            if gidx >= 2:
                                g.wait_ge(s_peK.h, gidx - 1)
                            g.dma_start(out=wkb[:, h % 2, :, :],
                                        in_=io["wk"].rearrange("(c p) o -> p c o", p=128)[:, :, h * 128:(h + 1) * 128]
                                        ).then_inc(m_wk[h % 2].h, 16)
                        for v in range(NVC):
                            gidx = tc * NVC + v
                            if gidx >= 1:
                                g.wait_ge(s_peV.h, 4 * gidx)
                            g.dma_start(out=wvb[:, :, :],
                                        in_=io["wv"].rearrange("(c p) o -> p c o", p=128)[:, :, v * VW:(v + 1) * VW]
                                        ).then_inc(m_wv.h, 16)
                        for j in range(NJ):
                            gidx = tc * NJ + j
                            if gidx >= 2:
                                g.wait_ge(s_peQ.h, gidx - 1)
                            g.dma_start(out=wqb[:, j % 2, :, :],
                                        in_=io["wq"].rearrange("(c p) o -> p c o", p=128)[:, :, j * 128:(j + 1) * 128]
                                        ).then_inc(m_wq[j % 2].h, 16)

                @block.sync
                def _(s):
                    s.dma_start(out=bl_sb[0:NHC, :], in_=io["bl"][:, None]).then_inc(m_csts.h, 16)
                    for tc in range(NTC):
                        t0 = tc * 512
                        if tc >= 1:
                            s.wait_ge(s_dvK.h, NHC * tc)
                            s.wait_ge(s_dvQ.h, NJ * tc)
                        s.dma_start(out=cosb[:, :], in_=io["cosk"][:, t0:t0 + 512]).then_inc(m_cs.h, 16)
                        s.dma_start(out=sinb[:, :], in_=io["sink"][:, t0:t0 + 512]).then_inc(m_cs.h, 16)
                        for j in range(NJ):
                            s.wait_ge(s_dvQ.h, tc * NJ + j + 1)
                            s.dma_start(out=io["q_scr"][j * 128:(j + 1) * 128, t0:t0 + 512],
                                        in_=stg[:, j % 2, :]).then_inc(m_qst[j % 2].h, 16)
                        s.wait_ge(s_acL.h, tc + 1)
                        s.dma_start(out=io["lam_scr"][0:NHC, t0:t0 + 512],
                                    in_=lam_st[0:NHC, :]).then_inc(m_lam.h, 16)

                @block.tensor
                def _(t):
                    for tc in range(NTC):
                        t.wait_ge(m_x.h, 16 * (tc + 1))
                        for h in range(NHC):
                            gidx = tc * NHC + h
                            t.wait_ge(m_wk[h % 2].h, 16 * (gidx // 2 + 1))
                            if gidx >= 2:
                                t.wait_ge(s_dvK.h, gidx - 1)
                            for kc in range(DC):
                                mm = t.matmul(ps8[:, gidx % 2, :], wkb[:, h % 2, kc, :],
                                              xbf[:, kc, :], start=(kc == 0), stop=(kc == DC - 1))
                            mm.then_inc(s_peK.h, 1)
                            s_peK.n += 1
                        for v in range(NVC):
                            t.wait_ge(m_wv.h, 16 * (tc * NVC + v + 1))
                            for tt in range(4):
                                gidx = tc * NVT + v * 4 + tt
                                if gidx >= 2:
                                    t.wait_ge(s_dvV.h, gidx - 1)
                                for kc in range(DC):
                                    mm = t.matmul(ps8[:, 2 + gidx % 2, 0:VW],
                                                  xbf[:, kc, tt * 128:(tt + 1) * 128],
                                                  wvb[:, kc, :], start=(kc == 0), stop=(kc == DC - 1))
                                mm.then_inc(s_peV.h, 1)
                                s_peV.n += 1
                        for j in range(NJ):
                            gidx = tc * NJ + j
                            t.wait_ge(m_wq[j % 2].h, 16 * (gidx // 2 + 1))
                            if gidx >= 2:
                                t.wait_ge(s_dvQ.h, gidx - 1)
                            for kc in range(DC):
                                mm = t.matmul(ps8[:, 4 + gidx % 2, :], wqb[:, j % 2, kc, :],
                                              xbf[:, kc, :], start=(kc == 0), stop=(kc == DC - 1))
                            mm.then_inc(s_peQ.h, 1)
                            s_peQ.n += 1
                        if tc == 0:
                            t.wait_ge(m_cstg.h, 16)
                        if tc >= 1:
                            t.wait_ge(s_acL.h, tc)
                        for kc in range(DC):
                            mm = t.matmul(ps8[0:NHC, 6, :], wlb[:, kc, 0:NHC], xbf[:, kc, :],
                                          start=(kc == 0), stop=(kc == DC - 1))
                        mm.then_inc(s_peL.h, 1)
                        s_peL.n += 1

                @block.scalar
                def _(sc):
                    sc.wait_ge(m_csts.h, 16)
                    for tc in range(NTC):
                        sc.wait_ge(s_peL.h, tc + 1)
                        if tc >= 1:
                            sc.wait_ge(m_lam.h, 16 * tc)
                        op = sc.activation(out=lam_st[0:NHC, :], in_=ps8[0:NHC, 6, :],
                                           func=AF.Sigmoid, bias=bl_sb[0:NHC, :])
                        op.then_inc(s_acL.h, 1)
                        s_acL.n += 1

                @block.vector
                def _(v):
                    v.memset(ones_col[:, :], 1.0)
                    v.memset(ones_row_f[:, :], 1.0)
                    v.memset(ones_col_f[:, :], 1.0)
                    for tc in range(NTC):
                        v.wait_ge(m_cs.h, 32 * (tc + 1))
                        for h in range(NHC):
                            gidx = tc * NHC + h
                            v.wait_ge(s_peK.h, gidx + 1)
                            pk = ps8[:, gidx % 2, :]
                            v.tensor_tensor(out=tmp1[:, :], in0=pk, in1=cosb[:, :], op=OP.mult)
                            v.scalar_tensor_tensor(out=tmp2[0:64, :], in0=pk[64:128, :],
                                                   scalar=-1.0, in1=sinb[0:64, :],
                                                   op0=OP.mult, op1=OP.mult)
                            v.tensor_tensor(out=tmp2[64:128, :], in0=pk[0:64, :],
                                            in1=sinb[64:128, :], op=OP.mult)
                            op = v.tensor_tensor(out=Kf[:, h, tc * 512:tc * 512 + 512],
                                                 in0=tmp1[:, :], in1=tmp2[:, :], op=OP.add)
                            op.then_inc(s_dvK.h, 1)
                            s_dvK.n += 1
                        for vi in range(NVC):
                            for tt in range(4):
                                gidx = tc * NVT + vi * 4 + tt
                                v.wait_ge(s_peV.h, gidx + 1)
                                dst = Vt[:, tc * 4 + tt, vi * (VW // 128):(vi + 1) * (VW // 128), :]
                                op = v.tensor_copy(dst, ps8[:, 2 + gidx % 2, 0:VW])
                                op.then_inc(s_dvV.h, 1)
                                s_dvV.n += 1
                        for j in range(NJ):
                            gidx = tc * NJ + j
                            v.wait_ge(s_peQ.h, gidx + 1)
                            if gidx >= 2:
                                v.wait_ge(m_qst[gidx % 2].h, 16 * (gidx // 2))
                            pq_ = ps8[:, 4 + gidx % 2, :]
                            v.tensor_tensor(out=tmp1[:, :], in0=pq_, in1=cosb[:, :], op=OP.mult)
                            v.scalar_tensor_tensor(out=tmp2[0:64, :], in0=pq_[64:128, :],
                                                   scalar=-1.0, in1=sinb[0:64, :],
                                                   op0=OP.mult, op1=OP.mult)
                            v.tensor_tensor(out=tmp2[64:128, :], in0=pq_[0:64, :],
                                            in1=sinb[64:128, :], op=OP.mult)
                            op = v.tensor_tensor(out=stg[:, j % 2, :], in0=tmp1[:, :],
                                                 in1=tmp2[:, :], op=OP.add)
                            op.then_inc(s_dvQ.h, 1)
                            s_dvQ.n += 1

            with ExitStack() as sb_:
                kb = K(nc, block, sb_)
                qb = kb.sb("B_qb", [128, 2, L], BF16)
                ptb = kb.sb("B_pt", [128, 2, 512], BF16)
                mask_sb = kb.sb("B_mask", [128, 4, 512], BF16)
                sums_sb = kb.sb("B_sums", [1, 2, 512], F32)
                rec_sb = kb.sb("B_rec", [1, 2, 512], F32)
                lamr = kb.sb("B_lamr", [1, 512], F32)
                lr1 = kb.sb("B_lr1", [1, 512], F32)
                rb_sb = kb.sb("B_rb", [128, 2, 512], F32)
                ct0 = kb.sb("B_ct0", [128, 512], F32)

                @block.sync
                def _(s):
                    s.wait_ge(s_peL.h, NTC)
                    s.wait_ge(s_dvQ.h, NJ * NTC)
                    s.wait_ge(s_acL.h, NTC)
                    s.dma_start(out=mask_sb[:, :, :],
                                in_=io["masks"].rearrange("m p q -> p m q")).then_inc(m_csts.h, 16)
                    for h in range(NHC):
                        if h >= 1:
                            s.wait_ge(s_st.h, 2 * KTT * h)
                        s.dma_start(out=qb[:, 0, :],
                                    in_=io["q_scr"][(2 * h) * 128:(2 * h + 1) * 128, :]).then_inc(m_qb.h, 16)
                        s.dma_start(out=qb[:, 1, :],
                                    in_=io["q_scr"][(2 * h + 1) * 128:(2 * h + 2) * 128, :]).then_inc(m_qb.h, 16)
                        for qc in range(NQC):
                            hq = h * NQC + qc
                            if hq >= 1:
                                s.wait_ge(s_dvS.h, 2 * hq)
                            s.dma_start(out=lamr[:, :],
                                        in_=io["lam_scr"][h:h + 1, qc * 512:(qc + 1) * 512]
                                        ).then_inc(m_lr.h, 16)

                @block.tensor
                def _(t):
                    t.wait_ge(s_dvK.h, NHC * NTC)
                    t.wait_ge(s_dvV.h, NVT * NTC)
                    nst = 0
                    ncs = 0
                    nmsk = 0
                    for h in range(NHC):
                        t.wait_ge(m_qb.h, 32 * (h + 1))
                        for qc in range(NQC):
                            hq = h * NQC + qc
                            for p in range(2):
                                pq = 2 * hq + p
                                n = nkt[qc]
                                for kt in range(n + 1):
                                    if kt < n:
                                        i = nst
                                        if i >= 2:
                                            t.wait_ge(s_exp.h, i - 1)
                                        mm = t.matmul(ps8[:, i % 2, :],
                                                      Kf[:, h, kt * 128:(kt + 1) * 128],
                                                      qb[:, p, qc * 512:(qc + 1) * 512],
                                                      start=True, stop=True)
                                        mm.then_inc(s_st.h, 1)
                                        nst += 1
                                        s_st.n += 1
                                    if kt >= 1:
                                        ck = kt - 1
                                        i = ncs
                                        diag = ck >= 4 * qc
                                        if diag:
                                            t.wait_ge(s_msk.h, nmsk + (ck - 4 * qc) + 1)
                                        else:
                                            t.wait_ge(s_exp.h, i + 1)
                                        if ck == 0:
                                            if pq >= 2:
                                                t.wait_ge(s_dvS.h, pq - 1)
                                            if hq >= 1:
                                                t.wait_ge(s_cmb.h, hq)
                                        mm = t.matmul(ps8[0:1, 2 + pq % 2, :], ones_col[:, :],
                                                      ptb[:, i % 2, :],
                                                      start=(ck == 0), stop=(ck == n - 1))
                                        mm.then_inc(s_sums.h, 1)
                                        s_sums.n += 1
                                        mm = t.matmul(ps8[:, 4 + p, :], Vt[:, ck, h, :],
                                                      ptb[:, i % 2, :],
                                                      start=(ck == 0), stop=(ck == n - 1))
                                        mm.then_inc(s_ctx.h, 1)
                                        ncs += 1
                                        s_ctx.n += 1
                                nmsk += 4
                            t.wait_ge(s_dvS.h, 2 * hq + 2)
                            if hq >= 1:
                                t.wait_ge(s_cmb.h, hq)
                            mm = t.matmul(ps8[:, 6, :], ones_row_f[:, :], rec_sb[:, 0, :],
                                          start=True, stop=True)
                            mm = t.matmul(ps8[:, 7, :], ones_row_f[:, :], lr1[:, :],
                                          start=True, stop=True)
                            mm.then_inc(s_peB.h, 1)
                            s_peB.n += 1

                @block.scalar
                def _(sc):
                    for i in range(AT):
                        sc.wait_ge(s_st.h, i + 1)
                        if i >= 2:
                            sc.wait_ge(s_ctx.h, i - 1)
                        op = sc.activation(out=ptb[:, i % 2, :], in_=ps8[:, i % 2, :],
                                           func=AF.Exp, scale=ESC)
                        op.then_inc(s_exp.h, 1)
                        s_exp.n += 1

                @block.vector
                def _(v):
                    v.wait_ge(m_csts.h, 32)
                    for h in range(NHC):
                        for qc in range(NQC):
                            hq = h * NQC + qc
                            for p in range(2):
                                pq = 2 * hq + p
                                n = nkt[qc]
                                base = tile0(h, qc, p)
                                for m in range(4):
                                    i = base + 4 * qc + m
                                    v.wait_ge(s_exp.h, i + 1)
                                    op = v.tensor_tensor(out=ptb[:, i % 2, :], in0=ptb[:, i % 2, :],
                                                         in1=mask_sb[:, m, :], op=OP.mult)
                                    op.then_inc(s_msk.h, 1)
                                    s_msk.n += 1
                                v.wait_ge(s_sums.h, base + n)
                                v.tensor_copy(sums_sb[:, pq % 2, :], ps8[0:1, 2 + pq % 2, :])
                                if p == 0:
                                    op = v.reciprocal(rec_sb[:, pq % 2, :], sums_sb[:, pq % 2, :])
                                    op.then_inc(s_dvS.h, 1)
                                    s_dvS.n += 1
                                else:
                                    v.reciprocal(rec_sb[:, pq % 2, :], sums_sb[:, pq % 2, :])
                                    v.wait_ge(m_lr.h, 16 * (hq + 1))
                                    op = v.tensor_tensor(out=lr1[:, :], in0=lamr[:, :],
                                                         in1=rec_sb[:, pq % 2, :], op=OP.mult)
                                    op.then_inc(s_dvS.h, 1)
                                    s_dvS.n += 1
                            v.wait_ge(s_peB.h, hq + 1)
                            v.tensor_copy(rb_sb[:, 0, :], ps8[:, 6, :])
                            v.tensor_copy(rb_sb[:, 1, :], ps8[:, 7, :])
                            v.wait_ge(s_ctx.h, tile0(h, qc, 1) + nkt[qc])
                            v.tensor_tensor(out=ct0[:, :], in0=ps8[:, 4, :], in1=rb_sb[:, 0, :],
                                            op=OP.mult)
                            v.tensor_tensor(out=tmp1[:, :], in0=ps8[:, 5, :], in1=rb_sb[:, 1, :],
                                            op=OP.mult)
                            dst = Vt[:, qc * 4:(qc + 1) * 4, vslot[h], :]
                            op = v.tensor_tensor(out=dst, in0=ct0[:, :], in1=tmp1[:, :],
                                                 op=OP.subtract)
                            op.then_inc(s_cmb.h, 1)
                            s_cmb.n += 1

        with ExitStack() as sc2:
            kc2 = K(nc, block, sc2)
            wob = kc2.sb("C_wob", [128, NHC, D], BF16)
            sA = kc2.sb("C_sA", [128, DC, 512], F32)

            @block.gpsimd
            def _(g):
                g.wait_ge(s_cmb.h, NHC * NQC)
                g.dma_start(out=wob[:, :, :],
                            in_=io["wo"].rearrange("(c p) o -> p c o", p=128)
                            ).then_inc(m_wo.h, 16)
                g.wait_ge(m_ao.h, 16 * NQC)
                g.collective_compute("ReduceScatter", OP.add,
                                     replica_groups=[[0, 1], [2, 3], [4, 5], [6, 7]],
                                     ins=[io["cc_in"].ap().opt()],
                                     outs=[io["cc_out"].ap().opt()]).then_inc(s_cc.h, 1)
                s_cc.n += 1

            @block.tensor
            def _(t):
                t.wait_ge(m_wo.h, 16)
                for qc8 in range(NQC):
                    t.wait_ge(s_cmb.h, (NHC - 1) * NQC + qc8 + 1)
                    for oc in range(DC):
                        gi = qc8 * DC + oc
                        if gi >= 2:
                            t.wait_ge(s_exC.h, gi - 1)
                        for dc in range(NHC):
                            mm = t.matmul(ps8[:, gi % 2, :], wob[:, dc, oc * 128:(oc + 1) * 128],
                                          Vt[:, qc8 * 4:(qc8 + 1) * 4, vslot[dc], :],
                                          start=(dc == 0), stop=(dc == NHC - 1))
                        mm.then_inc(s_peD.h, 1)
                        s_peD.n += 1

            @block.vector
            def _(v):
                for qc8 in range(NQC):
                    for oc in range(DC):
                        gi = qc8 * DC + oc
                        v.wait_ge(s_peD.h, gi + 1)
                        if oc == 0 and qc8 >= 1:
                            v.wait_ge(m_ao.h, 16 * qc8)
                        op = v.tensor_copy(sA[:, oc, :], ps8[:, gi % 2, :])
                        op.then_inc(s_exC.h, 1)
                        s_exC.n += 1

            @block.sync
            def _(s):
                for qc8 in range(NQC):
                    r = qc8 // NQL
                    q0 = (qc8 % NQL) * 512
                    s.wait_ge(s_exC.h, DC * (qc8 + 1))
                    s.dma_start(out=io["cc_in"].ap()[r].rearrange("(c p) t -> p c t", p=128)[:, :, q0:q0 + 512],
                                in_=sA[:, :, :]).then_inc(m_ao.h, 16)

    with ExitStack() as sd:
        kd = K(nc, block, sd)
        aoslab = kd.sb("D_ao", [128, DC, 512], F32)
        xslab = kd.sb("D_x", [128, DC, 512], F32)
        sD = kd.sb("D_s", [128, DC, 512], F32)
        scr2 = kd.sb("D_scr2", [128, 2, 512], F32)
        rowbuf = kd.sb("D_rows", [1, 8, 512], F32)
        rb2 = kd.sb("D_rb2", [128, 2, 512], F32)

        @block.sync
        def _(s):
            for ql in range(NQL):
                q0 = ql * 512
                if ql == 0:
                    s.wait_ge(s_cc.h, 1)
                else:
                    s.wait_ge(s_ex.h, DC * ql)
                s.dma_start(out=aoslab[:, :, :],
                            in_=io["cc_out"].ap().rearrange("(c p) t -> p c t", p=128)[:, :, q0:q0 + 512]
                            ).then_inc(m_dld.h, 16)
                s.dma_start(out=xslab[:, :, :],
                            in_=io["xTh"].rearrange("(c p) t -> p c t", p=128)[:, :, q0:q0 + 512]
                            ).then_inc(m_dld.h, 16)
                s.wait_ge(s_nrm.h, ql + 1)
                s.dma_start(out=io["out"].rearrange("(c p) t -> p c t", p=128)[:, :, q0:q0 + 512],
                            in_=sD[:, :, :]).then_inc(m_out.h, 16)
            s.wait_ge(m_out.h, 16 * NQL)

        @block.tensor
        def _(t):
            for ql in range(NQL):
                for oc in range(DC):
                    gi = ql * DC + oc
                    t.wait_ge(s_ex.h, gi + 1)
                    if oc == 0 and ql >= 1:
                        t.wait_ge(s_row.h, 2 * ql - 1)
                    mm = t.matmul(ps8[0:1, 2, :], ones_col_f[:, :], sD[:, oc, :],
                                  start=(oc == 0), stop=(oc == DC - 1))
                    t.wait_ge(s_sq2.h, (DC + 1) * ql + oc + 1)
                    mm = t.matmul(ps8[0:1, 3, :], ones_col_f[:, :], scr2[:, oc % 2, :],
                                  start=(oc == 0), stop=(oc == DC - 1))
                    mm.then_inc(s_stat.h, 1)
                    s_stat.n += 1
                t.wait_ge(s_row.h, 2 * ql + 2)
                if ql >= 1:
                    t.wait_ge(s_nrm.h, ql)
                mm = t.matmul(ps8[:, 4, :], ones_row_f[:, :], rowbuf[:, 4, :],
                              start=True, stop=True)
                mm = t.matmul(ps8[:, 5, :], ones_row_f[:, :], rowbuf[:, 5, :],
                              start=True, stop=True)
                mm.then_inc(s_peB2.h, 1)
                s_peB2.n += 1

        @block.scalar
        def _(sc):
            for ql in range(NQL):
                for oc in range(DC):
                    gi = ql * DC + oc
                    sc.wait_ge(s_ex.h, gi + 1)
                    if gi >= 2:
                        sc.wait_ge(s_stat.h, gi - 1)
                    op = sc.activation(out=scr2[:, oc % 2, :], in_=sD[:, oc, :],
                                       func=AF.Square)
                    op.then_inc(s_sq2.h, 1)
                    s_sq2.n += 1
                sc.wait_ge(s_row.h, 2 * ql + 1)
                op = sc.activation(out=rowbuf[:, 6, :], in_=rowbuf[:, 1, :], func=AF.Sqrt)
                op.then_inc(s_sq2.h, 1)
                s_sq2.n += 1

        @block.vector
        def _(v):
            for ql in range(NQL):
                for oc in range(DC):
                    v.wait_ge(m_dld.h, 32 * (ql + 1))
                    if ql == 0 and oc == 0:
                        v.wait_ge(m_ao.h, 16 * NQC)
                    op = v.tensor_tensor(out=sD[:, oc, :], in0=aoslab[:, oc, :],
                                         in1=xslab[:, oc, :], op=OP.add)
                    op.then_inc(s_ex.h, 1)
                    s_ex.n += 1
                v.wait_ge(s_stat.h, DC * (ql + 1))
                v.tensor_scalar(out=rowbuf[:, 2, :], in0=ps8[0:1, 2, :],
                                scalar1=1.0 / D, scalar2=None, op0=OP.mult)
                v.tensor_scalar(out=rowbuf[:, 3, :], in0=ps8[0:1, 3, :],
                                scalar1=1.0 / D, scalar2=None, op0=OP.mult)
                v.tensor_tensor(out=rowbuf[:, 0, :], in0=rowbuf[:, 2, :],
                                in1=rowbuf[:, 2, :], op=OP.mult)
                v.tensor_tensor(out=rowbuf[:, 3, :], in0=rowbuf[:, 3, :],
                                in1=rowbuf[:, 0, :], op=OP.subtract)
                op = v.tensor_scalar(out=rowbuf[:, 1, :], in0=rowbuf[:, 3, :],
                                     scalar1=1e-5, scalar2=None, op0=OP.add)
                op.then_inc(s_row.h, 1)
                s_row.n += 1
                v.wait_ge(s_sq2.h, (DC + 1) * (ql + 1))
                v.reciprocal(rowbuf[:, 4, :], rowbuf[:, 6, :])
                op = v.tensor_tensor(out=rowbuf[:, 5, :], in0=rowbuf[:, 2, :],
                                     in1=rowbuf[:, 4, :], op=OP.mult)
                op.then_inc(s_row.h, 1)
                s_row.n += 1
                v.wait_ge(s_peB2.h, ql + 1)
                v.tensor_copy(rb2[:, 0, :], ps8[:, 4, :])
                v.tensor_copy(rb2[:, 1, :], ps8[:, 5, :])
                for oc in range(DC):
                    v.tensor_tensor(out=scr2[:, oc % 2, :], in0=sD[:, oc, :],
                                    in1=rb2[:, 0, :], op=OP.mult)
                    op = v.tensor_tensor(out=sD[:, oc, :], in0=scr2[:, oc % 2, :],
                                         in1=rb2[:, 1, :], op=OP.subtract)
                op.then_inc(s_nrm.h, 1)
                s_nrm.n += 1


def build_l0attn(cfg):
    c = derive(cfg)
    L, D, NHC, NJ, TH = c["L"], c["D"], c["NHC"], c["NJ"], c["TH"]
    HD = NHC * 128
    nc = bass.Bass("TRN2", target_bir_lowering=False, num_devices=8,
                   detect_race_conditions=False)
    io = {}
    io["xT"] = nc.dram_tensor("xT", [D, L], F32, kind="ExternalInput").ap()
    io["xTh"] = nc.dram_tensor("xTh", [D, TH], F32, kind="ExternalInput").ap()
    io["wq"] = nc.dram_tensor("wq", [D, NJ * 128], F32, kind="ExternalInput").ap()
    io["wk"] = nc.dram_tensor("wk", [D, HD], F32, kind="ExternalInput").ap()
    io["wv"] = nc.dram_tensor("wv", [D, HD], F32, kind="ExternalInput").ap()
    io["wl"] = nc.dram_tensor("wl", [D, NHC], F32, kind="ExternalInput").ap()
    io["bl"] = nc.dram_tensor("bl", [NHC], F32, kind="ExternalInput").ap()
    io["wo"] = nc.dram_tensor("wo", [NHC * 128, D], F32, kind="ExternalInput").ap()
    io["cosk"] = nc.dram_tensor("cosk", [128, L], F32, kind="ExternalInput").ap()
    io["sink"] = nc.dram_tensor("sink", [128, L], F32, kind="ExternalInput").ap()
    io["masks"] = nc.dram_tensor("masks", [4, 128, 512], BF16, kind="ExternalInput").ap()
    io["out"] = nc.dram_tensor("stream_out", [D, TH], F32, kind="ExternalOutput").ap()
    io["q_scr"] = nc.dram_tensor("q_scr", [NJ * 128, L], BF16).ap()
    io["lam_scr"] = nc.dram_tensor("lam_scr", [NHC, L], F32).ap()
    io["cc_in"] = nc.dram_tensor("cc_in", [2, D, TH], F32)
    io["cc_out"] = nc.dram_tensor("cc_out", [D, TH], F32)
    with ExitStack() as stack:
        block = stack.enter_context(nc.Block())
        k = K(nc, block, stack)
        emit_l0attn(k, c, io)
    return nc


def rope_tables_l0(L):
    half = 64
    inv = 1.0 / (10000.0 ** (np.arange(half, dtype=np.float64) * 2.0 / 128.0))
    freqs = np.arange(L, dtype=np.float64)[None, :] * inv[:, None]
    emb = np.concatenate([freqs, freqs], axis=0)
    return (np.cos(emb).astype(np.float32).copy(),
            np.sin(emb).astype(np.float32).copy())


def causal_masks():
    m = np.zeros((4, 128, 512), dtype=BF)
    qq = np.arange(512)[None, :]
    kk = np.arange(128)[:, None]
    for off in range(4):
        m[off] = (qq >= kk + 128 * off).astype(BF)
    return m


def prep_l0attn_inputs(cfg, inp):
    c = derive(cfg)
    L, D, NHC, NJ, TH = c["L"], c["D"], c["NHC"], c["NJ"], c["TH"]
    x = np.asarray(inp["L0_attn_x"], dtype=np.float32)
    Wq = np.asarray(inp["L0_attn_Wq"], dtype=np.float32)
    Wk = np.asarray(inp["L0_attn_Wk"], dtype=np.float32)
    Wv = np.asarray(inp["L0_attn_Wv"], dtype=np.float32)
    Wl = np.asarray(inp["L0_attn_Wl"], dtype=np.float32)
    blv = np.asarray(inp["L0_attn_bl"], dtype=np.float32)
    Wo = np.asarray(inp["L0_attn_Wo"], dtype=np.float32)
    cosk, sink = rope_tables_l0(L)
    msk = causal_masks()
    maps = []
    for core in range(8):
        b, g = core // 2, core % 2
        xT = np.ascontiguousarray(x[b].T)
        xTh = np.ascontiguousarray(x[b, g * TH:(g + 1) * TH, :].T)
        j0 = g * NJ * 128
        maps.append(dict(
            xT=xT, xTh=xTh,
            wq=np.ascontiguousarray(Wq[:, j0:j0 + NJ * 128]),
            wk=np.ascontiguousarray(Wk[:, g * NHC * 128:(g + 1) * NHC * 128]),
            wv=np.ascontiguousarray(Wv[:, g * NHC * 128:(g + 1) * NHC * 128]),
            wl=np.ascontiguousarray(Wl[:, g * NHC:(g + 1) * NHC]),
            bl=np.ascontiguousarray(blv[g * NHC:(g + 1) * NHC]),
            wo=np.ascontiguousarray(Wo[g * NHC * 128:(g + 1) * NHC * 128, :]),
            cosk=cosk, sink=sink, masks=msk,
        ))
    return maps


_PROG_CACHE = {}


def _get_prog(name, builder, cfg):
    key = (name, repr(sorted(cfg.items())))
    if key not in _PROG_CACHE:
        _PROG_CACHE[key] = builder(cfg)
    return _PROG_CACHE[key]


def kernel(**inputs):
    cfg = FULL
    c = derive(cfg)
    TH, D, L = c["TH"], c["D"], c["L"]
    cores = list(range(8))
    nc1 = _get_prog("l0attn", build_l0attn, cfg)
    res1 = run_bass_kernel_spmd(nc1, prep_l0attn_inputs(cfg, inputs), core_ids=cores)
    streams = [np.ascontiguousarray(res1.results[i]["stream_out"]) for i in range(8)]
    nc2 = _get_prog("l0ffn", build_l0ffn, cfg)
    res2 = run_bass_kernel_spmd(nc2, prep_l0ffn_inputs(cfg, inputs, streams), core_ids=cores)
    streams = [np.ascontiguousarray(res2.results[i]["stream_out"]) for i in range(8)]
    nc3 = _get_prog("l1attn", build_l1attn, cfg)
    res3 = run_bass_kernel_spmd(nc3, prep_l1attn_inputs(cfg, inputs, streams), core_ids=cores)
    streams = [np.ascontiguousarray(res3.results[i]["stream_out"]) for i in range(8)]
    nc4 = _get_prog("l1ffn", build_l1ffn, cfg)
    res4 = run_bass_kernel_spmd(nc4, prep_l1ffn_inputs(cfg, inputs, streams), core_ids=cores)
    streams = [res4.results[i]["stream_out"] for i in range(8)]
    out = np.zeros((4, L, D), dtype=np.float32)
    for core in range(8):
        b, g = core // 2, core % 2
        out[b, g * TH:(g + 1) * TH, :] = streams[core].T
    return out
def emit_l0ffn(k, c, io):
    nc, block = k.nc, k.block
    L, D, DC, NHC = c["L"], c["D"], c["DC"], c["NHC"]
    TH, NQL = c["TH"], c["NQL"]
    I0 = c["I0"]
    NG = I0 // 128
    W0 = D // 8
    G0 = D // 128
    W1 = I0 // 8
    G1 = I0 // 128
    NHW = max(1, D // 512)
    HB = min(D, 512)
    NTCH = TH // 512

    ones_row_f = k.sb("ones_row_f", [1, 128], F32)
    ones_col_f = k.sb("ones_col_f", [128, 1], F32)
    identb = k.sb("identb", [128, 128], BF16)
    ps8 = k.ps("ps8", [128, 8, 512])
    psbf = ps8.bitcast(BF16)

    m_id = k.sem("m_id")
    m_pk = k.sem("m_pk")
    m_sc = k.sem("m_sc")
    m_sg = k.sem("m_sg")
    m_sd = k.sem("m_sd")
    m_xbf = k.sem("m_xbf")
    m_wgu = [k.sem("m_wgu0"), k.sem("m_wgu1")]
    m_wdn = [k.sem("m_wdn0"), k.sem("m_wdn1")]
    m_str = [k.sem("m_str0"), k.sem("m_str1")]
    m_out = k.sem("m_out")

    s_dq = k.sem("s_dq")
    s_fn = k.sem("s_fn")
    s_tr = k.sem("s_tr")
    s_ev = k.sem("s_ev")
    s_gu = k.sem("s_gu")
    s_si = k.sem("s_si")
    s_hb = k.sem("s_hb")
    s_dn = k.sem("s_dn")
    s_ye = k.sem("s_ye")
    s_sq = k.sem("s_sq")
    s_st2 = k.sem("s_st2")
    s_row = k.sem("s_row")
    s_bc = k.sem("s_bc")
    s_rs = k.sem("s_rs")
    s_nrm = k.sem("s_nrm")
    s_cc = k.sem("s_cc")

    NSH = c["NSH"]
    NGU_loc = 2 * NG // NSH
    NDN_loc = DC // NSH
    NT1 = NGU_loc + NDN_loc
    if "dbg_val" in io:
        io["_dbgf"] = k.sem("m_dbgf")

    def tinfo(ti):
        if ti < NGU_loc:
            return ("gu", ti)
        return ("dn", ti - NGU_loc)

    def ntr_of(ti):
        return DC if tinfo(ti)[0] == "gu" else NG

    trbase = [0]
    for ti in range(NT1):
        trbase.append(trbase[-1] + ntr_of(ti))

    with ExitStack() as sf:
        kf = K(nc, block, sf)
        pk_sb = kf.sb("F_pk", [128, max(W0, W1)], I32)
        nib = kf.sb("F_nib", [128, 2, max(W0, W1)], I32)
        ge8 = kf.sb("F_ge8", [128, 2, max(W0, W1)], F32)
        bits = kf.sb("F_bits", [128, 2, max(W0, W1)], U32)
        val = kf.sb("F_val", [128, max(8 * W0, 8 * W1)], BF16)
        scT = kf.sb("F_scT", [128, max(G0, G1)], F32)
        scP = kf.sb("F_scP", [128, max(G0, G1)], F32)
        stg = kf.sb("F_stg", [128, max(DC, NG), 128], BF16)

        @block.gpsimd
        def _(g):
            for ti in range(NT1):
                kind, idx = tinfo(ti)
                if ti >= 1:
                    g.wait_ge(s_dq.h, ti)
                if kind == "gu":
                    g.dma_start(out=pk_sb[:, 0:W0],
                                in_=io["gup"][idx * 128:(idx + 1) * 128, :]).then_inc(m_pk.h, 16)
                    g.dma_start(out=scT[:, 0:G0],
                                in_=io["gusT"][idx * 128:(idx + 1) * 128, :]).then_inc(m_sc.h, 16)
                else:
                    g.dma_start(out=pk_sb[:, 0:W1],
                                in_=io["dnp"][idx * 128:(idx + 1) * 128, :]).then_inc(m_pk.h, 16)
                    g.dma_start(out=scT[:, 0:G1],
                                in_=io["dnsT"][idx * 128:(idx + 1) * 128, :]).then_inc(m_sc.h, 16)
            g.wait_ge(m_sg.h, 16 * NGU_loc)
            g.wait_ge(m_sd.h, 16 * NDN_loc)
            g.collective_compute("AllGather", OP.bypass, replica_groups=c["CCG"],
                                 ins=[io["wgu_sh"].opt()],
                                 outs=[io["wguT_scr"].opt()]).then_inc(s_cc.h, 1)
            g.collective_compute("AllGather", OP.bypass, replica_groups=c["CCG"],
                                 ins=[io["wdn_sh"].opt()],
                                 outs=[io["wdn_all"].opt()]).then_inc(s_cc.h, 1)

        @block.vector
        def _(v):
            v.memset(ones_row_f[:, :], 1.0)
            v.memset(ones_col_f[:, :], 1.0)
            ngu_st = 0
            ndn_st = 0
            for ti in range(NT1):
                kind, idx = tinfo(ti)
                nw = W0 if kind == "gu" else W1
                ngr = G0 if kind == "gu" else G1
                ntr = ntr_of(ti)
                v.wait_ge(m_pk.h, 16 * (ti + 1))
                v.wait_ge(m_sc.h, 16 * (ti + 1))
                v.tensor_scalar(out=scP[:, 0:ngr], in0=scT[:, 0:ngr],
                                scalar1=float(2.0 ** 126), scalar2=None, op0=OP.mult)
                for kp in range(0, 8, 2):
                    for q_ in (0, 1):
                        op = v.tensor_scalar(out=nib[:, q_, 0:nw], in0=pk_sb[:, 0:nw],
                                             scalar1=4 * (kp + q_), scalar2=15,
                                             op0=OP.logical_shift_right, op1=OP.bitwise_and)
                    op.then_inc(s_fn.h, 1)
                    s_fn.n += 1
                    v.wait_ge(s_fn.h, s_fn.n)
                    for q_ in (0, 1):
                        op = v.tensor_scalar(out=ge8[:, q_, 0:nw], in0=nib[:, q_, 0:nw],
                                             scalar1=8.0, scalar2=float(2 ** 31 - 2 ** 25),
                                             op0=OP.is_ge, op1=OP.mult)
                    op.then_inc(s_fn.h, 1)
                    s_fn.n += 1
                    v.wait_ge(s_fn.h, s_fn.n)
                    for q_ in (0, 1):
                        op = v.scalar_tensor_tensor(out=bits[:, q_, 0:nw], in0=nib[:, q_, 0:nw],
                                                    scalar=float(2 ** 22), in1=ge8[:, q_, 0:nw],
                                                    op0=OP.mult, op1=OP.add)
                    op.then_inc(s_fn.h, 1)
                    s_fn.n += 1
                    v.wait_ge(s_fn.h, s_fn.n)
                    for q_ in (0, 1):
                        op = v.tensor_tensor(
                            out=val[:, kp + q_:8 * nw:8],
                            in0=bits[:, q_, 0:nw].bitcast(F32),
                            in1=scP[:, 0:ngr, None].broadcast_to([128, ngr, 16]),
                            op=OP.mult)
                op.then_inc(s_dq.h, 1)
                s_dq.n += 1
                for tr in range(ntr):
                    gi = trbase[ti] + tr
                    v.wait_ge(s_tr.h, gi + 1)
                    if tr == 0 and ti >= 1:
                        pkind, _ = tinfo(ti - 1)
                        if pkind == "gu":
                            v.wait_ge(m_sg.h, 16 * ngu_st)
                        else:
                            v.wait_ge(m_sd.h, 16 * ndn_st)
                    op = v.tensor_copy(stg[:, tr, :], psbf[:, gi % 4, 0:128])
                    op.then_inc(s_ev.h, 1)
                    s_ev.n += 1
                if kind == "gu":
                    ngu_st += 1
                else:
                    ndn_st += 1

        @block.tensor
        def _(t):
            t.wait_ge(m_id.h, 16)
            for ti in range(NT1):
                ntr = ntr_of(ti)
                t.wait_ge(s_dq.h, ti + 1)
                for tr in range(ntr):
                    gi = trbase[ti] + tr
                    if gi >= 4:
                        t.wait_ge(s_ev.h, gi - 3)
                    mm = t.transpose(psbf[:, gi % 4, 0:128],
                                     val[:, tr * 128:(tr + 1) * 128], identb[:, :])
                    mm.then_inc(s_tr.h, 1)
                    s_tr.n += 1

        @block.sync
        def _(s):
            s.dma_start(out=identb[:, :], in_=io["ident"]).then_inc(m_id.h, 16)
            for ti in range(NT1):
                kind, idx = tinfo(ti)
                if "dbg_val" in io and ti == io["dbg_ti"]:
                    s.wait_ge(s_dq.h, ti + 1)
                    s.dma_start(out=io["dbg_val"], in_=val[:, 0:8*W0]).then_inc(io["_dbgf"].h, 16)
                s.wait_ge(s_ev.h, trbase[ti + 1])
                if kind == "gu":
                    s.dma_start(out=io["wgu_sh"][idx].rearrange("c p o -> p c o"),
                                in_=stg[:, 0:DC, :]).then_inc(m_sg.h, 16)
                else:
                    s.dma_start(out=io["wdn_sh"][idx].rearrange("c p o -> p c o"),
                                in_=stg[:, 0:NG, :]).then_inc(m_sd.h, 16)

    with ExitStack() as sf2:
        kf2 = K(nc, block, sf2)
        xbf = kf2.sb("G_xbf", [128, DC, 512], BF16)
        wgu = kf2.sb("G_wgu", [128, 2, 2, DC, 128], BF16)
        wdn = kf2.sb("G_wdn", [128, 2, NG, 128], BF16)
        hbuf = kf2.sb("G_hbuf", [128, NG, 512], BF16)
        gsil = kf2.sb("G_gsil", [128, 2, 512], BF16)
        ybuf = kf2.sb("G_ybuf", [128, DC, 512], F32)
        strsl = kf2.sb("G_strsl", [128, 2, 512], F32)
        scr2 = kf2.sb("G_scr2", [128, 2, 512], F32)
        rowbuf = kf2.sb("G_rows", [1, 8, 512], F32)
        rb2 = kf2.sb("G_rb2", [128, 2, 512], F32)
        rmsb = kf2.sb("G_rmsb", [128, 512], F32)
        atmp = kf2.sb("G_atmp", [128, 512], F32)

        @block.gpsimd
        def _(g):
            g.wait_ge(s_ev.h, trbase[NT1])
            g.wait_ge(s_cc.h, 2)
            for tc in range(NTCH):
                t0 = tc * 512
                if tc >= 1:
                    g.wait_ge(s_gu.h, NG * tc)
                g.dma_start(out=xbf[:, :, :],
                            in_=io["stream_in"].rearrange("(c p) t -> p c t", p=128)[:, :, t0:t0 + 512]
                            ).then_inc(m_xbf.h, 16)

        @block.sync
        def _(s):
            s.wait_ge(s_cc.h, 2)
            for tc in range(NTCH):
                t0 = tc * 512
                for it in range(NG):
                    gi = tc * NG + it
                    if gi >= 2:
                        s.wait_ge(s_gu.h, gi - 1)
                    s.dma_start(out=wgu[:, it % 2, 0, :, :],
                                in_=io["wguT_scr"][it].rearrange("c p o -> p c o")
                                ).then_inc(m_wgu[it % 2].h, 16)
                    s.dma_start(out=wgu[:, it % 2, 1, :, :],
                                in_=io["wguT_scr"][NG + it].rearrange("c p o -> p c o")
                                ).then_inc(m_wgu[it % 2].h, 16)
                for ho in range(DC):
                    g2 = tc * DC + ho
                    if g2 >= 2:
                        s.wait_ge(s_dn.h, g2 - 1)
                    s.dma_start(out=wdn[:, ho % 2, :, :],
                                in_=io["wdn_all"][ho].rearrange("c p o -> p c o")
                                ).then_inc(m_wdn[ho % 2].h, 16)
                for ho in range(DC):
                    g2 = tc * DC + ho
                    if g2 >= 2:
                        s.wait_ge(s_rs.h, g2 - 1)
                    s.dma_start(out=strsl[:, g2 % 2, :],
                                in_=io["stream_in"].rearrange("(c p) t -> p c t", p=128)[:, ho, t0:t0 + 512]
                                ).then_inc(m_str[g2 % 2].h, 16)
                s.wait_ge(s_nrm.h, tc + 1)
                s.dma_start(out=io["out"].rearrange("(c p) t -> p c t", p=128)[:, :, t0:t0 + 512],
                            in_=ybuf[:, :, :]).then_inc(m_out.h, 16)
            s.wait_ge(m_out.h, 16 * NTCH)

        @block.tensor
        def _(t):
            for tc in range(NTCH):
                t.wait_ge(m_xbf.h, 16 * (tc + 1))
                for it in range(NG):
                    gi = tc * NG + it
                    t.wait_ge(m_wgu[it % 2].h, 32 * (gi // 2 + 1))
                    if gi >= 2:
                        t.wait_ge(s_hb.h, gi - 1)
                    if it == 0 and tc >= 1:
                        t.wait_ge(s_nrm.h, tc)
                    for kc in range(DC):
                        mm = t.matmul(ps8[:, 2 * (gi % 2), :], wgu[:, it % 2, 0, kc, :],
                                      xbf[:, kc, :], start=(kc == 0), stop=(kc == DC - 1))
                    for kc in range(DC):
                        mm2 = t.matmul(ps8[:, 2 * (gi % 2) + 1, :], wgu[:, it % 2, 1, kc, :],
                                       xbf[:, kc, :], start=(kc == 0), stop=(kc == DC - 1))
                    mm2.then_inc(s_gu.h, 1)
                    s_gu.n += 1
                for ho in range(DC):
                    g2 = tc * DC + ho
                    t.wait_ge(m_wdn[ho % 2].h, 16 * (g2 // 2 + 1))
                    if g2 >= 2:
                        t.wait_ge(s_ye.h, g2 - 1)
                    for ic in range(NG):
                        t.wait_ge(s_hb.h, tc * NG + ic + 1)
                        mm = t.matmul(ps8[:, 4 + g2 % 2, :], wdn[:, ho % 2, ic, :],
                                      hbuf[:, ic, :], start=(ic == 0), stop=(ic == NG - 1))
                    mm.then_inc(s_dn.h, 1)
                    s_dn.n += 1
                for ho in range(DC):
                    t.wait_ge(s_sq.h, (2 * DC + 2) * tc + ho + 1)
                    if ho == 0 and tc >= 1:
                        t.wait_ge(s_row.h, 4 * tc - 1)
                    mm = t.matmul(ps8[0:1, 6, :], ones_col_f[:, :], scr2[:, ho % 2, :],
                                  start=(ho == 0), stop=(ho == DC - 1))
                    mm.then_inc(s_st2.h, 1)
                    s_st2.n += 1
                t.wait_ge(s_row.h, 4 * tc + 2)
                mm = t.matmul(ps8[:, 7, :], ones_row_f[:, :], rowbuf[:, 4, :],
                              start=True, stop=True)
                mm.then_inc(s_bc.h, 1)
                s_bc.n += 1
                for ho in range(DC):
                    t.wait_ge(s_sq.h, (2 * DC + 2) * tc + DC + 1 + ho + 1)
                    if ho == 0:
                        t.wait_ge(s_row.h, 4 * tc + 1)
                    mm = t.matmul(ps8[0:1, 0, :], ones_col_f[:, :], ybuf[:, ho, :],
                                  start=(ho == 0), stop=(ho == DC - 1))
                    mm2 = t.matmul(ps8[0:1, 6, :], ones_col_f[:, :], scr2[:, ho % 2, :],
                                   start=(ho == 0), stop=(ho == DC - 1))
                    mm2.then_inc(s_st2.h, 1)
                    s_st2.n += 1
                t.wait_ge(s_row.h, 4 * tc + 4)
                mm = t.matmul(ps8[:, 7, :], ones_row_f[:, :], rowbuf[:, 4, :],
                              start=True, stop=True)
                mm2 = t.matmul(ps8[:, 1, :], ones_row_f[:, :], rowbuf[:, 5, :],
                               start=True, stop=True)
                mm2.then_inc(s_bc.h, 1)
                s_bc.n += 1

        @block.scalar
        def _(sc):
            for tc in range(NTCH):
                for it in range(NG):
                    gi = tc * NG + it
                    sc.wait_ge(s_gu.h, gi + 1)
                    if gi >= 2:
                        sc.wait_ge(s_hb.h, gi - 1)
                    op = sc.activation(out=gsil[:, gi % 2, :], in_=ps8[:, 2 * (gi % 2), :],
                                       func=AF.Silu)
                    op.then_inc(s_si.h, 1)
                    s_si.n += 1
                for ho in range(DC):
                    g2 = tc * DC + ho
                    sc.wait_ge(s_ye.h, g2 + 1)
                    if ho >= 2:
                        sc.wait_ge(s_st2.h, 2 * DC * tc + ho - 1)
                    op = sc.activation(out=scr2[:, ho % 2, :], in_=ybuf[:, ho, :],
                                       func=AF.Square)
                    op.then_inc(s_sq.h, 1)
                    s_sq.n += 1
                sc.wait_ge(s_row.h, 4 * tc + 1)
                op = sc.activation(out=rowbuf[:, 6, :], in_=rowbuf[:, 1, :], func=AF.Sqrt)
                op.then_inc(s_sq.h, 1)
                s_sq.n += 1
                for ho in range(DC):
                    g2 = tc * DC + ho
                    sc.wait_ge(s_rs.h, g2 + 1)
                    if ho >= 2:
                        sc.wait_ge(s_st2.h, 2 * DC * tc + DC + ho - 1)
                    op = sc.activation(out=scr2[:, ho % 2, :], in_=ybuf[:, ho, :],
                                       func=AF.Square)
                    op.then_inc(s_sq.h, 1)
                    s_sq.n += 1
                sc.wait_ge(s_row.h, 4 * tc + 3)
                op = sc.activation(out=rowbuf[:, 6, :], in_=rowbuf[:, 1, :], func=AF.Sqrt)
                op.then_inc(s_sq.h, 1)
                s_sq.n += 1

        @block.vector
        def _(v):
            for tc in range(NTCH):
                for it in range(NG):
                    gi = tc * NG + it
                    v.wait_ge(s_si.h, gi + 1)
                    if it == 0 and tc >= 1:
                        v.wait_ge(s_dn.h, DC * tc)
                    op = v.tensor_tensor(out=hbuf[:, it, :], in0=gsil[:, gi % 2, :],
                                         in1=ps8[:, 2 * (gi % 2) + 1, :], op=OP.mult)
                    op.then_inc(s_hb.h, 1)
                    s_hb.n += 1
                for ho in range(DC):
                    g2 = tc * DC + ho
                    v.wait_ge(s_dn.h, g2 + 1)
                    if ho == 0:
                        v.wait_ge(m_out.h, 16 * tc)
                    op = v.tensor_copy(ybuf[:, ho, :], ps8[:, 4 + g2 % 2, :])
                    op.then_inc(s_ye.h, 1)
                    s_ye.n += 1
                v.wait_ge(s_st2.h, 2 * DC * tc + DC)
                v.tensor_scalar(out=rowbuf[:, 0, :], in0=ps8[0:1, 6, :],
                                scalar1=1.0 / D, scalar2=None, op0=OP.mult)
                op = v.tensor_scalar(out=rowbuf[:, 1, :], in0=rowbuf[:, 0, :],
                                     scalar1=1e-6, scalar2=None, op0=OP.add)
                op.then_inc(s_row.h, 1)
                s_row.n += 1
                v.wait_ge(s_sq.h, (2 * DC + 2) * tc + DC + 1)
                op = v.reciprocal(rowbuf[:, 4, :], rowbuf[:, 6, :])
                op.then_inc(s_row.h, 1)
                s_row.n += 1
                v.wait_ge(s_bc.h, 2 * tc + 1)
                v.tensor_copy(rmsb[:, :], ps8[:, 7, :])
                for ho in range(DC):
                    g2 = tc * DC + ho
                    v.wait_ge(m_str[g2 % 2].h, 16 * (g2 // 2 + 1))
                    v.tensor_tensor(out=atmp[:, :], in0=ybuf[:, ho, :],
                                    in1=rmsb[:, :], op=OP.mult)
                    op = v.tensor_tensor(out=ybuf[:, ho, :], in0=atmp[:, :],
                                         in1=strsl[:, g2 % 2, :], op=OP.add)
                    op.then_inc(s_rs.h, 1)
                    s_rs.n += 1
                v.wait_ge(s_st2.h, 2 * DC * (tc + 1))
                v.tensor_scalar(out=rowbuf[:, 2, :], in0=ps8[0:1, 0, :],
                                scalar1=1.0 / D, scalar2=None, op0=OP.mult)
                v.tensor_scalar(out=rowbuf[:, 3, :], in0=ps8[0:1, 6, :],
                                scalar1=1.0 / D, scalar2=None, op0=OP.mult)
                v.tensor_tensor(out=rowbuf[:, 0, :], in0=rowbuf[:, 2, :],
                                in1=rowbuf[:, 2, :], op=OP.mult)
                v.tensor_tensor(out=rowbuf[:, 3, :], in0=rowbuf[:, 3, :],
                                in1=rowbuf[:, 0, :], op=OP.subtract)
                op = v.tensor_scalar(out=rowbuf[:, 1, :], in0=rowbuf[:, 3, :],
                                     scalar1=1e-5, scalar2=None, op0=OP.add)
                op.then_inc(s_row.h, 1)
                s_row.n += 1
                v.wait_ge(s_sq.h, (2 * DC + 2) * (tc + 1))
                v.reciprocal(rowbuf[:, 4, :], rowbuf[:, 6, :])
                op = v.tensor_tensor(out=rowbuf[:, 5, :], in0=rowbuf[:, 2, :],
                                     in1=rowbuf[:, 4, :], op=OP.mult)
                op.then_inc(s_row.h, 1)
                s_row.n += 1
                v.wait_ge(s_bc.h, 2 * tc + 2)
                v.tensor_copy(rb2[:, 0, :], ps8[:, 7, :])
                v.tensor_copy(rb2[:, 1, :], ps8[:, 1, :])
                for ho in range(DC):
                    v.tensor_tensor(out=atmp[:, :], in0=ybuf[:, ho, :],
                                    in1=rb2[:, 0, :], op=OP.mult)
                    op = v.tensor_tensor(out=ybuf[:, ho, :], in0=atmp[:, :],
                                         in1=rb2[:, 1, :], op=OP.subtract)
                op.then_inc(s_nrm.h, 1)
                s_nrm.n += 1


def build_l0ffn(cfg):
    c = derive(cfg)
    D, TH, I0, DC = c["D"], c["TH"], c["I0"], c["DC"]
    NG = I0 // 128
    NHW = max(1, D // 512)
    HB = min(D, 512)
    nc = bass.Bass("TRN2", target_bir_lowering=False, num_devices=8,
                   detect_race_conditions=False)
    io = {}
    NSH = c["NSH"]
    NGU_loc = 2 * NG // NSH
    NDN_loc = DC // NSH
    io["stream_in"] = nc.dram_tensor("stream_in", [D, TH], F32, kind="ExternalInput").ap()
    io["gup"] = nc.dram_tensor("gup", [NGU_loc * 128, D // 8], I32, kind="ExternalInput").ap()
    io["gusT"] = nc.dram_tensor("gusT", [NGU_loc * 128, D // 128], F16, kind="ExternalInput").ap()
    io["dnp"] = nc.dram_tensor("dnp", [NDN_loc * 128, I0 // 8], I32, kind="ExternalInput").ap()
    io["dnsT"] = nc.dram_tensor("dnsT", [NDN_loc * 128, I0 // 128], F16, kind="ExternalInput").ap()
    io["ident"] = nc.dram_tensor("ident", [128, 128], BF16, kind="ExternalInput").ap()
    io["out"] = nc.dram_tensor("stream_out", [D, TH], F32, kind="ExternalOutput").ap()
    io["wgu_sh"] = nc.dram_tensor("wgu_sh", [NGU_loc, DC, 128, 128], BF16).ap()
    io["wdn_sh"] = nc.dram_tensor("wdn_sh", [NDN_loc, NG, 128, 128], BF16).ap()
    io["wguT_scr"] = nc.dram_tensor("wguT_scr", [2 * NG, DC, 128, 128], BF16).ap()
    io["wdn_all"] = nc.dram_tensor("wdn_all", [DC, NG, 128, 128], BF16).ap()
    with ExitStack() as stack:
        block = stack.enter_context(nc.Block())
        k = K(nc, block, stack)
        emit_l0ffn(k, c, io)
    return nc


def prep_l0ffn_inputs(cfg, inp, streams):
    c = derive(cfg)
    gup = np.ascontiguousarray(np.asarray(inp["L0_ffn_gate_up_packed"], dtype=np.int32))
    gusT = np.ascontiguousarray(np.asarray(inp["L0_ffn_gate_up_scales"]).T)
    dnp = np.ascontiguousarray(np.asarray(inp["L0_ffn_down_packed"], dtype=np.int32))
    dnsT = np.ascontiguousarray(np.asarray(inp["L0_ffn_down_scales"]).T)
    ident = np.eye(128, dtype=BF)
    NSH = c["NSH"]
    RGU = gup.shape[0] // NSH
    RDN = dnp.shape[0] // NSH
    maps = []
    for core in range(8):
        sh = core % NSH
        maps.append(dict(
            stream_in=streams[core],
            gup=np.ascontiguousarray(gup[sh * RGU:(sh + 1) * RGU]),
            gusT=np.ascontiguousarray(gusT[sh * RGU:(sh + 1) * RGU]),
            dnp=np.ascontiguousarray(dnp[sh * RDN:(sh + 1) * RDN]),
            dnsT=np.ascontiguousarray(dnsT[sh * RDN:(sh + 1) * RDN]),
            ident=ident))
    return maps
def emit_l1attn(k, c, io):
    nc, block = k.nc, k.block
    D, DC = c["D"], c["DC"]
    TH = c["TH"]
    E3 = 3 * DC
    RH = DC // 4
    NTCH = TH // 512
    RPT = 2 * RH + DC

    ones_row_f = k.sb("ones_row_f", [1, 128], F32)
    ones_col_f = k.sb("ones_col_f", [128, 1], F32)
    identf = k.sb("identf_sb", [128, 128], F32)
    binT = k.sb("binT_sb", [128, E3], F32)
    boutT = k.sb("boutT_sb", [128, DC], F32)
    aacc = k.sb("aacc", [128, DC], F32)
    aA = k.sb("aA", [128, DC], F32)
    art = k.sb("art", [128, 128], F32)
    ps8 = k.ps("ps8", [128, 8, 512])

    m_cst = k.sem("m_cst")
    m_xbf = k.sem("m_xbf")
    m_win = [k.sem("m_win0"), k.sem("m_win1")]
    m_cs = k.sem("m_cs")
    m_qst = [k.sem("m_qst0"), k.sem("m_qst1")]
    m_ain = k.sem("m_ain")
    m_aout = k.sem("m_aout")
    m_qld = k.sem("m_qld")
    m_wout = [k.sem("m_wout0"), k.sem("m_wout1")]
    m_str = [k.sem("m_str0"), k.sem("m_str1")]
    m_out = k.sem("m_out")

    s_qkv = k.sem("s_qkv")
    s_ev1 = k.sem("s_ev1")
    s_rp = k.sem("s_rp")
    s_kst = k.sem("s_kst")
    s_nrw = k.sem("s_nrw")
    s_sqr = k.sem("s_sqr")
    s_bc1 = k.sem("s_bc1")
    s_ap = k.sem("s_ap")
    s_tra = k.sem("s_tra")
    s_aev = k.sem("s_aev")
    s_red = k.sem("s_red")
    s_cc = k.sem("s_cc")
    s_aok = k.sem("s_aok")
    s_yb = k.sem("s_yb")
    s_op = k.sem("s_op")
    s_ev2 = k.sem("s_ev2")
    s_rs = k.sem("s_rs")
    s_sq = k.sem("s_sq")
    s_st2 = k.sem("s_st2")
    s_row = k.sem("s_row")
    s_bc = k.sem("s_bc")
    s_nrm = k.sem("s_nrm")

    io["_m_ain"] = m_ain
    io["_sccn"] = s_cc
    if "dbg_k" in io:
        io["_dbg"] = k.sem("m_dbg")
    with ExitStack() as s1:
        k1 = K(nc, block, s1)
        xbf = k1.sb("H_xbf", [128, DC, 512], BF16)
        winb = k1.sb("H_win", [128, 2, DC, 128], BF16)
        kfull = k1.sb("H_k", [128, DC, 512], F32)
        vstg = k1.sb("H_v", [128, DC, 512], F32)
        qstg = k1.sb("H_q", [128, DC, 512], F32)
        cosb = k1.sb("H_cos", [128, RH, 512], F32)
        sinb = k1.sb("H_sin", [128, RH, 512], F32)
        rt1 = k1.sb("H_rt1", [128, 512], F32)
        rt2 = k1.sb("H_rt2", [128, 512], F32)
        rt3 = k1.sb("H_rt3", [128, 512], F32)
        qbf_st = k1.sb("H_qbf", [128, 2, 512], BF16)
        rrow = k1.sb("H_rrow", [1, 512], F32)
        rbb = k1.sb("H_rbb", [128, 512], F32)
        scr1 = k1.sb("H_scr1", [128, 2, 512], F32)

        @block.gpsimd
        def _(g):
            for tc in range(NTCH):
                t0 = tc * 512
                if tc >= 1:
                    g.wait_ge(s_qkv.h, E3 * tc)
                g.dma_start(out=xbf[:, :, :],
                            in_=io["stream_in"].rearrange("(c p) t -> p c t", p=128)[:, :, t0:t0 + 512]
                            ).then_inc(m_xbf.h, 16)
                for et in range(E3):
                    gi = tc * E3 + et
                    if gi >= 2:
                        g.wait_ge(s_qkv.h, gi - 1)
                    g.dma_start(out=winb[:, et % 2, :, :],
                                in_=io["w_inT"].rearrange("(c p) o -> p c o", p=128)[:, :, et * 128:(et + 1) * 128]
                                ).then_inc(m_win[et % 2].h, 16)
            g.wait_ge(m_ain.h, 16)
            g.collective_compute("AllReduce", OP.add,
                                 replica_groups=[[0, 1], [2, 3], [4, 5], [6, 7]],
                                 ins=[io["cc_a_in"].ap().opt()],
                                 outs=[io["cc_a_out"].ap().opt()]).then_inc(s_cc.h, 1)
            s_cc.n += 1

        @block.sync
        def _(s):
            s.dma_start(out=binT[:, :], in_=io["binT"]).then_inc(m_cst.h, 16)
            s.dma_start(out=boutT[:, :], in_=io["boutT"]).then_inc(m_cst.h, 16)
            s.dma_start(out=identf[:, :], in_=io["identf"]).then_inc(m_cst.h, 16)
            for tc in range(NTCH):
                t0 = tc * 512
                if tc >= 1:
                    s.wait_ge(s_rp.h, RPT * tc)
                s.dma_start(out=cosb[:, :, :],
                            in_=io["cosT1"].rearrange("(c p) t -> p c t", p=128)[:, :, t0:t0 + 512]
                            ).then_inc(m_cs.h, 16)
                s.dma_start(out=sinb[:, :, :],
                            in_=io["sinT1"].rearrange("(c p) t -> p c t", p=128)[:, :, t0:t0 + 512]
                            ).then_inc(m_cs.h, 16)
                for qi in range(DC):
                    s.wait_ge(s_rp.h, RPT * tc + 2 * RH + qi + 1)
                    s.dma_start(out=io["q_scr"][qi * 128:(qi + 1) * 128, t0:t0 + 512],
                                in_=qbf_st[:, qi % 2, :]).then_inc(m_qst[qi % 2].h, 16)
                if tc == 0 and "dbg_k" in io:
                    s.wait_ge(s_ap.h, 1)
                    s.dma_start(out=io["dbg_k"], in_=kfull[:, :, :]).then_inc(io["_dbg"].h, 16)
                    s.dma_start(out=io["dbg_v"], in_=vstg[:, :, :]).then_inc(io["_dbg"].h, 16)
                    s.dma_start(out=io["dbg_r"], in_=rbb[:, :]).then_inc(io["_dbg"].h, 16)
                    s.dma_start(out=io["dbg_a0"], in_=aacc[:, :]).then_inc(io["_dbg"].h, 16)
                    io["_dbg"].n += 64
            s.wait_ge(s_ap.h, NTCH)
            s.dma_start(out=io["cc_a_in"].ap(), in_=aacc[:, :]).then_inc(m_ain.h, 16)
            s.wait_ge(s_cc.h, 1)
            s.dma_start(out=aA[:, :], in_=io["cc_a_out"].ap()).then_inc(m_aout.h, 16)

        @block.tensor
        def _(t):
            t.wait_ge(m_cst.h, 48)
            for tc in range(NTCH):
                t.wait_ge(m_xbf.h, 16 * (tc + 1))
                for et in range(E3):
                    gi = tc * E3 + et
                    t.wait_ge(m_win[et % 2].h, 16 * (gi // 2 + 1))
                    if gi >= 4:
                        t.wait_ge(s_ev1.h, gi - 3)
                    for kc in range(DC):
                        mm = t.matmul(ps8[:, gi % 4, :], winb[:, et % 2, kc, :],
                                      xbf[:, kc, :], start=(kc == 0), stop=(kc == DC - 1))
                    mm.then_inc(s_qkv.h, 1)
                    s_qkv.n += 1
                for kc in range(DC):
                    t.wait_ge(s_sqr.h, (DC + 1) * tc + kc + 1)
                    if kc == 0 and tc >= 1:
                        t.wait_ge(s_nrw.h, 2 * tc)
                    mm = t.matmul(ps8[0:1, 6, :], ones_col_f[:, :], scr1[:, kc % 2, :],
                                  start=(kc == 0), stop=(kc == DC - 1))
                    mm.then_inc(s_kst.h, 1)
                    s_kst.n += 1
                t.wait_ge(s_nrw.h, 2 * tc + 2)
                mm = t.matmul(ps8[:, 7, :], ones_row_f[:, :], rrow[:, :],
                              start=True, stop=True)
                mm.then_inc(s_bc1.h, 1)
                s_bc1.n += 1

        @block.scalar
        def _(sc):
            sc.wait_ge(m_cst.h, 48)
            for tc in range(NTCH):
                for et in range(E3):
                    gi = tc * E3 + et
                    sc.wait_ge(s_qkv.h, gi + 1)
                    if et == 0 and tc >= 1:
                        sc.wait_ge(s_ap.h, tc)
                    bias = binT[:, et:et + 1]
                    if et < DC:
                        dst = qstg[:, et, :]
                    elif et < 2 * DC:
                        dst = kfull[:, et - DC, :]
                    else:
                        dst = vstg[:, et - 2 * DC, :]
                    op = sc.activation(out=dst, in_=ps8[:, gi % 4, :],
                                       func=AF.Identity, bias=bias)
                    op.then_inc(s_ev1.h, 1)
                    s_ev1.n += 1
                for kc in range(DC):
                    if kc < 2 * RH:
                        sc.wait_ge(s_rp.h, RPT * tc + RH)
                    if kc >= 2:
                        sc.wait_ge(s_kst.h, DC * tc + kc - 1)
                    op = sc.activation(out=scr1[:, kc % 2, :], in_=kfull[:, kc, :],
                                       func=AF.Square)
                    op.then_inc(s_sqr.h, 1)
                    s_sqr.n += 1
                sc.wait_ge(s_nrw.h, 2 * tc + 1)
                op = sc.activation(out=rrow[:, :], in_=rrow[:, :], func=AF.Sqrt)
                op.then_inc(s_sqr.h, 1)
                s_sqr.n += 1

        @block.vector
        def _(v):
            v.memset(ones_row_f[:, :], 1.0)
            v.memset(ones_col_f[:, :], 1.0)
            v.memset(aacc[:, :], 0.0)
            for tc in range(NTCH):
                v.wait_ge(m_cs.h, 32 * (tc + 1))
                for cp in range(RH):
                    v.wait_ge(s_ev1.h, tc * E3 + DC + RH + cp + 1)
                    lo = kfull[:, cp, :]
                    hi = kfull[:, cp + RH, :]
                    v.tensor_tensor(out=rt1[:, :], in0=lo, in1=cosb[:, cp, :], op=OP.mult)
                    v.tensor_tensor(out=rt2[:, :], in0=hi, in1=sinb[:, cp, :], op=OP.mult)
                    v.tensor_tensor(out=rt3[:, :], in0=rt1[:, :], in1=rt2[:, :], op=OP.subtract)
                    v.tensor_tensor(out=rt1[:, :], in0=hi, in1=cosb[:, cp, :], op=OP.mult)
                    v.tensor_tensor(out=rt2[:, :], in0=lo, in1=sinb[:, cp, :], op=OP.mult)
                    v.tensor_tensor(out=hi, in0=rt1[:, :], in1=rt2[:, :], op=OP.add)
                    op = v.tensor_copy(lo, rt3[:, :])
                    op.then_inc(s_rp.h, 1)
                    s_rp.n += 1
                for cp in range(RH):
                    v.wait_ge(s_ev1.h, tc * E3 + RH + cp + 1)
                    lo = qstg[:, cp, :]
                    hi = qstg[:, cp + RH, :]
                    v.tensor_tensor(out=rt1[:, :], in0=lo, in1=cosb[:, cp, :], op=OP.mult)
                    v.tensor_tensor(out=rt2[:, :], in0=hi, in1=sinb[:, cp, :], op=OP.mult)
                    v.tensor_tensor(out=rt3[:, :], in0=rt1[:, :], in1=rt2[:, :], op=OP.subtract)
                    v.tensor_tensor(out=rt1[:, :], in0=hi, in1=cosb[:, cp, :], op=OP.mult)
                    v.tensor_tensor(out=rt2[:, :], in0=lo, in1=sinb[:, cp, :], op=OP.mult)
                    v.tensor_tensor(out=hi, in0=rt1[:, :], in1=rt2[:, :], op=OP.add)
                    op = v.tensor_copy(lo, rt3[:, :])
                    op.then_inc(s_rp.h, 1)
                    s_rp.n += 1
                for qi in range(DC):
                    gq_i = tc * DC + qi
                    if gq_i >= 2:
                        v.wait_ge(m_qst[gq_i % 2].h, 16 * (gq_i // 2))
                    if qi >= 2 * RH:
                        v.wait_ge(s_ev1.h, tc * E3 + qi + 1)
                    op = v.tensor_copy(qbf_st[:, qi % 2, :], qstg[:, qi, :])
                    op.then_inc(s_rp.h, 1)
                    s_rp.n += 1
                v.wait_ge(s_kst.h, DC * (tc + 1))
                op = v.tensor_copy(rrow[:, :], ps8[0:1, 6, :])
                op.then_inc(s_nrw.h, 1)
                s_nrw.n += 1
                v.wait_ge(s_sqr.h, (DC + 1) * (tc + 1))
                v.tensor_scalar(out=rrow[:, :], in0=rrow[:, :],
                                scalar1=1e-12, scalar2=None, op0=OP.max)
                op = v.reciprocal(rrow[:, :], rrow[:, :])
                op.then_inc(s_nrw.h, 1)
                s_nrw.n += 1
                v.wait_ge(s_bc1.h, tc + 1)
                v.tensor_copy(rbb[:, :], ps8[:, 7, :])
                v.wait_ge(s_ev1.h, (tc + 1) * E3)
                for kc in range(DC):
                    v.tensor_tensor(out=rt1[:, :], in0=kfull[:, kc, :],
                                    in1=vstg[:, kc, :], op=OP.mult)
                    v.tensor_tensor(out=rt2[:, :], in0=rt1[:, :], in1=rbb[:, :], op=OP.mult)
                    op = v.reduce_sum(rt3[:, 0:1], rt2[:, :], axis=AX.X)
                    op.then_inc(s_red.h, 1)
                    s_red.n += 1
                    v.wait_ge(s_red.h, s_red.n)
                    op = v.tensor_tensor(out=aacc[:, kc:kc + 1], in0=aacc[:, kc:kc + 1],
                                         in1=rt3[:, 0:1], op=OP.add)
                op.then_inc(s_ap.h, 1)
                s_ap.n += 1
            v.wait_ge(m_aout.h, 16)
            op = v.tensor_copy(rt1[:, 0:1], aA[:, 0:1])
            op.then_inc(s_aok.h, 1)
            s_aok.n += 1

    with ExitStack() as s2:
        k2 = K(nc, block, s2)
        qbf = k2.sb("I_qbf", [128, DC, 512], BF16)
        ybf = k2.sb("I_ybf", [128, DC, 512], BF16)
        woutb = k2.sb("I_wout", [128, 2, DC, 128], BF16)
        sD = k2.sb("I_sD", [128, DC, 512], F32)
        strsl = k2.sb("I_strsl", [128, 2, 512], F32)
        evst = k2.sb("I_evst", [128, 2, 512], F32)
        scr2 = k2.sb("I_scr2", [128, 2, 512], F32)
        rowbuf = k2.sb("I_rows", [1, 8, 512], F32)
        rb2 = k2.sb("I_rb2", [128, 2, 512], F32)

        @block.gpsimd
        def _(g):
            g.wait_ge(s_aok.h, 1)
            for tc in range(NTCH):
                for ot in range(DC):
                    gi = tc * DC + ot
                    if gi >= 2:
                        g.wait_ge(s_op.h, gi - 1)
                    g.dma_start(out=woutb[:, ot % 2, :, :],
                                in_=io["w_outT"].rearrange("(c p) o -> p c o", p=128)[:, :, ot * 128:(ot + 1) * 128]
                                ).then_inc(m_wout[ot % 2].h, 16)

        @block.sync
        def _(s):
            s.wait_ge(s_aok.h, 1)
            for tc in range(NTCH):
                t0 = tc * 512
                if tc >= 1:
                    s.wait_ge(s_yb.h, DC * tc)
                s.dma_start(out=qbf[:, :, :],
                            in_=io["q_scr"].rearrange("(c p) t -> p c t", p=128)[:, :, t0:t0 + 512]
                            ).then_inc(m_qld.h, 16)
                for ot in range(DC):
                    gi = tc * DC + ot
                    if gi >= 2:
                        s.wait_ge(s_rs.h, gi - 1)
                    s.dma_start(out=strsl[:, gi % 2, :],
                                in_=io["stream_in"].rearrange("(c p) t -> p c t", p=128)[:, ot, t0:t0 + 512]
                                ).then_inc(m_str[gi % 2].h, 16)
                s.wait_ge(s_nrm.h, tc + 1)
                s.dma_start(out=io["out"].rearrange("(c p) t -> p c t", p=128)[:, :, t0:t0 + 512],
                            in_=sD[:, :, :]).then_inc(m_out.h, 16)
            s.wait_ge(m_out.h, 16 * NTCH)

        @block.tensor
        def _(t):
            for tc in range(NTCH):
                for ot in range(DC):
                    gi = tc * DC + ot
                    t.wait_ge(m_wout[ot % 2].h, 16 * (gi // 2 + 1))
                    t.wait_ge(s_yb.h, DC * tc + DC)
                    if gi >= 2:
                        t.wait_ge(s_ev2.h, gi - 1)
                    for dc in range(DC):
                        mm = t.matmul(ps8[:, 4 + gi % 2, :], woutb[:, ot % 2, dc, :],
                                      ybf[:, dc, :], start=(dc == 0), stop=(dc == DC - 1))
                    mm.then_inc(s_op.h, 1)
                    s_op.n += 1
                for ot in range(DC):
                    t.wait_ge(s_sq.h, (DC + 1) * tc + ot + 1)
                    if ot == 0 and tc >= 1:
                        t.wait_ge(s_row.h, 2 * tc - 1)
                    mm = t.matmul(ps8[0:1, 0, :], ones_col_f[:, :], sD[:, ot, :],
                                  start=(ot == 0), stop=(ot == DC - 1))
                    mm2 = t.matmul(ps8[0:1, 6, :], ones_col_f[:, :], scr2[:, ot % 2, :],
                                   start=(ot == 0), stop=(ot == DC - 1))
                    mm2.then_inc(s_st2.h, 1)
                    s_st2.n += 1
                t.wait_ge(s_row.h, 2 * tc + 2)
                if tc >= 1:
                    t.wait_ge(s_nrm.h, tc)
                mm = t.matmul(ps8[:, 7, :], ones_row_f[:, :], rowbuf[:, 4, :],
                              start=True, stop=True)
                mm2 = t.matmul(ps8[:, 1, :], ones_row_f[:, :], rowbuf[:, 5, :],
                               start=True, stop=True)
                mm2.then_inc(s_bc.h, 1)
                s_bc.n += 1

        @block.scalar
        def _(sc):
            for tc in range(NTCH):
                for ot in range(DC):
                    gi = tc * DC + ot
                    sc.wait_ge(s_op.h, gi + 1)
                    if gi >= 2:
                        sc.wait_ge(s_rs.h, gi - 1)
                    op = sc.activation(out=evst[:, gi % 2, :], in_=ps8[:, 4 + gi % 2, :],
                                       func=AF.Identity, bias=boutT[:, ot:ot + 1])
                    op.then_inc(s_ev2.h, 1)
                    s_ev2.n += 1
                for ot in range(DC):
                    gi = tc * DC + ot
                    sc.wait_ge(s_rs.h, gi + 1)
                    if ot >= 2:
                        sc.wait_ge(s_st2.h, DC * tc + ot - 1)
                    op = sc.activation(out=scr2[:, ot % 2, :], in_=sD[:, ot, :],
                                       func=AF.Square)
                    op.then_inc(s_sq.h, 1)
                    s_sq.n += 1
                sc.wait_ge(s_row.h, 2 * tc + 1)
                op = sc.activation(out=rowbuf[:, 6, :], in_=rowbuf[:, 1, :], func=AF.Sqrt)
                op.then_inc(s_sq.h, 1)
                s_sq.n += 1

        @block.vector
        def _(v):
            for tc in range(NTCH):
                v.wait_ge(m_qld.h, 16 * (tc + 1))
                for dc in range(DC):
                    if tc >= 1 and dc == 0:
                        v.wait_ge(s_op.h, DC * tc)
                    op = v.tensor_scalar(out=ybf[:, dc, :], in0=qbf[:, dc, :],
                                         scalar1=aA[:, dc:dc + 1], scalar2=None,
                                         op0=OP.mult)
                    op.then_inc(s_yb.h, 1)
                    s_yb.n += 1
                for ot in range(DC):
                    gi = tc * DC + ot
                    v.wait_ge(s_ev2.h, gi + 1)
                    v.wait_ge(m_str[gi % 2].h, 16 * (gi // 2 + 1))
                    if ot == 0:
                        v.wait_ge(m_out.h, 16 * tc)
                    op = v.scalar_tensor_tensor(out=sD[:, ot, :], in0=strsl[:, gi % 2, :],
                                                scalar=2.0, in1=evst[:, gi % 2, :],
                                                op0=OP.mult, op1=OP.add)
                    op.then_inc(s_rs.h, 1)
                    s_rs.n += 1
                v.wait_ge(s_st2.h, DC * (tc + 1))
                v.tensor_scalar(out=rowbuf[:, 2, :], in0=ps8[0:1, 0, :],
                                scalar1=1.0 / D, scalar2=None, op0=OP.mult)
                v.tensor_scalar(out=rowbuf[:, 3, :], in0=ps8[0:1, 6, :],
                                scalar1=1.0 / D, scalar2=None, op0=OP.mult)
                v.tensor_tensor(out=rowbuf[:, 0, :], in0=rowbuf[:, 2, :],
                                in1=rowbuf[:, 2, :], op=OP.mult)
                v.tensor_tensor(out=rowbuf[:, 3, :], in0=rowbuf[:, 3, :],
                                in1=rowbuf[:, 0, :], op=OP.subtract)
                op = v.tensor_scalar(out=rowbuf[:, 1, :], in0=rowbuf[:, 3, :],
                                     scalar1=1e-5, scalar2=None, op0=OP.add)
                op.then_inc(s_row.h, 1)
                s_row.n += 1
                v.wait_ge(s_sq.h, (DC + 1) * (tc + 1))
                v.reciprocal(rowbuf[:, 4, :], rowbuf[:, 6, :])
                op = v.tensor_tensor(out=rowbuf[:, 5, :], in0=rowbuf[:, 2, :],
                                     in1=rowbuf[:, 4, :], op=OP.mult)
                op.then_inc(s_row.h, 1)
                s_row.n += 1
                v.wait_ge(s_bc.h, tc + 1)
                v.tensor_copy(rb2[:, 0, :], ps8[:, 7, :])
                v.tensor_copy(rb2[:, 1, :], ps8[:, 1, :])
                for ot in range(DC):
                    v.tensor_tensor(out=scr2[:, ot % 2, :], in0=sD[:, ot, :],
                                    in1=rb2[:, 0, :], op=OP.mult)
                    op = v.tensor_tensor(out=sD[:, ot, :], in0=scr2[:, ot % 2, :],
                                         in1=rb2[:, 1, :], op=OP.subtract)
                op.then_inc(s_nrm.h, 1)
                s_nrm.n += 1


def qkv_perm(D):
    rot = D // 2
    p = np.arange(D)
    half = rot // 2
    p[:half] = np.arange(0, rot, 2)
    p[half:rot] = np.arange(1, rot, 2)
    return p


def build_l1attn(cfg):
    c = derive(cfg)
    D, TH, DC = c["D"], c["TH"], c["DC"]
    E = 3 * D
    nc = bass.Bass("TRN2", target_bir_lowering=False, num_devices=8,
                   detect_race_conditions=False)
    io = {}
    io["stream_in"] = nc.dram_tensor("stream_in", [D, TH], F32, kind="ExternalInput").ap()
    io["w_inT"] = nc.dram_tensor("w_inT", [D, E], F32, kind="ExternalInput").ap()
    io["binT"] = nc.dram_tensor("binT", [128, 3 * DC], F32, kind="ExternalInput").ap()
    io["w_outT"] = nc.dram_tensor("w_outT", [D, D], F32, kind="ExternalInput").ap()
    io["boutT"] = nc.dram_tensor("boutT", [128, DC], F32, kind="ExternalInput").ap()
    io["cosT1"] = nc.dram_tensor("cosT1", [D // 4, TH], F32, kind="ExternalInput").ap()
    io["sinT1"] = nc.dram_tensor("sinT1", [D // 4, TH], F32, kind="ExternalInput").ap()
    io["identf"] = nc.dram_tensor("identf", [128, 128], F32, kind="ExternalInput").ap()
    io["out"] = nc.dram_tensor("stream_out", [D, TH], F32, kind="ExternalOutput").ap()
    io["q_scr"] = nc.dram_tensor("q_scr", [D, TH], BF16).ap()
    io["cc_a_in"] = nc.dram_tensor("cc_a_in", [128, DC], F32)
    io["cc_a_out"] = nc.dram_tensor("cc_a_out", [128, DC], F32)
    with ExitStack() as stack:
        block = stack.enter_context(nc.Block())
        k = K(nc, block, stack)
        emit_l1attn(k, c, io)
    return nc


def prep_l1attn_inputs(cfg, inp, streams):
    c = derive(cfg)
    D, TH, DC = c["D"], c["TH"], c["DC"]
    rot = D // 2
    W_in = np.asarray(inp["L1_attn_W_in"], dtype=np.float32)
    b_in = np.asarray(inp["L1_attn_b_in"], dtype=np.float32)
    W_out = np.asarray(inp["L1_attn_W_out"], dtype=np.float32)
    b_out = np.asarray(inp["L1_attn_b_out"], dtype=np.float32)
    p = qkv_perm(D)
    permE = np.concatenate([p, D + p, 2 * D + p])
    w_inT = np.ascontiguousarray(W_in.T[:, permE])
    binP = b_in[permE]
    binT = np.ascontiguousarray(binP.reshape(3 * DC, 128).T)
    w_outT = np.ascontiguousarray(W_out.T[p, :])
    boutT = np.ascontiguousarray(b_out.reshape(DC, 128).T)
    identf = np.eye(128, dtype=np.float32)
    half = rot // 2
    inv = (10000.0 ** (-2.0 * np.arange(half) / rot))
    maps = []
    for core in range(8):
        b, g = core // 2, core % 2
        pos = np.arange(g * TH, (g + 1) * TH, dtype=np.float64)
        fr = pos[None, :] * inv[:, None]
        maps.append(dict(stream_in=streams[core], w_inT=w_inT, binT=binT,
                         w_outT=w_outT, boutT=boutT,
                         cosT1=np.cos(fr).astype(np.float32),
                         sinT1=np.sin(fr).astype(np.float32),
                         identf=identf))
    return maps
def emit_l1ffn(k, c, io):
    nc, block = k.nc, k.block
    D, DC = c["D"], c["DC"]
    TH = c["TH"]
    I1 = c["I1"]
    NG = I1 // 128
    NTCH = TH // 512

    ones_row_f = k.sb("ones_row_f", [1, 128], F32)
    ones_col_f = k.sb("ones_col_f", [128, 1], F32)
    ps8 = k.ps("ps8", [128, 8, 512])

    m_xbf = k.sem("m_xbf")
    m_wgu = [k.sem("m_wgu0"), k.sem("m_wgu1")]
    m_wdn = [k.sem("m_wdn0"), k.sem("m_wdn1")]
    m_str = [k.sem("m_str0"), k.sem("m_str1")]
    m_out = k.sem("m_out")

    s_gu = k.sem("s_gu")
    s_si = k.sem("s_si")
    s_hb = k.sem("s_hb")
    s_dn = k.sem("s_dn")
    s_ye = k.sem("s_ye")
    s_rs = k.sem("s_rs")
    s_sq = k.sem("s_sq")
    s_st2 = k.sem("s_st2")
    s_row = k.sem("s_row")
    s_bc = k.sem("s_bc")
    s_nrm = k.sem("s_nrm")

    with ExitStack() as sf2:
        kf2 = K(nc, block, sf2)
        xbf = kf2.sb("J_xbf", [128, DC, 512], BF16)
        wgu = kf2.sb("J_wgu", [128, 2, 2, DC, 128], BF16)
        wdn = kf2.sb("J_wdn", [128, 2, NG, 128], BF16)
        hbuf = kf2.sb("J_hbuf", [128, NG, 512], BF16)
        gsil = kf2.sb("J_gsil", [128, 2, 512], BF16)
        ybuf = kf2.sb("J_ybuf", [128, DC, 512], F32)
        strsl = kf2.sb("J_strsl", [128, 2, 512], F32)
        scr2 = kf2.sb("J_scr2", [128, 2, 512], F32)
        rowbuf = kf2.sb("J_rows", [1, 8, 512], F32)
        rb2 = kf2.sb("J_rb2", [128, 2, 512], F32)
        atmp = kf2.sb("J_atmp", [128, 512], F32)

        @block.gpsimd
        def _(g):
            for tc in range(NTCH):
                t0 = tc * 512
                if tc >= 1:
                    g.wait_ge(s_gu.h, NG * tc)
                g.dma_start(out=xbf[:, :, :],
                            in_=io["stream_in"].rearrange("(c p) t -> p c t", p=128)[:, :, t0:t0 + 512]
                            ).then_inc(m_xbf.h, 16)
                for it in range(NG):
                    gi = tc * NG + it
                    if gi >= 2:
                        g.wait_ge(s_gu.h, gi - 1)
                    g.dma_start(out=wgu[:, it % 2, 0, :, :],
                                in_=io["w_guT"].rearrange("(c p) o -> p c o", p=128)[:, :, it * 128:(it + 1) * 128]
                                ).then_inc(m_wgu[it % 2].h, 16)
                    g.dma_start(out=wgu[:, it % 2, 1, :, :],
                                in_=io["w_guT"].rearrange("(c p) o -> p c o", p=128)[:, :, I1 + it * 128:I1 + (it + 1) * 128]
                                ).then_inc(m_wgu[it % 2].h, 16)
                for ho in range(DC):
                    g2 = tc * DC + ho
                    if g2 >= 2:
                        g.wait_ge(s_dn.h, g2 - 1)
                    g.dma_start(out=wdn[:, ho % 2, :, :],
                                in_=io["w_dnT"].rearrange("(c p) o -> p c o", p=128)[:, :, ho * 128:(ho + 1) * 128]
                                ).then_inc(m_wdn[ho % 2].h, 16)

        @block.sync
        def _(s):
            for tc in range(NTCH):
                t0 = tc * 512
                for ho in range(DC):
                    g2 = tc * DC + ho
                    if g2 >= 2:
                        s.wait_ge(s_ye.h, g2 - 1)
                    s.dma_start(out=strsl[:, g2 % 2, :],
                                in_=io["stream_in"].rearrange("(c p) t -> p c t", p=128)[:, ho, t0:t0 + 512]
                                ).then_inc(m_str[g2 % 2].h, 16)
                s.wait_ge(s_nrm.h, tc + 1)
                s.dma_start(out=io["out"].rearrange("(c p) t -> p c t", p=128)[:, :, t0:t0 + 512],
                            in_=ybuf[:, :, :]).then_inc(m_out.h, 16)
            s.wait_ge(m_out.h, 16 * NTCH)

        @block.tensor
        def _(t):
            for tc in range(NTCH):
                t.wait_ge(m_xbf.h, 16 * (tc + 1))
                for it in range(NG):
                    gi = tc * NG + it
                    t.wait_ge(m_wgu[it % 2].h, 32 * (gi // 2 + 1))
                    if gi >= 2:
                        t.wait_ge(s_hb.h, gi - 1)
                    if it == 0 and tc >= 1:
                        t.wait_ge(s_nrm.h, tc)
                    for kc in range(DC):
                        mm = t.matmul(ps8[:, 2 * (gi % 2), :], wgu[:, it % 2, 0, kc, :],
                                      xbf[:, kc, :], start=(kc == 0), stop=(kc == DC - 1))
                    for kc in range(DC):
                        mm2 = t.matmul(ps8[:, 2 * (gi % 2) + 1, :], wgu[:, it % 2, 1, kc, :],
                                       xbf[:, kc, :], start=(kc == 0), stop=(kc == DC - 1))
                    mm2.then_inc(s_gu.h, 1)
                    s_gu.n += 1
                for ho in range(DC):
                    g2 = tc * DC + ho
                    t.wait_ge(m_wdn[ho % 2].h, 16 * (g2 // 2 + 1))
                    if g2 >= 2:
                        t.wait_ge(s_ye.h, g2 - 1)
                    for ic in range(NG):
                        t.wait_ge(s_hb.h, tc * NG + ic + 1)
                        mm = t.matmul(ps8[:, 4 + g2 % 2, :], wdn[:, ho % 2, ic, :],
                                      hbuf[:, ic, :], start=(ic == 0), stop=(ic == NG - 1))
                    mm.then_inc(s_dn.h, 1)
                    s_dn.n += 1
                for ho in range(DC):
                    t.wait_ge(s_sq.h, (DC + 1) * tc + ho + 1)
                    if ho == 0 and tc >= 1:
                        t.wait_ge(s_row.h, 2 * tc - 1)
                    mm = t.matmul(ps8[0:1, 0, :], ones_col_f[:, :], ybuf[:, ho, :],
                                  start=(ho == 0), stop=(ho == DC - 1))
                    mm2 = t.matmul(ps8[0:1, 6, :], ones_col_f[:, :], scr2[:, ho % 2, :],
                                   start=(ho == 0), stop=(ho == DC - 1))
                    mm2.then_inc(s_st2.h, 1)
                    s_st2.n += 1
                t.wait_ge(s_row.h, 2 * tc + 2)
                if tc >= 1:
                    t.wait_ge(s_nrm.h, tc)
                mm = t.matmul(ps8[:, 7, :], ones_row_f[:, :], rowbuf[:, 4, :],
                              start=True, stop=True)
                mm2 = t.matmul(ps8[:, 1, :], ones_row_f[:, :], rowbuf[:, 5, :],
                               start=True, stop=True)
                mm2.then_inc(s_bc.h, 1)
                s_bc.n += 1

        @block.scalar
        def _(sc):
            for tc in range(NTCH):
                for it in range(NG):
                    gi = tc * NG + it
                    sc.wait_ge(s_gu.h, gi + 1)
                    if gi >= 2:
                        sc.wait_ge(s_hb.h, gi - 1)
                    op = sc.activation(out=gsil[:, gi % 2, :], in_=ps8[:, 2 * (gi % 2), :],
                                       func=AF.Silu)
                    op.then_inc(s_si.h, 1)
                    s_si.n += 1
                for ho in range(DC):
                    g2 = tc * DC + ho
                    sc.wait_ge(s_ye.h, g2 + 1)
                    if ho >= 2:
                        sc.wait_ge(s_st2.h, DC * tc + ho - 1)
                    op = sc.activation(out=scr2[:, ho % 2, :], in_=ybuf[:, ho, :],
                                       func=AF.Square)
                    op.then_inc(s_sq.h, 1)
                    s_sq.n += 1
                sc.wait_ge(s_row.h, 2 * tc + 1)
                op = sc.activation(out=rowbuf[:, 6, :], in_=rowbuf[:, 1, :], func=AF.Sqrt)
                op.then_inc(s_sq.h, 1)
                s_sq.n += 1

        @block.vector
        def _(v):
            v.memset(ones_row_f[:, :], 1.0)
            v.memset(ones_col_f[:, :], 1.0)
            for tc in range(NTCH):
                for it in range(NG):
                    gi = tc * NG + it
                    v.wait_ge(s_si.h, gi + 1)
                    if it == 0 and tc >= 1:
                        v.wait_ge(s_dn.h, DC * tc)
                    op = v.tensor_tensor(out=hbuf[:, it, :], in0=gsil[:, gi % 2, :],
                                         in1=ps8[:, 2 * (gi % 2) + 1, :], op=OP.mult)
                    op.then_inc(s_hb.h, 1)
                    s_hb.n += 1
                for ho in range(DC):
                    g2 = tc * DC + ho
                    v.wait_ge(s_dn.h, g2 + 1)
                    v.wait_ge(m_str[g2 % 2].h, 16 * (g2 // 2 + 1))
                    if ho == 0:
                        v.wait_ge(m_out.h, 16 * tc)
                    op = v.tensor_tensor(out=ybuf[:, ho, :], in0=ps8[:, 4 + g2 % 2, :],
                                         in1=strsl[:, g2 % 2, :], op=OP.add)
                    op.then_inc(s_ye.h, 1)
                    s_ye.n += 1
                v.wait_ge(s_st2.h, DC * (tc + 1))
                v.tensor_scalar(out=rowbuf[:, 2, :], in0=ps8[0:1, 0, :],
                                scalar1=1.0 / D, scalar2=None, op0=OP.mult)
                v.tensor_scalar(out=rowbuf[:, 3, :], in0=ps8[0:1, 6, :],
                                scalar1=1.0 / D, scalar2=None, op0=OP.mult)
                v.tensor_tensor(out=rowbuf[:, 0, :], in0=rowbuf[:, 2, :],
                                in1=rowbuf[:, 2, :], op=OP.mult)
                v.tensor_tensor(out=rowbuf[:, 3, :], in0=rowbuf[:, 3, :],
                                in1=rowbuf[:, 0, :], op=OP.subtract)
                op = v.tensor_scalar(out=rowbuf[:, 1, :], in0=rowbuf[:, 3, :],
                                     scalar1=1e-5, scalar2=None, op0=OP.add)
                op.then_inc(s_row.h, 1)
                s_row.n += 1
                v.wait_ge(s_sq.h, (DC + 1) * (tc + 1))
                v.reciprocal(rowbuf[:, 4, :], rowbuf[:, 6, :])
                op = v.tensor_tensor(out=rowbuf[:, 5, :], in0=rowbuf[:, 2, :],
                                     in1=rowbuf[:, 4, :], op=OP.mult)
                op.then_inc(s_row.h, 1)
                s_row.n += 1
                v.wait_ge(s_bc.h, tc + 1)
                v.tensor_copy(rb2[:, 0, :], ps8[:, 7, :])
                v.tensor_copy(rb2[:, 1, :], ps8[:, 1, :])
                for ho in range(DC):
                    v.tensor_tensor(out=atmp[:, :], in0=ybuf[:, ho, :],
                                    in1=rb2[:, 0, :], op=OP.mult)
                    op = v.tensor_tensor(out=ybuf[:, ho, :], in0=atmp[:, :],
                                         in1=rb2[:, 1, :], op=OP.subtract)
                op.then_inc(s_nrm.h, 1)
                s_nrm.n += 1


def build_l1ffn(cfg):
    c = derive(cfg)
    D, TH, I1 = c["D"], c["TH"], c["I1"]
    nc = bass.Bass("TRN2", target_bir_lowering=False, num_devices=8,
                   detect_race_conditions=False)
    io = {}
    io["stream_in"] = nc.dram_tensor("stream_in", [D, TH], F32, kind="ExternalInput").ap()
    io["w_guT"] = nc.dram_tensor("w_guT", [D, 2 * I1], F32, kind="ExternalInput").ap()
    io["w_dnT"] = nc.dram_tensor("w_dnT", [I1, D], F32, kind="ExternalInput").ap()
    io["out"] = nc.dram_tensor("stream_out", [D, TH], F32, kind="ExternalOutput").ap()
    with ExitStack() as stack:
        block = stack.enter_context(nc.Block())
        k = K(nc, block, stack)
        emit_l1ffn(k, c, io)
    return nc


def prep_l1ffn_inputs(cfg, inp, streams):
    w_guT = np.ascontiguousarray(np.asarray(inp["L1_ffn_w_gate_up"], dtype=np.float32).T)
    w_dnT = np.ascontiguousarray(np.asarray(inp["L1_ffn_w_down"], dtype=np.float32).T)
    maps = []
    for core in range(8):
        maps.append(dict(stream_in=streams[core], w_guT=w_guT, w_dnT=w_dnT))
    return maps


_PFX = [""]


class GatedBlock:
    def __init__(self, block, gate):
        self._b = block
        self._gate = gate

    def _wrap(self, dec):
        gate = self._gate

        def deco(fn):
            def wrapped(eng):
                if gate is not None:
                    eng.wait_ge(gate[0], gate[1])
                fn(eng)
            return dec(wrapped)
        return deco

    @property
    def gpsimd(self):
        return self._wrap(self._b.gpsimd)

    @property
    def sync(self):
        return self._wrap(self._b.sync)

    @property
    def tensor(self):
        return self._wrap(self._b.tensor)

    @property
    def vector(self):
        return self._wrap(self._b.vector)

    @property
    def scalar(self):
        return self._wrap(self._b.scalar)


def _mangle(name):
    return _PFX[0] + name


_orig_sb = K.sb
_orig_ps = K.ps
_orig_sem = K.sem


def _sb(self, name, shape, dtype):
    return _orig_sb(self, _mangle(name), shape, dtype)


def _ps(self, name, shape, dtype=F32):
    return _orig_ps(self, _mangle(name), shape, dtype)


def _sem(self, name):
    return _orig_sem(self, _mangle(name))


K.sb = _sb
K.ps = _ps
K.sem = _sem


def build_fused(cfg):
    c = derive(cfg)
    L, D, DC, NHC, NJ, TH = c["L"], c["D"], c["DC"], c["NHC"], c["NJ"], c["TH"]
    I0, I1 = c["I0"], c["I1"]
    HD = NHC * 128
    NG0 = I0 // 128
    NHW = max(1, D // 512)
    HB = min(D, 512)
    E = 3 * D
    nc = bass.Bass("TRN2", target_bir_lowering=False, num_devices=8,
                   detect_race_conditions=False)
    T = lambda n, s, d, **kw: nc.dram_tensor(n, s, d, **kw)
    io1 = {}
    io1["xT"] = T("xT", [D, L], F32, kind="ExternalInput").ap()
    io1["xTh"] = T("xTh", [D, TH], F32, kind="ExternalInput").ap()
    io1["wq"] = T("wq", [D, NJ * 128], F32, kind="ExternalInput").ap()
    io1["wk"] = T("wk", [D, HD], F32, kind="ExternalInput").ap()
    io1["wv"] = T("wv", [D, HD], F32, kind="ExternalInput").ap()
    io1["wl"] = T("wl", [D, NHC], F32, kind="ExternalInput").ap()
    io1["bl"] = T("bl", [NHC], F32, kind="ExternalInput").ap()
    io1["wo"] = T("wo", [HD, D], F32, kind="ExternalInput").ap()
    io1["cosk"] = T("cosk", [128, L], F32, kind="ExternalInput").ap()
    io1["sink"] = T("sink", [128, L], F32, kind="ExternalInput").ap()
    io1["masks"] = T("masks", [4, 128, 512], BF16, kind="ExternalInput").ap()
    io1["q_scr"] = T("f1_q_scr", [NJ * 128, L], BF16).ap()
    io1["lam_scr"] = T("f1_lam_scr", [NHC, L], F32).ap()
    io1["cc_in"] = T("f1_cc_in", [2, D, TH], F32)
    io1["cc_out"] = T("f1_cc_out", [D, TH], F32)
    s1 = T("f_s1", [D, TH], F32).ap()
    s2 = T("f_s2", [D, TH], F32).ap()
    s3 = T("f_s3", [D, TH], F32).ap()
    io1["out"] = s1
    NSH = c["NSH"]
    NGU_loc = 2 * NG0 // NSH
    NDN_loc = DC // NSH
    io2 = {}
    io2["stream_in"] = s1
    io2["gup"] = T("gup", [NGU_loc * 128, D // 8], I32, kind="ExternalInput").ap()
    io2["gusT"] = T("gusT", [NGU_loc * 128, D // 128], F16, kind="ExternalInput").ap()
    io2["dnp"] = T("dnp", [NDN_loc * 128, I0 // 8], I32, kind="ExternalInput").ap()
    io2["dnsT"] = T("dnsT", [NDN_loc * 128, I0 // 128], F16, kind="ExternalInput").ap()
    io2["ident"] = T("ident", [128, 128], BF16, kind="ExternalInput").ap()
    io2["wgu_sh"] = T("f2_wgu_sh", [NGU_loc, DC, 128, 128], BF16).ap()
    io2["wdn_sh"] = T("f2_wdn_sh", [NDN_loc, NG0, 128, 128], BF16).ap()
    io2["wguT_scr"] = T("f2_wguT", [2 * NG0, DC, 128, 128], BF16, addr_space="Shared").ap()
    io2["wdn_all"] = T("f2_wdn_all", [DC, NG0, 128, 128], BF16, addr_space="Shared").ap()
    io2["out"] = s2
    io3 = {}
    io3["stream_in"] = s2
    io3["w_inT"] = T("w_inT", [D, E], F32, kind="ExternalInput").ap()
    io3["binT"] = T("binT", [128, 3 * DC], F32, kind="ExternalInput").ap()
    io3["w_outT"] = T("w_outT", [D, D], F32, kind="ExternalInput").ap()
    io3["boutT"] = T("boutT", [128, DC], F32, kind="ExternalInput").ap()
    io3["cosT1"] = T("cosT1", [D // 4, TH], F32, kind="ExternalInput").ap()
    io3["sinT1"] = T("sinT1", [D // 4, TH], F32, kind="ExternalInput").ap()
    io3["identf"] = T("identf", [128, 128], F32, kind="ExternalInput").ap()
    io3["q_scr"] = T("f3_q_scr", [D, TH], BF16).ap()
    io3["cc_a_in"] = T("f3_cc_a_in", [128, DC], F32)
    io3["cc_a_out"] = T("f3_cc_a_out", [128, DC], F32)
    io3["out"] = s3
    io4 = {}
    io4["stream_in"] = s3
    io4["w_guT"] = T("w_guT", [D, 2 * I1], F32, kind="ExternalInput").ap()
    io4["w_dnT"] = T("w_dnT", [I1, D], F32, kind="ExternalInput").ap()
    io4["out"] = T("stream_out", [D, TH], F32, kind="ExternalOutput").ap()

    with ExitStack() as stack:
        block = stack.enter_context(nc.Block())
        bars = [stack.enter_context(nc.semaphore("fbar%d" % i)) for i in range(3)]
        emits = [(emit_l0attn, io1, "f1_"), (emit_l0ffn, io2, "f2_"),
                 (emit_l1attn, io3, "f3_"), (emit_l1ffn, io4, "f4_")]
        gate = None
        for idx, (emit, io, pfx) in enumerate(emits):
            _PFX[0] = pfx
            gb = GatedBlock(block, gate)
            _SEM_LOG.clear()
            with ExitStack() as bstack:
                k = K(nc, gb, bstack)
                emit(k, c, io)
            blocksems = sorted(set(_SEM_LOG))
            if idx < 3:
                bar = bars[idx]

                @block.sync
                def _(s, _bar=bar, _g=gate):
                    if _g is not None:
                        s.wait_ge(_g[0], _g[1])
                    s.nop().then_inc(_bar, 1)

                for engname in ("tensor", "vector", "scalar"):
                    dec = getattr(block, engname)

                    @dec
                    def _(e, _bar=bar, _g=gate):
                        if _g is not None:
                            e.wait_ge(_g[0], _g[1])
                        e.wait_ge(_bar, 1)
                        e.nop().then_inc(_bar, 1)

                @block.gpsimd
                def _(g, _bar=bar, _g=gate, _sems=blocksems):
                    if _g is not None:
                        g.wait_ge(_g[0], _g[1])
                    g.wait_ge(_bar, 4)
                    g.dma_reset(None)
                    for snum in _sems:
                        g.sem_clear(range(snum, snum + 1))
                    g.nop().then_inc(_bar, 1)
                gate = (bar, 5)
        _PFX[0] = ""
    return nc


def prep_fused_inputs(cfg, inputs):
    m1 = prep_l0attn_inputs(cfg, inputs)
    m2 = prep_l0ffn_inputs(cfg, inputs, [None] * 8)
    m3 = prep_l1attn_inputs(cfg, inputs, [None] * 8)
    m4 = prep_l1ffn_inputs(cfg, inputs, [None] * 8)
    maps = []
    for core in range(8):
        m = {}
        for mm_ in (m1[core], m2[core], m3[core], m4[core]):
            for kk, vv in mm_.items():
                if kk != "stream_in":
                    m[kk] = vv
        maps.append(m)
    return maps


FUSED = True


def kernel_fused(**inputs):
    cfg = FULL
    c = derive(cfg)
    TH, D, L = c["TH"], c["D"], c["L"]
    nc = _get_prog("fused", build_fused, cfg)
    maps = prep_fused_inputs(cfg, inputs)
    res = run_bass_kernel_spmd(nc, maps, core_ids=list(range(8)))
    out = np.zeros((4, L, D), dtype=np.float32)
    for core in range(8):
        b, g = core // 2, core % 2
        out[b, g * TH:(g + 1) * TH, :] = res.results[core]["stream_out"].T
    return out


_kernel_unfused = kernel


def kernel(**inputs):
    if FUSED:
        return kernel_fused(**inputs)
    return _kernel_unfused(**inputs)
